# version 09
# speedup vs baseline: 1.3205x; 1.0225x over previous
import numpy as np
import ml_dtypes
from contextlib import ExitStack

import concourse.bass as bass
import concourse.mybir as mybir
from concourse.bass_utils import run_bass_kernel_spmd

F32 = mybir.dt.float32
BF16 = mybir.dt.bfloat16
ALU = mybir.AluOpType
ACT = mybir.ActivationFunctionType
AXX = mybir.AxisListType.X

MAGIC = 12582912.0
B = 2
D = 1024
HD = 4096
NCORE = 8
GRP = 4
S = 8192
T = 2048
NP = D // 128
NPH = HD // 128
TSL = 512
NTS = T // TSL
QTS = 512
NQT = T // QTS
NKT = S // 128
KCH = 256
NKC = S // KCH
KTPC = KCH // 128
MSL = 6 * D // GRP
WCH = 256

NSH = 8
NLVL = 128
Q1_TARGET = (B * S * D - 1.0) * 0.005 / NLVL
Q2_TARGET = (B * S * HD - 1.0) * 0.005 / NLVL
KAPPA1 = 3.05
KAPPA2 = 2.576
NRETAIN = 16

DEBUG: list = []
PHASES = 99
SBUF_TRACE = False


def _ffac_level(l):
    if 16 <= l < 112:
        return 0.92 + (l - 16) * (1.08 - 0.92) / 95.0
    if l < 16:
        return 0.35 * (0.92 / 0.35) ** (l / 16.0)
    return 1.08 * (2.6 / 1.08) ** ((l - 111) / 16.0)


def make_constants():
    ffac = np.array([_ffac_level(l) for l in range(NLVL)], np.float32)
    gfine = np.zeros((128, NSH), np.float32)
    for p in range(128):
        for k in range(NSH):
            gfine[p, k] = ffac[(p + 16 * k) % 128]
    return dict(
        gfine=gfine,
        gfrow=np.ascontiguousarray(ffac.reshape(1, NLVL)),
        ones_bf=np.ones((128, 128), ml_dtypes.bfloat16),
        ident_f32=np.eye(128, dtype=np.float32),
        ones_f32row=np.ones((1, 128), np.float32),
        ones_f32col=np.ones((128, 1), np.float32),
        eps_col=np.full((128, 1), 1e-5, np.float32),
    )


def build_program(debug=(), phases=99):
    nc = bass.Bass("TRN2", target_bir_lowering=False, debug=False)
    es = ExitStack()
    CNT = {}
    SEM = {}

    def mksem(name):
        SEM[name] = es.enter_context(nc.semaphore(name))
        CNT[name] = 0

    def inc(name, by=1):
        CNT[name] += by
        return SEM[name]

    def nopinc(eng, name):
        eng.wait_ge(SEM[name], CNT[name])
        eng.nop().then_inc(inc(name), 1)

    def draininc(eng, name):
        eng.wait_ge(SEM[name], CNT[name])
        eng.drain().then_inc(inc(name), 1)

    di = lambda n, sh, dt=F32: nc.dram_tensor(n, sh, dt, kind="ExternalInput").ap()
    do = lambda n, sh, dt=F32: nc.dram_tensor(n, sh, dt, kind="ExternalOutput").ap()
    dint = lambda n, sh, dt=F32: nc.dram_tensor(n, sh, dt)

    xTI = di("xT", [D, T])
    wqT = di("wqT", [D, D]); wkT = di("wkT", [D, D])
    wvT = di("wvT", [D, D]); woT = di("woT", [D, D])
    w1T = di("w1T", [D, HD]); w2T = di("w2T", [HD, D])
    w1slI = di("w1sl", [D, HD // NCORE]); w2slI = di("w2sl", [HD, D // NCORE])
    wmodslI = di("wmodsl", [D, MSL])
    cTcol = di("cTcol", [D, 1])
    bqI = di("bq", [D, 1]); bkI = di("bk", [D, 1]); bvI = di("bv", [D, 1]); boI = di("bo", [D, 1])
    lngI = di("lng", [D, 1]); lnbI = di("lnb", [D, 1])
    gfineI = di("gfine", [128, NSH]); gfrowI = di("gfrow", [1, NLVL])
    onesbI = di("ones_bf", [128, 128], BF16)
    identI = di("ident_f32", [128, 128])
    onesrI = di("ones_f32row", [1, 128])
    onescI = di("ones_f32col", [128, 1])
    epsI = di("eps_col", [128, 1])

    outT = do("outT", [D, T])

    cc_mod_in = dint("cc_mod_in", [1, MSL]); cc_mod_out = dint("cc_mod_out", [GRP, MSL])
    cc_k_in = dint("cc_k_in", [D, T], BF16); cc_k_out = dint("cc_k_out", [GRP * D, T], BF16)
    ccv_in = dint("ccv_in", [T, D], BF16); ccv_out = dint("ccv_out", [S, D], BF16)
    cc_g_in = dint("cc_g_in", [1, 2]); cc_g_out = dint("cc_g_out", [1, 2])
    cc_q1_in = dint("cc_q1_in", [128, 2 * NSH]); cc_q1_out = dint("cc_q1_out", [128, 2 * NSH])
    cc_q2a_in = dint("cc_q2a_in", [1, 1]); cc_q2a_out = dint("cc_q2a_out", [1, 1])
    cc_q2_in = dint("cc_q2_in", [128, NSH]); cc_q2_out = dint("cc_q2_out", [128, NSH])
    q_dram = dint("q_dram", [D, T], BF16)
    spill_ge = dint("spill_ge", [HD, T])
    t1_dram = dint("t1_dram", [D, HD], BF16)
    t2_dram = dint("t2_dram", [HD, D], BF16)
    flat_dram = dint("flat_dram", [1, 16 * 128])

    dbg = {}
    def tap(name, sh, dt=F32):
        if name in debug and name not in dbg:
            dbg[name] = do("dbg_" + name, sh, dt)
        return dbg.get(name)

    for s in ["ld_const", "ld_x", "ld_wmod", "ld_wa", "ld_wb", "ld_gsl", "ld_gam",
              "st_cc", "ld_modc",
              "st_k0", "st_k1", "st_v0", "st_v1", "st_q0", "st_q1",
              "cc_mod", "cc_k", "cc_v", "cc_g", "cc_q1", "cc_q2a", "cc_q2",
              "ld_q",
              "ld_k0", "ld_k1", "ld_k2", "ld_k3", "ld_vb0", "ld_vb1", "ld_vb2",
              "ld_vc0", "ld_vc1", "ld_vc2", "ld_t2c2",
              "ld_wt0", "ld_wt1", "st_ta", "st_tb", "ld_t1c0", "ld_t1c1",
              "ld_t2c0", "ld_t2c1",
              "st_ge0", "st_ge1", "ld_ge0", "ld_ge1",
              "ld_cnt", "st_out0", "st_out1", "dbgs", "st_q2a", "ld_q2a",
              "pe", "ve", "sc", "gp", "vcnt"]:
        mksem(s)

    sbp = lambda n, sh, dt=F32: es.enter_context(nc.sbuf_tensor(n, sh, dt))
    xt = sbp("xt", [128, NP * T])
    modc = sbp("modc", [128, 48])
    derc = sbp("derc", [128, 48])
    c_sl = sbp("c_sl", [128, NP])
    bqc = sbp("bqc", [128, NP]); bkc = sbp("bkc", [128, NP]); bvc = sbp("bvc", [128, NP]); boc = sbp("boc", [128, NP])
    lngc = sbp("lngc", [128, NP]); lnbc = sbp("lnbc", [128, NP])
    bqp = sbp("bqp", [128, NP]); bkp = sbp("bkp", [128, NP]); bvp = sbp("bvp", [128, NP])
    onesb = sbp("onesb", [128, 128], BF16)
    identf = sbp("identf", [128, 128])
    onesr = sbp("onesr", [1, 128])
    onesc = sbp("onesc", [128, 1])
    epscol = sbp("epscol", [128, 1])
    gfine_sb = sbp("gfine_sb", [128, NSH]); gfrow_sb = sbp("gfrow_sb", [1, NLVL])
    g1mat = sbp("g1mat", [128, NSH]); g1nmat = sbp("g1nmat", [128, NSH])
    g2mat = sbp("g2mat", [128, NSH]); g2row = sbp("g2row", [1, NLVL])
    cnt1 = sbp("cnt1", [128, 2 * NSH])
    cnt2 = sbp("cnt2", [128, NSH])
    accs = [sbp(f"acc{i}", [128, 1]) for i in range(4)]
    srow = sbp("srow", [1, 64])
    lvlrow = sbp("lvlrow", [1, 128])
    thr1row = sbp("thr1row", [1, NLVL])
    scol = sbp("scol", [128, 16])
    gpart = sbp("gpart", [128, 2])
    cnt_ar = sbp("cnt_ar", [128, 16])
    trsb = sbp("trsb", [16, 128])
    smallA = sbp("smallA", [128, 16]); smallB = sbp("smallB", [128, 16]); smallC = sbp("smallC", [128, 16])
    junk = [sbp(f"junk{i}", [128, TSL]) for i in range(2)]
    c_sib = sbp("c_sib", [128, NP], F32)
    sa_b = sbp("sa_b", [128, NP], BF16)

    PB = [es.enter_context(nc.psum_tensor(f"pb{i}", [128, 512], F32)) for i in range(8)]

    open_stacks = []

    def new_stack():
        st = ExitStack()
        open_stacks.append(st)
        return st

    def close_stack(st):
        open_stacks.remove(st)
        st.close()

    def sbuf_trace(where):
        if SBUF_TRACE:
            print(f"[sbuf] {where}: remaining={nc.sbuf_bytes_remaining}")

    stack_w = new_stack()
    wbf_b = stack_w.enter_context(nc.sbuf_tensor("wbf_b", [128, NP * D], BF16))

    blockctx = nc.Block()
    block = blockctx.__enter__()

    def ph0_sy(sy):
        for dst, src in [(onesb, onesbI), (identf, identI), (gfine_sb, gfineI)]:
            sy.dma_start(out=dst[:], in_=src).then_inc(inc("ld_const", 16), 16)
        sy.dma_start(out=onesr[:], in_=onesrI).then_inc(inc("ld_const", 16), 16)
        sy.dma_start(out=onesc[:], in_=onescI).then_inc(inc("ld_const", 16), 16)
        sy.dma_start(out=epscol[:], in_=epsI).then_inc(inc("ld_const", 16), 16)
        sy.dma_start(out=gfrow_sb[0:1, 0:NLVL], in_=gfrowI).then_inc(inc("ld_const", 16), 16)
        sy.dma_start(out=c_sl[:].rearrange("p (np o) -> p np o", o=1), in_=cTcol.rearrange("(np p) o -> p np o", p=128), allow_slow_non_contiguous=True).then_inc(inc("ld_const", 16), 16)
        for dst, src in [(bqc, bqI), (bkc, bkI), (bvc, bvI), (boc, boI), (lngc, lngI), (lnbc, lnbI)]:
            sy.dma_start(out=dst[:].rearrange("p (np o) -> p np o", o=1), in_=src.rearrange("(np p) o -> p np o", p=128), allow_slow_non_contiguous=True).then_inc(inc("ld_const", 16), 16)
    block.sync(ph0_sy)
    LD_CONST = CNT["ld_const"]

    stack_proj = new_stack()
    wbf_a = stack_proj.enter_context(nc.sbuf_tensor("wbf_a", [128, NP * D], BF16))
    sbuf_trace("wbf_a")
    stack_gam = new_stack()
    gbuf = [stack_gam.enter_context(nc.sbuf_tensor(f"gbuf{i}", [128, 2048], F32)) for i in range(2)]
    sbuf_trace("gbuf")
    gtmp = stack_gam.enter_context(nc.sbuf_tensor("gtmp", [128, 4], F32))

    def ph0_g(g):
        g.dma_start(out=wbf_a[:].rearrange("p (np e) -> p np e", np=NP), in_=wqT.rearrange("(np p) e -> p np e", p=128)).then_inc(inc("ld_wa", 16), 16)
        g.dma_start(out=wbf_b[:].rearrange("p (np e) -> p np e", np=NP), in_=wkT.rearrange("(np p) e -> p np e", p=128)).then_inc(inc("ld_wb", 16), 16)
    block.gpsimd(ph0_g)
    LD_WQ = CNT["ld_wa"]; LD_WK = CNT["ld_wb"]

    gchunks = [
        lambda: w1slI[:, 0:256].rearrange("(np p) h -> p np h", p=128),
        lambda: w1slI[:, 256:512].rearrange("(np p) h -> p np h", p=128),
        lambda: w2slI[:, 0:64].rearrange("(nph p) e -> p nph e", p=128),
        lambda: w2slI[:, 64:128].rearrange("(nph p) e -> p nph e", p=128),
    ]
    block.vector(lambda v: (v.memzero(srow[:]), draininc(v, "ve")))
    gred_marks = []
    for gi in range(4):
        prev_mark = gred_marks[gi - 2] if gi >= 2 else None

        gsem = "ld_gsl" if gi % 2 == 0 else "ld_gam"

        def g_ld(sy, gi=gi, prev_mark=prev_mark, gsem=gsem):
            if prev_mark is not None:
                sy.wait_ge(SEM["ve"], prev_mark)
            sy.dma_start(out=gbuf[gi % 2][:].rearrange("p (a b) -> p a b", a=(NP if gi < 2 else NPH)),
                         in_=gchunks[gi]()).then_inc(inc(gsem, 16), 16)
        block.sync(g_ld)
        ldv = CNT[gsem]

        def g_red(v, gi=gi, ldv=ldv, gsem=gsem):
            v.wait_ge(SEM[gsem], ldv)
            v.tensor_reduce(gtmp[:, gi:gi + 1], gbuf[gi % 2][:], AXX, ALU.add, apply_absolute_value=True)
            draininc(v, "ve")
        block.vector(g_red)
        gred_marks.append(CNT["ve"])

    block.sync(lambda sy: sy.dma_start(out=xt[:].rearrange("p (np t) -> p np t", np=NP),
                                       in_=xTI.rearrange("(np p) t -> p np t", p=128)).then_inc(inc("ld_x", 16), 16))
    LD_X = CNT["ld_x"]

    block.vector(lambda v: (
        v.tensor_tensor(gpart[:, 0:1], gtmp[:, 0:1], gtmp[:, 1:2], ALU.add),
        v.tensor_tensor(gpart[:, 1:2], gtmp[:, 2:3], gtmp[:, 3:4], ALU.add),
        draininc(v, "ve")))
    VE_GP = CNT["ve"]

    block.tensor(lambda t: (
        t.wait_ge(SEM["ve"], VE_GP),
        t.wait_ge(SEM["ld_const"], LD_CONST),
        t.wait_ge(SEM["pe"], CNT["pe"]),
        t.matmul(PB[7][0:1, 0:2], onesc[:], gpart[:], start=True, stop=True).then_inc(inc("pe"), 1)))
    PE_GP = CNT["pe"]

    block.vector(lambda v: (
        v.wait_ge(SEM["pe"], PE_GP),
        v.tensor_copy(smallA[0:1, 0:2], PB[7][0:1, 0:2]),
        draininc(v, "ve")))
    VE_GPC = CNT["ve"]
    close_stack(stack_gam)
    sbuf_trace("close_stack")

    block.sync(lambda sy: (
        sy.wait_ge(SEM["ve"], VE_GPC),
        sy.dma_start(out=cc_g_in[:, :], in_=smallA[0:1, 0:2]).then_inc(inc("st_cc", 16), 16)))
    ST_G = CNT["st_cc"]

    block.gpsimd(lambda g: (
        g.wait_ge(SEM["st_cc"], ST_G),
        g.collective_compute("AllReduce", ALU.add, replica_groups=[list(range(NCORE))],
                             ins=[cc_g_in.ap().opt()], outs=[cc_g_out.ap().opt()]).then_inc(inc("cc_g"), 1)))
    CC_G = CNT["cc_g"]

    block.sync(lambda sy: (
        sy.wait_ge(SEM["cc_g"], CC_G),
        sy.dma_start(out=smallB[0:1, 0:2], in_=cc_g_out[:, :]).then_inc(inc("ld_gam", 16), 16)))
    LD_GAM = CNT["ld_gam"]

    def gam_scalars(v):
        v.wait_ge(SEM["ld_gam"], LD_GAM)
        v.tensor_scalar(srow[0:1, 4:5], smallB[0:1, 0:1], 1.0 / (HD * D), 1e-5, op0=ALU.mult, op1=ALU.add)
        v.tensor_scalar(srow[0:1, 5:6], smallB[0:1, 1:2], 1.0 / (HD * D), 1e-5, op0=ALU.mult, op1=ALU.add)
        v.drain()
        v.reciprocal(srow[0:1, 6:7], srow[0:1, 4:5])
        v.reciprocal(srow[0:1, 7:8], srow[0:1, 5:6])
        draininc(v, "ve")
    block.vector(gam_scalars)
    VE_GAMS = CNT["ve"]

    block.tensor(lambda t: (
        t.wait_ge(SEM["ve"], VE_GAMS),
        t.wait_ge(SEM["ve"], VE_GPC),
        t.wait_ge(SEM["pe"], CNT["pe"]),
        t.matmul(PB[7][:, 0:2], onesr[:], srow[0:1, 6:8], start=True, stop=True).then_inc(inc("pe"), 1)))
    PE_RG = CNT["pe"]

    block.vector(lambda v: (
        v.wait_ge(SEM["pe"], PE_RG),
        v.tensor_copy(scol[:, 6:8], PB[7][:, 0:2]),
        draininc(v, "ve")))
    VE_RGCOL = CNT["ve"]

    block.scalar(lambda s_: (
        s_.wait_ge(SEM["ld_const"], LD_CONST),
        s_.activation(c_sib[:], c_sl[:], ACT.Silu),
        draininc(s_, "sc")))
    SC_SILU = CNT["sc"]
    stack_wmod = new_stack()
    wmodc = stack_wmod.enter_context(nc.sbuf_tensor("wmodc", [128, NP * 512], F32))
    mod_row = stack_wmod.enter_context(nc.sbuf_tensor("mod_row", [1, MSL], F32))
    sbuf_trace("wmodc")

    NMOD = MSL // 512
    pe_mod_base = CNT["pe"]
    for m in range(NMOD):
        def wm_ld(g, m=m):
            if m > 0:
                g.wait_ge(SEM["pe"], pe_mod_base + m)
            src = wmodslI[:, m * 512:(m + 1) * 512].rearrange("(np p) m2 -> p np m2", p=128)
            g.dma_start(out=wmodc[:].rearrange("p (np m2) -> p np m2", np=NP), in_=src).then_inc(inc("ld_wmod", 16), 16)
        block.gpsimd(wm_ld)

    ve_modevac_base = CNT["ve"]

    def mod_mm(t):
        t.wait_ge(SEM["sc"], SC_SILU)
        t.wait_ge(SEM["ve"], VE_RGCOL)
        for m in range(NMOD):
            t.wait_ge(SEM["ld_wmod"], 16 * (m + 1))
            if m >= 2:
                t.wait_ge(SEM["ve"], ve_modevac_base + m - 1)
            bank = PB[6 + m % 2]
            t.wait_ge(SEM["pe"], CNT["pe"])
            for dp in range(NP):
                mm = t.matmul(bank[0:1, 0:512], c_sib[:, dp:dp + 1],
                              wmodc[:, dp * 512:(dp + 1) * 512],
                              start=(dp == 0), stop=(dp == NP - 1))
            mm.then_inc(inc("pe"), 1)
    block.tensor(mod_mm)
    PE_MODS = [pe_mod_base + 1 + i for i in range(NMOD)]

    def mod_evac(v):
        for m in range(NMOD):
            v.wait_ge(SEM["pe"], PE_MODS[m])
            v.tensor_copy(mod_row[0:1, m * 512:(m + 1) * 512], PB[6 + m % 2][0:1, 0:512])
            draininc(v, "ve")
    block.vector(mod_evac)
    VE_MODR = CNT["ve"]

    block.sync(lambda sy: (
        sy.wait_ge(SEM["ve"], VE_MODR),
        sy.dma_start(out=cc_mod_in[:, :], in_=mod_row[:]).then_inc(inc("st_cc", 16), 16)))
    ST_MOD = CNT["st_cc"]
    close_stack(stack_wmod)
    sbuf_trace("close_stack")

    block.gpsimd(lambda g: (
        g.wait_ge(SEM["st_cc"], ST_MOD),
        g.collective_compute("AllGather", ALU.bypass, replica_groups=[[0, 1, 2, 3], [4, 5, 6, 7]],
                             ins=[cc_mod_in.ap().opt()], outs=[cc_mod_out.ap().opt()]).then_inc(inc("cc_mod"), 1)))
    CC_MOD = CNT["cc_mod"]

    block.sync(lambda sy: (
        sy.wait_ge(SEM["cc_mod"], CC_MOD),
        sy.dma_start(out=modc[:, 0:48].rearrange("p (v np) -> p v np", np=NP),
                     in_=cc_mod_out.rearrange("g m -> (g m)").rearrange("(v np p) -> p v np", p=128, np=NP),
                     allow_slow_non_contiguous=True).then_inc(inc("ld_modc", 16), 16)))
    LD_MODC = CNT["ld_modc"]

    def derc_fn(v):
        v.wait_ge(SEM["ld_modc"], LD_MODC)
        v.memzero(derc[:])
        v.drain()
        for i, vsrc in [(0, 2), (1, 5)]:
            gsl = modc[:, vsrc * 8:(vsrc + 1) * 8]
            dcl = derc[:, i * 8:(i + 1) * 8]
            v.tensor_scalar(smallA[:, 0:8], gsl, 0.0, None, op0=ALU.is_ge)
            v.tensor_scalar(smallB[:, 0:8], gsl, -0.25, 0.25, op0=ALU.max, op1=ALU.min)
            v.drain()
            v.tensor_scalar(smallC[:, 0:8], smallA[:, 0:8], 2.0, -1.0, op0=ALU.mult, op1=ALU.add)
            v.drain()
            v.tensor_scalar(smallC[:, 8:16], smallC[:, 0:8], 0.25, None, op0=ALU.mult)
            v.drain()
            v.tensor_tensor(dcl, gsl, smallC[:, 8:16], ALU.add)
            v.drain()
            v.tensor_tensor(dcl, dcl, smallB[:, 0:8], ALU.subtract)
            v.drain()
        v.tensor_scalar(derc[:, 16:24], modc[:, 8:16], 1.0, None, op0=ALU.add)
        v.tensor_scalar(derc[:, 24:32], modc[:, 32:40], 1.0, None, op0=ALU.add)
        v.tensor_tensor(derc[:, 32:40], derc[:, 0:8], boc[:], ALU.mult)
        draininc(v, "ve")
    block.vector(derc_fn)
    VE_DERC = CNT["ve"]

    block.scalar(lambda s_: (
        s_.wait_ge(SEM["ld_const"], LD_CONST),
        s_.activation(smallA[:, 8:16], lngc[:], ACT.Square),
        s_.activation(smallB[:, 8:16], lnbc[:], ACT.Square),
        draininc(s_, "sc")))
    SC_GB2 = CNT["sc"]

    block.vector(lambda v: (
        v.wait_ge(SEM["sc"], SC_GB2),
        v.tensor_tensor(smallC[:, 8:16], smallA[:, 8:16], smallB[:, 8:16], ALU.add),
        v.drain(),
        v.tensor_reduce(smallC[:, 1:2], smallC[:, 8:16], AXX, ALU.add),
        draininc(v, "ve")))
    VE_C1A = CNT["ve"]

    block.tensor(lambda t: (
        t.wait_ge(SEM["ve"], VE_C1A),
        t.wait_ge(SEM["pe"], CNT["pe"]),
        t.matmul(PB[5][0:1, 0:1], onesc[:], smallC[:, 1:2], start=True, stop=True).then_inc(inc("pe"), 1)))
    PE_C1 = CNT["pe"]

    block.scalar(lambda s_: (
        s_.wait_ge(SEM["pe"], PE_C1),
        s_.activation(srow[0:1, 0:1], PB[5][0:1, 0:1], ACT.Sqrt, scale=1.0 / D),
        draininc(s_, "sc")))
    SC_C1 = CNT["sc"]

    block.vector(lambda v: (
        v.wait_ge(SEM["sc"], SC_C1),
        v.tensor_scalar(srow[0:1, 1:2], srow[0:1, 0:1], KAPPA1, None, op0=ALU.mult),
        v.drain(),
        v.tensor_scalar(thr1row[0:1, 0:NLVL], gfrow_sb[0:1, 0:NLVL], srow[0:1, 1:2], None, op0=ALU.mult),
        draininc(v, "ve")))
    VE_C1B = CNT["ve"]

    block.tensor(lambda t: (
        t.wait_ge(SEM["ve"], VE_C1B),
        t.wait_ge(SEM["pe"], CNT["pe"]),
        t.matmul(PB[5][:, 0:1], onesr[:], srow[0:1, 1:2], start=True, stop=True).then_inc(inc("pe"), 1)))
    PE_C1B = CNT["pe"]

    block.vector(lambda v: (
        v.wait_ge(SEM["pe"], PE_C1B),
        v.tensor_copy(scol[:, 4:5], PB[5][:, 0:1]),
        v.drain(),
        v.tensor_scalar(g1mat[:], gfine_sb[:], scol[:, 4:5], None, op0=ALU.mult),
        v.drain(),
        v.tensor_scalar(g1nmat[:], g1mat[:], -1.0, None, op0=ALU.mult),
        v.memzero(cnt1[:]),
        v.memzero(cnt2[:]),
        draininc(v, "ve")))
    VE_G1 = CNT["ve"]

    block.vector(lambda v: (
        v.wait_ge(SEM["ld_modc"], LD_MODC),
        v.tensor_copy(sa_b[:], modc[:, 0:8]),
        draininc(v, "ve")))
    VE_SAB = CNT["ve"]

    def bias_mod(wtile, wsem, wval, bc, bp):
        ve_now = CNT["ve"]

        def tm(t):
            t.wait_ge(SEM["ve"], VE_SAB)
            t.wait_ge(SEM["ve"], ve_now)
            t.wait_ge(SEM[wsem], wval)
            t.wait_ge(SEM["pe"], CNT["pe"])
            for es2 in range(NP):
                for dp in range(NP):
                    mm = t.matmul(PB[4][0:128, es2:es2 + 1],
                                  wtile[:, dp * D + es2 * 128: dp * D + (es2 + 1) * 128],
                                  sa_b[:, dp:dp + 1], start=(dp == 0), stop=(dp == NP - 1))
            mm.then_inc(inc("pe"), 1)
        block.tensor(tm)
        pe_done = CNT["pe"]

        block.vector(lambda v: (
            v.wait_ge(SEM["pe"], pe_done),
            v.tensor_tensor(bp[:], PB[4][:, 0:NP], bc[:], ALU.add),
            draininc(v, "ve")))
        return CNT["ve"]

    VE_BQP = bias_mod(wbf_a, "ld_wa", LD_WQ, bqc, bqp)
    VE_BKP = bias_mod(wbf_b, "ld_wb", LD_WK, bkc, bkp)

    ln1b = stack_proj.enter_context(nc.sbuf_tensor("ln1b", [128, NP * T], BF16))
    sbuf_trace("ln1b")
    stage = [stack_proj.enter_context(nc.sbuf_tensor(f"stage{i}", [128, NP * TSL], BF16)) for i in range(2)]
    castsq = stack_proj.enter_context(nc.sbuf_tensor("castsq", [128, NP * TSL], BF16))
    bcs = [stack_proj.enter_context(nc.sbuf_tensor(f"bc{i}", [128, TSL], F32)) for i in range(11)]
    wide1 = stack_proj.enter_context(nc.sbuf_tensor("wide1", [128, NP * TSL], F32))
    sbuf_trace("bcs")

    castsq3 = castsq[:].rearrange("p (dp t) -> p dp t", dp=NP)
    xt3 = xt[:].rearrange("p (dp q) -> p dp q", dp=NP)

    def bcast_t(bc):
        return bc[:].unsqueeze(1).broadcast_to((128, NP, TSL))

    def bcast_c(cols):
        return cols.unsqueeze(2).broadcast_to((128, NP, TSL))

    ln_state = {"pe_guard": None}

    def emit_ln_stats(src3, eps, ve_guard=None):
        def sc_fn(s_):
            if ve_guard is not None:
                s_.wait_ge(SEM["ve"], ve_guard)
            if ln_state["pe_guard"] is not None:
                s_.wait_ge(SEM["pe"], ln_state["pe_guard"])
            s_.activation(castsq3, src3, ACT.Copy)
            draininc(s_, "sc")
        block.scalar(sc_fn)
        sc_cast = CNT["sc"]

        def t_fn1(t):
            t.wait_ge(SEM["sc"], sc_cast)
            t.wait_ge(SEM["pe"], CNT["pe"])
            for dp in range(NP):
                mm = t.matmul(PB[0][:, 0:TSL], onesb[:], castsq[:, dp * TSL:(dp + 1) * TSL],
                              start=(dp == 0), stop=(dp == NP - 1))
            mm.then_inc(inc("pe"), 1)
        block.tensor(t_fn1)
        pe_a = CNT["pe"]

        def sc_fn2(s_):
            s_.wait_ge(SEM["pe"], pe_a)
            s_.activation(castsq3, src3, ACT.Square)
            draininc(s_, "sc")
        block.scalar(sc_fn2)
        sc_sq = CNT["sc"]

        def t_fn2(t):
            t.wait_ge(SEM["sc"], sc_sq)
            t.wait_ge(SEM["pe"], CNT["pe"])
            for dp in range(NP):
                mm = t.matmul(PB[1][:, 0:TSL], onesb[:], castsq[:, dp * TSL:(dp + 1) * TSL],
                              start=(dp == 0), stop=(dp == NP - 1))
            mm.then_inc(inc("pe"), 1)
        block.tensor(t_fn2)
        pe_st = CNT["pe"]
        ln_state["pe_guard"] = pe_st

        block.vector(lambda v: (
            v.wait_ge(SEM["pe"], pe_st),
            v.tensor_scalar(bcs[0][:], PB[0][:, 0:TSL], 1.0 / D, None, op0=ALU.mult),
            v.tensor_scalar(bcs[1][:], PB[1][:, 0:TSL], 1.0 / D, None, op0=ALU.mult),
            v.drain(),
            v.tensor_tensor(bcs[2][:], bcs[0][:], bcs[0][:], ALU.mult),
            v.drain(),
            v.tensor_tensor(bcs[3][:], bcs[1][:], bcs[2][:], ALU.subtract),
            draininc(v, "ve")))
        ve_var = CNT["ve"]

        block.scalar(lambda s_: (
            s_.wait_ge(SEM["ve"], ve_var),
            s_.activation(bcs[4][:], bcs[3][:], ACT.Sqrt, bias=epscol[:, 0:1]),
            draininc(s_, "sc")))
        sc_sd = CNT["sc"]

        block.vector(lambda v: (
            v.wait_ge(SEM["sc"], sc_sd),
            v.reciprocal(bcs[5][:], bcs[4][:]),
            v.drain(),
            v.tensor_tensor(bcs[6][:], bcs[0][:], bcs[5][:], ALU.mult),
            draininc(v, "ve")))
        return CNT["ve"]

    ln1b3 = ln1b[:].rearrange("p (dp q) -> p dp q", dp=NP)
    ln1_ve_marks = []
    for ts in range(NTS):
        if ts == 0:
            block.scalar(lambda s_: s_.wait_ge(SEM["ld_x"], LD_X))
        ve_stats = emit_ln_stats(xt3[:, :, ts * TSL:(ts + 1) * TSL], 1e-5)

        def norm_fn(v, ts=ts, ve_stats=ve_stats):
            v.wait_ge(SEM["ve"], ve_stats)
            if ts == 0:
                v.wait_ge(SEM["ve"], VE_DERC)
            x3s = xt3[:, :, ts * TSL:(ts + 1) * TSL]
            w3 = wide1[:].rearrange("p (dp t) -> p dp t", dp=NP)
            v.tensor_tensor(w3, x3s, bcast_t(bcs[5]), ALU.mult)
            v.drain()
            v.tensor_tensor(w3, w3, bcast_t(bcs[6]), ALU.subtract)
            v.drain()
            v.tensor_tensor(ln1b3[:, :, ts * TSL:(ts + 1) * TSL], w3, bcast_c(derc[:, 16:24]), ALU.mult)
            draininc(v, "ve")
        block.vector(norm_fn)
        ln1_ve_marks.append(CNT["ve"])
    VE_LN1 = CNT["ve"]

    def proj_T(wtile, bias_tile, out_dram, ssem, extra_waits, prev_store_waits=(), bias_add=True):
        pe_base = CNT["pe"]
        ve_base = CNT["ve"]
        st_base = [CNT[ssem[0]], CNT[ssem[1]]]

        def t_fn(t):
            for sname, sval in extra_waits:
                t.wait_ge(SEM[sname], sval)
            n = 0
            for ts in range(NTS):
                for es2 in range(NP):
                    if n >= 2:
                        t.wait_ge(SEM["ve"], ve_base + n - 1)
                    pbk = PB[2 + (n % 2)]
                    t.wait_ge(SEM["pe"], CNT["pe"])
                    for dp in range(NP):
                        mm = t.matmul(pbk[:, 0:TSL], wtile[:, dp * D + es2 * 128: dp * D + (es2 + 1) * 128],
                                      ln1b[:, dp * T + ts * TSL: dp * T + (ts + 1) * TSL],
                                      start=(dp == 0), stop=(dp == NP - 1))
                    mm.then_inc(inc("pe"), 1)
                    n += 1
        block.tensor(t_fn)

        def v_fn(v):
            n = 0
            for ts in range(NTS):
                st = stage[ts % 2]
                for es2 in range(NP):
                    v.wait_ge(SEM["pe"], pe_base + n + 1)
                    if n == 0:
                        for sname, sval in prev_store_waits:
                            v.wait_ge(SEM[sname], sval)
                    if es2 == 0 and ts >= 2:
                        v.wait_ge(SEM[ssem[ts % 2]], st_base[ts % 2] + 16 * (ts // 2))
                    if bias_add:
                        v.tensor_scalar(st[:, es2 * TSL:(es2 + 1) * TSL], PB[2 + (n % 2)][:, 0:TSL],
                                        bias_tile[:, es2:es2 + 1], None, op0=ALU.add)
                    else:
                        v.tensor_copy(st[:, es2 * TSL:(es2 + 1) * TSL], PB[2 + (n % 2)][:, 0:TSL])
                    draininc(v, "ve")
                    n += 1
        block.vector(v_fn)
        ve_all = CNT["ve"]

        def sy_fn(sy):
            for ts in range(NTS):
                sy.wait_ge(SEM["ve"], ve_base + (ts + 1) * NP)
                sy.dma_start(out=out_dram.rearrange("(np p) t -> p np t", p=128)[:, :, ts * TSL:(ts + 1) * TSL],
                             in_=stage[ts % 2][:].rearrange("p (np t) -> p np t", np=NP)
                             ).then_inc(inc(ssem[ts % 2], 16), 16)
        block.sync(sy_fn)
        return ve_all

    VE_K = proj_T(wbf_b, bkp, cc_k_in, ("st_k0", "st_k1"), [("ve", VE_LN1), ("ve", VE_BKP)])
    ST_K = [CNT["st_k0"], CNT["st_k1"]]

    block.gpsimd(lambda g: (
        g.wait_ge(SEM["st_k0"], ST_K[0]),
        g.wait_ge(SEM["st_k1"], ST_K[1]),
        g.collective_compute("AllGather", ALU.bypass, replica_groups=[[0, 1, 2, 3], [4, 5, 6, 7]],
                             ins=[cc_k_in.ap().opt()], outs=[cc_k_out.ap().opt()]).then_inc(inc("cc_k"), 1)))
    CC_K = CNT["cc_k"]

    VE_Q = proj_T(wbf_a, bqp, q_dram, ("st_q0", "st_q1"), [("ve", VE_K), ("ve", VE_BQP)],
                  prev_store_waits=[("st_k0", ST_K[0]), ("st_k1", ST_K[1])])
    ST_Q = [CNT["st_q0"], CNT["st_q1"]]

    block.gpsimd(lambda g: (
        g.wait_ge(SEM["ve"], VE_K),
        g.dma_start(out=wbf_b[:].rearrange("p (np e) -> p np e", np=NP), in_=woT.rearrange("(np p) e -> p np e", p=128)).then_inc(inc("ld_wb", 16), 16)))
    LD_WO = CNT["ld_wb"]

    block.gpsimd(lambda g: (
        g.wait_ge(SEM["ve"], VE_Q),
        g.dma_start(out=wbf_a[:].rearrange("p (np e) -> p np e", np=NP), in_=wvT.rearrange("(np p) e -> p np e", p=128)).then_inc(inc("ld_wa", 16), 16)))
    LD_WV = CNT["ld_wa"]

    VE_BVP = bias_mod(wbf_a, "ld_wa", LD_WV, bvc, bvp)

    pe_v_base = CNT["pe"]
    ve_v_base = CNT["ve"]

    def v_proj_t(t):
        t.wait_ge(SEM["ld_wa"], LD_WV)
        t.wait_ge(SEM["ve"], VE_Q)
        n = 0
        for ts in range(NTS):
            for tsub in range(TSL // 128):
                for eh in range(2):
                    if n >= 2:
                        t.wait_ge(SEM["ve"], ve_v_base + n - 1)
                    pbk = PB[2 + (n % 2)]
                    t.wait_ge(SEM["pe"], CNT["pe"])
                    for dp in range(NP):
                        mm = t.matmul(pbk[:, 0:TSL],
                                      ln1b[:, dp * T + ts * TSL + tsub * 128: dp * T + ts * TSL + (tsub + 1) * 128],
                                      wbf_a[:, dp * D + eh * 512: dp * D + (eh + 1) * 512],
                                      start=(dp == 0), stop=(dp == NP - 1))
                    mm.then_inc(inc("pe"), 1)
                    n += 1
    block.tensor(v_proj_t)

    def v_proj_v(v):
        n = 0
        for ts in range(NTS):
            st = stage[ts % 2]
            for tsub in range(TSL // 128):
                for eh in range(2):
                    v.wait_ge(SEM["pe"], pe_v_base + n + 1)
                    if n == 0:
                        v.wait_ge(SEM["st_q0"], ST_Q[0])
                        v.wait_ge(SEM["st_q1"], ST_Q[1])
                    if ts >= 2 and tsub == 0 and eh == 0:
                        v.wait_ge(SEM["st_v%d" % (ts % 2)], 16 * (ts // 2))
                    v.tensor_copy(st[:, tsub * 2 * TSL + eh * TSL: tsub * 2 * TSL + (eh + 1) * TSL],
                                  PB[2 + (n % 2)][:, 0:TSL])
                    draininc(v, "ve")
                    n += 1
    block.vector(v_proj_v)
    VE_V = CNT["ve"]

    def v_store(sy):
        for ts in range(NTS):
            sy.wait_ge(SEM["ve"], ve_v_base + (ts + 1) * (TSL // 128) * 2)
            sy.dma_start(out=ccv_in[ts * TSL:(ts + 1) * TSL, :].rearrange("(tsub p) (eh e) -> p tsub eh e", p=128, eh=2),
                         in_=stage[ts % 2][:].rearrange("p (tsub eh e) -> p tsub eh e", tsub=TSL // 128, eh=2)).then_inc(inc("st_v%d" % (ts % 2), 16), 16)
    block.sync(v_store)
    ST_V = [CNT["st_v0"], CNT["st_v1"]]

    block.gpsimd(lambda g: (
        g.wait_ge(SEM["st_v0"], ST_V[0]),
        g.wait_ge(SEM["st_v1"], ST_V[1]),
        g.collective_compute("AllGather", ALU.bypass, replica_groups=[[0, 1, 2, 3], [4, 5, 6, 7]],
                             ins=[ccv_in.ap().opt()], outs=[ccv_out.ap().opt()]).then_inc(inc("cc_v"), 1)))
    CC_V = CNT["cc_v"]

    def tap_dma(ap_out, src, waits):
        def f(sy):
            for sname, sval in waits:
                sy.wait_ge(SEM[sname], sval)
            sy.dma_start(out=ap_out, in_=src).then_inc(inc("dbgs", 16), 16)
        block.sync(f)

    t_ = tap("modcols", [128, 96])
    if t_ is not None:
        tap_dma(t_[:, 0:48], modc[:, 0:48], [("ve", VE_DERC)])
        tap_dma(t_[:, 48:96], derc[:, 0:48], [("ve", VE_DERC)])
    t_ = tap("ln1", [D, T], BF16)
    if t_ is not None:
        tap_dma(t_.rearrange("(np p) t -> p np t", p=128), ln1b[:].rearrange("p (np t) -> p np t", np=NP), [("ve", VE_LN1)])
    t_ = tap("qT", [D, T], BF16)
    if t_ is not None:
        tap_dma(t_, q_dram.ap(), [("st_q0", ST_Q[0]), ("st_q1", ST_Q[1])])
    t_ = tap("kTfull", [GRP * D, T], BF16)
    if t_ is not None:
        tap_dma(t_, cc_k_out.ap(), [("cc_k", CC_K)])
    t_ = tap("vfull", [S, D], BF16)
    if t_ is not None:
        tap_dma(t_, ccv_out.ap(), [("cc_v", CC_V)])
    t_ = tap("biasp", [128, 32])
    if t_ is not None:
        tap_dma(t_[:, 0:8], bqp[:], [("ve", VE_BVP)])
        tap_dma(t_[:, 8:16], bkp[:], [("ve", VE_BVP)])
        tap_dma(t_[:, 16:24], bvp[:], [("ve", VE_BVP)])
        tap_dma(t_[:, 24:32], scol[:, 0:8], [("ve", VE_RGCOL), ("ve", VE_G1)])

    def finish(extra_waits):
        def fin(sy):
            for sname, sval in extra_waits:
                sy.wait_ge(SEM[sname], sval)
            sy.wait_ge(SEM["dbgs"], CNT["dbgs"])
        block.sync(fin)
        blockctx.__exit__(None, None, None)
        for st in reversed(open_stacks):
            st.close()
        es.close()
        return nc, dbg

    if phases <= 3:
        block.vector(lambda v: (
            v.wait_ge(SEM["cc_v"], CC_V),
            v.memzero(bcs[0][:]),
            draininc(v, "ve")))
        VE_ZERO = CNT["ve"]

        def dummy_out(sy):
            sy.wait_ge(SEM["ve"], VE_ZERO)
            for dp in range(NP):
                for ts in range(NTS):
                    sy.dma_start(out=outT[dp * 128:(dp + 1) * 128, ts * TSL:(ts + 1) * TSL],
                                 in_=bcs[0][:]).then_inc(inc("st_out0", 16), 16)
        block.sync(dummy_out)
        return finish([("st_out0", CNT["st_out0"])])

    close_stack(stack_proj)
    sbuf_trace("close_stack")
    stack_att = new_stack()
    sba = lambda n, sh, dt=F32: stack_att.enter_context(nc.sbuf_tensor(n, sh, dt))
    Etile = sba("Etile", [128, NKT * QTS], BF16)
    qbuf = sba("qbuf", [128, NP * QTS], BF16)
    kbuf = [sba(f"kbuf{i}", [128, NP * KCH], BF16) for i in range(3)]
    vbufB = [sba(f"vbufB{i}", [128, KTPC * 512], BF16) for i in range(2)]
    vbufC = [sba(f"vbufC{i}", [128, KTPC * 512], BF16) for i in range(2)]
    ctxn = sba("ctxn", [128, NP * QTS], BF16)
    rsum = [sba(f"rsum{i}", [128, QTS], F32) for i in range(2)]
    t1w = sba("t1w", [128, NP * WCH], F32)
    t1q = sba("t1q", [128, NP * WCH], BF16)
    sbuf_trace("t1q")

    NWC1 = HD // WCH
    NWC2 = HD // WCH
    NTERN = NWC1 + NWC2
    tern_ve_marks = []

    tern_next = {"lq": 0, "st": 0}

    def emit_tern_lq(hi):
        for ci in range(tern_next["lq"], min(hi, NTERN)):
            wi, ch = (0, ci) if ci < NWC1 else (1, ci - NWC1)
            prev_q = tern_ve_marks[ci - 1] if ci >= 1 else None

            def g_load(sy, wi=wi, ch=ch, prev_q=prev_q, ci=ci):
                if ci == 0:
                    sy.wait_ge(SEM["st_v0"], ST_V[0])
                    sy.wait_ge(SEM["st_v1"], ST_V[1])
                if prev_q is not None:
                    sy.wait_ge(SEM["ve"], prev_q)
                if wi == 0:
                    src = w1T[:, ch * WCH:(ch + 1) * WCH].rearrange("(np p) h -> p np h", p=128)
                else:
                    src = w2T[ch * WCH:(ch + 1) * WCH, :].rearrange("(np p) e -> p np e", p=128)
                sy.dma_start(out=t1w[:].rearrange("p (np w) -> p np w", w=(WCH if wi == 0 else D)), in_=src).then_inc(inc("ld_wt%d" % (ci % 2), 16), 16)
            block.sync(g_load)
            ldsem = "ld_wt%d" % (ci % 2)
            ldval = CNT[ldsem]

            def v_q(v, ldsem=ldsem, ldval=ldval, wi=wi, ci=ci):
                v.wait_ge(SEM[ldsem], ldval)
                if ci == 0:
                    v.wait_ge(SEM["ve"], VE_RGCOL)
                if ci >= 1:
                    v.wait_ge(SEM["st_ta"], 16 * ci)
                v.tensor_scalar(t1w[:], t1w[:], scol[:, 6 + wi:7 + wi], MAGIC, op0=ALU.mult, op1=ALU.add)
                v.drain()
                v.tensor_scalar(t1w[:], t1w[:], MAGIC, 1.0, op0=ALU.subtract, op1=ALU.min)
                v.drain()
                v.tensor_scalar(t1q[:], t1w[:], -1.0, None, op0=ALU.max)
                draininc(v, "ve")
            block.vector(v_q)
            tern_ve_marks.append(CNT["ve"])
        tern_next["lq"] = min(hi, NTERN)

    def emit_tern_st(hi):
        for ci in range(tern_next["st"], min(hi, tern_next["lq"])):
            wi, ch = (0, ci) if ci < NWC1 else (1, ci - NWC1)

            def g_store(g, wi=wi, ch=ch, vq=tern_ve_marks[ci]):
                g.wait_ge(SEM["ve"], vq)
                if wi == 0:
                    dst = t1_dram[:, ch * WCH:(ch + 1) * WCH].rearrange("(np p) h -> p np h", p=128)
                else:
                    dst = t2_dram[ch * WCH:(ch + 1) * WCH, :].rearrange("(np p) e -> p np e", p=128)
                g.dma_start(out=dst, in_=t1q[:].rearrange("p (np w) -> p np w", w=(WCH if wi == 0 else D))).then_inc(inc("st_ta", 16), 16)
            block.gpsimd(g_store)
        tern_next["st"] = min(hi, tern_next["lq"])

    CPQ = -(-NTERN // NQT)
    CP3 = -(-CPQ // 3)

    SBANKS = [PB[0], PB[1], PB[5], PB[6]]
    prev_C_marks = None
    prev_exp_last = None
    prev_score_last = 0
    prev_B_last = 0
    prev_wo_ve = 0
    prev_rsum_ve = {0: 0, 1: 0}
    x2_ve_marks = []

    for qt in range(NQT):
        def q_load(sy, qt=qt, prev_score_last=prev_score_last):
            if qt == 0:
                sy.wait_ge(SEM["st_q0"], ST_Q[0])
                sy.wait_ge(SEM["st_q1"], ST_Q[1])
                sy.wait_ge(SEM["st_v0"], ST_V[0])
                sy.wait_ge(SEM["st_v1"], ST_V[1])
                sy.wait_ge(SEM["ve"], VE_V)
            else:
                sy.wait_ge(SEM["pe"], prev_score_last)
            sy.dma_start(out=qbuf[:].rearrange("p (np t) -> p np t", np=NP), in_=q_dram[:, qt * QTS:(qt + 1) * QTS].rearrange("(np p) t -> p np t", p=128)
                         ).then_inc(inc("ld_q", 16), 16)
        block.sync(q_load)
        LD_QT = CNT["ld_q"]

        exp_base = CNT["sc"]
        exp_marks = [exp_base + 1 + kt for kt in range(NKT)]
        kbase = [CNT["ld_k%d" % i] for i in range(3)]
        pe_scores = []
        pe_sums_last = None

        def passA_pe(t, qt=qt, kbase=kbase, exp_marks=exp_marks,
                     prev_exp_last=prev_exp_last, prev_rsum=prev_rsum_ve[(qt - 1) % 2] if qt > 0 else 0):
            t.wait_ge(SEM["ld_q"], LD_QT)
            if qt == 0:
                t.wait_ge(SEM["cc_k"], CC_K)
                t.wait_ge(SEM["ve"], VE_BVP)
            for kt in range(NKT):
                chunk = kt // KTPC
                slot = chunk % 3
                if kt % KTPC == 0:
                    t.wait_ge(SEM["ld_k%d" % slot], kbase[slot] + 16 * (chunk // 3 + 1))
                bank = SBANKS[kt % 4]
                if kt >= 4:
                    t.wait_ge(SEM["sc"], exp_marks[kt - 4])
                elif qt > 0:
                    t.wait_ge(SEM["sc"], prev_exp_last[NKT - 4 + kt])
                t.wait_ge(SEM["pe"], CNT["pe"])
                for dp in range(NP):
                    mm = t.matmul(bank[:, 0:QTS],
                                  kbuf[slot][:, dp * KCH + (kt % KTPC) * 128: dp * KCH + (kt % KTPC + 1) * 128],
                                  qbuf[:, dp * QTS:(dp + 1) * QTS],
                                  start=(dp == 0), stop=(dp == NP - 1))
                mm.then_inc(inc("pe"), 1)
                if kt >= 1:
                    t.wait_ge(SEM["sc"], exp_marks[kt - 1])
                    if kt == 1 and qt > 0:
                        t.wait_ge(SEM["ve"], prev_rsum)
                    t.wait_ge(SEM["pe"], CNT["pe"])
                    t.matmul(PB[2][:, 0:QTS], onesb[:], Etile[:, (kt - 1) * QTS: kt * QTS],
                             start=(kt - 1 == 0), stop=False).then_inc(inc("pe"), 1)
            t.wait_ge(SEM["sc"], exp_marks[NKT - 1])
            t.wait_ge(SEM["pe"], CNT["pe"])
            t.matmul(PB[2][:, 0:QTS], onesb[:], Etile[:, (NKT - 1) * QTS: NKT * QTS],
                     start=False, stop=True).then_inc(inc("pe"), 1)

        pe0 = CNT["pe"]
        block.tensor(passA_pe)
        n_inc = 0
        for kt in range(NKT):
            n_inc += 1
            pe_scores.append(pe0 + n_inc)
            if kt >= 1:
                n_inc += 1
        n_inc += 1
        assert CNT["pe"] == pe0 + n_inc, (CNT["pe"], pe0, n_inc)
        pe_sums_last = CNT["pe"]
        score_last = pe_scores[-1]

        def k_loads(sy, qt=qt, pe_scores=list(pe_scores), prev_score_last=prev_score_last):
            for kc in range(NKC):
                slot = kc % 3
                if kc >= 3:
                    sy.wait_ge(SEM["pe"], pe_scores[(kc - 2) * KTPC - 1])
                elif qt > 0:
                    sy.wait_ge(SEM["pe"], prev_score_last)
                else:
                    sy.wait_ge(SEM["cc_k"], CC_K)
                    sy.wait_ge(SEM["st_v0"], ST_V[0])
                    sy.wait_ge(SEM["st_v1"], ST_V[1])
                    sy.wait_ge(SEM["ve"], VE_V)
                r = (kc * KCH) // T
                c0 = (kc * KCH) % T
                src = cc_k_out[r * D:(r + 1) * D, c0:c0 + KCH].rearrange("(np p) t -> p np t", p=128)
                sy.dma_start(out=kbuf[slot][:].rearrange("p (np t) -> p np t", np=NP), in_=src).then_inc(inc("ld_k%d" % slot, 16), 16)
        block.sync(k_loads)

        def exp_fn(s_, qt=qt, pe_scores=list(pe_scores), prev_C_marks=prev_C_marks):
            for kt in range(NKT):
                s_.wait_ge(SEM["pe"], pe_scores[kt])
                if prev_C_marks is not None:
                    s_.wait_ge(SEM["pe"], prev_C_marks[kt])
                s_.activation(Etile[:, kt * QTS:(kt + 1) * QTS], SBANKS[kt % 4][:, 0:QTS], ACT.Exp, scale=1.0 / 32.0)
                draininc(s_, "sc")
        block.scalar(exp_fn)
        assert CNT["sc"] == exp_marks[-1], (CNT["sc"], exp_marks[-1])
        prev_exp_last = exp_marks

        def rsum_fn(v, qt=qt, pe_sums_last=pe_sums_last):
            v.wait_ge(SEM["pe"], pe_sums_last)
            v.reciprocal(rsum[qt % 2][:, 0:QTS], PB[2][:, 0:QTS])
            draininc(v, "ve")
        block.vector(rsum_fn)
        VE_RSUM = CNT["ve"]
        prev_rsum_ve[qt % 2] = VE_RSUM
        emit_tern_lq(qt * CPQ + CP3)

        pe_b0 = CNT["pe"]
        pe_B = [pe_b0 + 1 + kt for kt in range(NKT)]

        def passB_pe(t, qt=qt, vbase=[CNT["ld_vb%d" % i] for i in range(2)], prev_wo_ve=prev_wo_ve):
            if qt > 0:
                t.wait_ge(SEM["ve"], prev_wo_ve)
            for kt in range(NKT):
                chunk = kt // KTPC
                slot = chunk % 2
                if kt % KTPC == 0:
                    t.wait_ge(SEM["ld_vb%d" % slot], vbase[slot] + 16 * (chunk // 2 + 1))
                t.wait_ge(SEM["pe"], CNT["pe"])
                for ds in range(4):
                    mm = t.matmul(PB[3 + ds][:, 0:QTS],
                                  vbufB[slot][:, (kt % KTPC) * 512 + ds * 128: (kt % KTPC) * 512 + (ds + 1) * 128],
                                  Etile[:, kt * QTS:(kt + 1) * QTS],
                                  start=(kt == 0), stop=(kt == NKT - 1))
                mm.then_inc(inc("pe"), 1)
        block.tensor(passB_pe)
        assert CNT["pe"] == pe_B[-1]

        def b_loads(g, qt=qt, pe_B=list(pe_B), prev_B_last=prev_B_last):
            for kc in range(NKC):
                slot = kc % 2
                if kc >= 2:
                    g.wait_ge(SEM["pe"], pe_B[(kc - 1) * KTPC - 1])
                elif qt > 0:
                    g.wait_ge(SEM["pe"], prev_B_last)
                else:
                    g.wait_ge(SEM["cc_v"], CC_V)
                src = ccv_out[kc * KCH:(kc + 1) * KCH, 0:512].rearrange("(ts2 p) e -> p ts2 e", p=128)
                g.dma_start(out=vbufB[slot][:].rearrange("p (ts2 e) -> p ts2 e", ts2=KTPC), in_=src).then_inc(inc("ld_vb%d" % slot, 16), 16)
        block.gpsimd(b_loads)
        prev_B_last = pe_B[-1]

        def b_evac(v, qt=qt, pe_last=pe_B[-1]):
            v.wait_ge(SEM["pe"], pe_last)
            for ds in range(4):
                v.tensor_tensor(junk[ds % 2][:, 0:QTS], PB[3 + ds][:, 0:QTS], rsum[qt % 2][:, 0:QTS], ALU.mult)
                v.drain()
                v.tensor_scalar(ctxn[:, ds * QTS:(ds + 1) * QTS], junk[ds % 2][:, 0:QTS], bvp[:, ds:ds + 1], None, op0=ALU.add)
                v.drain()
            draininc(v, "ve")
        block.vector(b_evac)
        VE_BEV = CNT["ve"]
        emit_tern_lq(qt * CPQ + 2 * CP3)
        emit_tern_st(qt * CPQ + CP3)

        Cbanks = [PB[7], PB[0], PB[1], PB[2]]
        pe_c0 = CNT["pe"]
        pe_C = [pe_c0 + 1 + kt for kt in range(NKT)]

        def passC_pe(t, qt=qt, vbase=[CNT["ld_vc%d" % i] for i in range(2)], VE_RSUM=VE_RSUM):
            t.wait_ge(SEM["ve"], VE_RSUM)
            for kt in range(NKT):
                chunk = kt // KTPC
                slot = chunk % 2
                if kt % KTPC == 0:
                    t.wait_ge(SEM["ld_vc%d" % slot], vbase[slot] + 16 * (chunk // 2 + 1))
                t.wait_ge(SEM["pe"], CNT["pe"])
                for ds in range(4):
                    mm = t.matmul(Cbanks[ds][:, 0:QTS],
                                  vbufC[slot][:, (kt % KTPC) * 512 + ds * 128: (kt % KTPC) * 512 + (ds + 1) * 128],
                                  Etile[:, kt * QTS:(kt + 1) * QTS],
                                  start=(kt == 0), stop=(kt == NKT - 1))
                mm.then_inc(inc("pe"), 1)
        block.tensor(passC_pe)
        assert CNT["pe"] == pe_C[-1]
        prev_C_marks = list(pe_C)

        def c_loads(g, qt=qt, pe_C=list(pe_C), prev_C_first=pe_c0):
            for kc in range(NKC):
                slot = kc % 2
                if kc >= 2:
                    g.wait_ge(SEM["pe"], pe_C[(kc - 1) * KTPC - 1])
                elif qt == 0:
                    g.wait_ge(SEM["cc_v"], CC_V)
                src = ccv_out[kc * KCH:(kc + 1) * KCH, 512:1024].rearrange("(ts2 p) e -> p ts2 e", p=128)
                g.dma_start(out=vbufC[slot][:].rearrange("p (ts2 e) -> p ts2 e", ts2=KTPC), in_=src).then_inc(inc("ld_vc%d" % slot, 16), 16)
        block.gpsimd(c_loads)

        def c_evac(v, qt=qt, pe_last=pe_C[-1]):
            v.wait_ge(SEM["pe"], pe_last)
            for ds in range(4):
                v.tensor_tensor(junk[ds % 2][:, 0:QTS], Cbanks[ds][:, 0:QTS], rsum[qt % 2][:, 0:QTS], ALU.mult)
                v.drain()
                v.tensor_scalar(ctxn[:, (4 + ds) * QTS:(5 + ds) * QTS], junk[ds % 2][:, 0:QTS], bvp[:, 4 + ds:5 + ds],
                                None, op0=ALU.add)
                v.drain()
            draininc(v, "ve")
        block.vector(c_evac)
        VE_CEV = CNT["ve"]
        emit_tern_lq((qt + 1) * CPQ)
        emit_tern_st(qt * CPQ + 2 * CP3)

        ve_wo_base = CNT["ve"]
        pe_wo0 = CNT["pe"]
        pe_wo = [pe_wo0 + 1 + es2 for es2 in range(NP)]

        def wo_pe(t, qt=qt, VE_CEV=VE_CEV, ve_wo_base=ve_wo_base):
            t.wait_ge(SEM["ve"], VE_CEV)
            if qt == 0:
                t.wait_ge(SEM["ld_wb"], LD_WO)
            for es2 in range(NP):
                if es2 >= 2:
                    t.wait_ge(SEM["ve"], ve_wo_base + es2 - 1)
                bank = PB[3 + es2 % 2]
                t.wait_ge(SEM["pe"], CNT["pe"])
                for dp in range(NP):
                    mm = t.matmul(bank[:, 0:QTS],
                                  wbf_b[:, dp * D + es2 * 128: dp * D + (es2 + 1) * 128],
                                  ctxn[:, dp * QTS:(dp + 1) * QTS],
                                  start=(dp == 0), stop=(dp == NP - 1))
                mm.then_inc(inc("pe"), 1)
        block.tensor(wo_pe)
        assert CNT["pe"] == pe_wo[-1]

        def wo_evac(v, qt=qt, pe_wo=list(pe_wo)):
            for es2 in range(NP):
                v.wait_ge(SEM["pe"], pe_wo[es2])
                colx = es2 * T + qt * QTS
                v.tensor_scalar(junk[es2 % 2][:, 0:QTS], PB[3 + es2 % 2][:, 0:QTS], derc[:, es2:es2 + 1],
                                derc[:, 32 + es2:33 + es2], op0=ALU.mult, op1=ALU.add)
                v.drain()
                v.tensor_tensor(xt[:, colx:colx + QTS], junk[es2 % 2][:, 0:QTS], xt[:, colx:colx + QTS], ALU.add)
                draininc(v, "ve")
        block.vector(wo_evac)
        prev_wo_ve = CNT["ve"]
        x2_ve_marks.append(prev_wo_ve)
        emit_tern_st((qt + 1) * CPQ)
        prev_score_last = score_last
    emit_tern_lq(NTERN)
    emit_tern_st(NTERN)
    ST_T12 = (CNT["st_ta"], CNT["st_tb"])
    VE_ATT = CNT["ve"]
    PE_ATT = CNT["pe"]

    t_ = tap("x2", [D, T])
    if t_ is not None:
        tap_dma(t_.rearrange("(np p) t -> p np t", p=128), xt[:].rearrange("p (np t) -> p np t", np=NP), [("ve", VE_ATT)])

    if phases <= 4:
        def out4(sy):
            sy.wait_ge(SEM["ve"], VE_ATT)
            sy.wait_ge(SEM["st_ta"], ST_T12[0]); sy.wait_ge(SEM["st_tb"], ST_T12[1])
            for dp in range(NP):
                sy.dma_start(out=outT[dp * 128:(dp + 1) * 128, :], in_=xt[:, dp * T:(dp + 1) * T]
                             ).then_inc(inc("st_out0", 16), 16)
        block.sync(out4)
        return finish([("st_out0", CNT["st_out0"])])

    close_stack(stack_att)
    close_stack(stack_w)
    sbuf_trace("close_stack")
    stack_ffn = new_stack()
    qint1 = stack_ffn.enter_context(nc.sbuf_tensor("qint1", [128, NP * T], BF16))
    sbuf_trace("qint1")
    stack_z = new_stack()
    ztile = stack_z.enter_context(nc.sbuf_tensor("ztile", [128, NP * T], F32))
    castsq2 = qint1[:, 0:NP * TSL]
    qsc5 = stack_z.enter_context(nc.sbuf_tensor("qsc5", [128, TSL], F32))
    qrow5 = stack_z.enter_context(nc.sbuf_tensor("qrow5", [1, 16 * 128], F32))
    irow5 = [stack_z.enter_context(nc.sbuf_tensor(f"irow5_{i}", [1, 128], F32)) for i in range(10)]
    bc2 = [stack_z.enter_context(nc.sbuf_tensor(f"bc2_{i}", [128, TSL], F32)) for i in range(11)]
    sbuf_trace("bc2")

    ln2_state = {"pe_guard": PE_ATT, "first": True, "first_sc": True}
    castsq23 = castsq2.rearrange("p (dp t) -> p dp t", dp=NP)
    zt3 = ztile[:].rearrange("p (dp q) -> p dp q", dp=NP)

    def emit_ln_stats2(src3, eps, ve_guard):
        def sc_fn(s_):
            if ln2_state["first_sc"]:
                s_.wait_ge(SEM["st_ta"], ST_T12[0])
                s_.wait_ge(SEM["st_tb"], ST_T12[1])
                ln2_state["first_sc"] = False
            if ve_guard is not None:
                s_.wait_ge(SEM["ve"], ve_guard)
            if ln2_state["pe_guard"] is not None:
                s_.wait_ge(SEM["pe"], ln2_state["pe_guard"])
            s_.activation(castsq23, src3, ACT.Copy)
            draininc(s_, "sc")
        block.scalar(sc_fn)
        sc_cast = CNT["sc"]

        def t_fn1(t):
            t.wait_ge(SEM["sc"], sc_cast)
            if ln2_state["first"]:
                t.wait_ge(SEM["ve"], VE_ATT)
                ln2_state["first"] = False
            t.wait_ge(SEM["pe"], CNT["pe"])
            for dp in range(NP):
                mm = t.matmul(PB[0][:, 0:TSL], onesb[:], castsq2[:, dp * TSL:(dp + 1) * TSL],
                              start=(dp == 0), stop=(dp == NP - 1))
            mm.then_inc(inc("pe"), 1)
        block.tensor(t_fn1)
        pe_a = CNT["pe"]

        def sc_fn2(s_):
            s_.wait_ge(SEM["pe"], pe_a)
            s_.activation(castsq23, src3, ACT.Square)
            draininc(s_, "sc")
        block.scalar(sc_fn2)
        sc_sq = CNT["sc"]

        def t_fn2(t):
            t.wait_ge(SEM["sc"], sc_sq)
            t.wait_ge(SEM["pe"], CNT["pe"])
            for dp in range(NP):
                mm = t.matmul(PB[1][:, 0:TSL], onesb[:], castsq2[:, dp * TSL:(dp + 1) * TSL],
                              start=(dp == 0), stop=(dp == NP - 1))
            mm.then_inc(inc("pe"), 1)
        block.tensor(t_fn2)
        pe_st = CNT["pe"]
        ln2_state["pe_guard"] = pe_st

        block.vector(lambda v: (
            v.wait_ge(SEM["pe"], pe_st),
            v.tensor_scalar(bc2[0][:], PB[0][:, 0:TSL], 1.0 / D, None, op0=ALU.mult),
            v.tensor_scalar(bc2[1][:], PB[1][:, 0:TSL], 1.0 / D, None, op0=ALU.mult),
            v.drain(),
            v.tensor_tensor(bc2[2][:], bc2[0][:], bc2[0][:], ALU.mult),
            v.drain(),
            v.tensor_tensor(bc2[3][:], bc2[1][:], bc2[2][:], ALU.subtract),
            draininc(v, "ve")))
        ve_var = CNT["ve"]

        block.scalar(lambda s_: (
            s_.wait_ge(SEM["ve"], ve_var),
            s_.activation(bc2[4][:], bc2[3][:], ACT.Sqrt, bias=epscol[:, 0:1]),
            draininc(s_, "sc")))
        sc_sd = CNT["sc"]

        block.vector(lambda v: (
            v.wait_ge(SEM["sc"], sc_sd),
            v.reciprocal(bc2[5][:], bc2[4][:]),
            v.drain(),
            v.tensor_tensor(bc2[6][:], bc2[0][:], bc2[5][:], ALU.mult),
            draininc(v, "ve")))
        return CNT["ve"]

    for ts in range(NTS):
        ve_ln2 = emit_ln_stats2(xt3[:, :, ts * TSL:(ts + 1) * TSL], 1e-5, VE_ATT if ts == 0 else None)

        def y_fn(v, ts=ts, ve_ln2=ve_ln2):
            v.wait_ge(SEM["ve"], ve_ln2)
            z3s = zt3[:, :, ts * TSL:(ts + 1) * TSL]
            v.tensor_tensor(z3s, xt3[:, :, ts * TSL:(ts + 1) * TSL], bcast_t(bc2[5]), ALU.mult)
            v.drain()
            v.tensor_tensor(z3s, z3s, bcast_t(bc2[6]), ALU.subtract)
            v.drain()
            v.tensor_tensor(z3s, z3s, bcast_c(derc[:, 24:32]), ALU.mult)
            v.drain()
            v.tensor_tensor(z3s, z3s, bcast_c(modc[:, 24:32]), ALU.add)
            draininc(v, "ve")
        block.vector(y_fn)
        VE_Y = CNT["ve"]

        ve_ln3 = emit_ln_stats2(zt3[:, :, ts * TSL:(ts + 1) * TSL], 1e-5, VE_Y)

        def z_fn(v, ts=ts, ve_ln3=ve_ln3):
            v.wait_ge(SEM["ve"], ve_ln3)
            z3s = zt3[:, :, ts * TSL:(ts + 1) * TSL]
            v.tensor_tensor(z3s, z3s, bcast_t(bc2[5]), ALU.mult)
            v.drain()
            v.tensor_tensor(z3s, z3s, bcast_t(bc2[6]), ALU.subtract)
            v.drain()
            v.tensor_tensor(z3s, z3s, bcast_c(lngc[:, 0:NP]), ALU.mult)
            v.drain()
            v.tensor_tensor(z3s, z3s, bcast_c(lnbc[:, 0:NP]), ALU.add)
            v.drain()
            for dp in range(NP):
                colz = dp * T + ts * TSL
                a0, a1 = accs[dp % 2], accs[2 + dp % 2]
                v.tensor_scalar(junk[dp % 2][:], ztile[:, colz:colz + TSL], g1mat[:, dp:dp + 1], 0.0,
                                op0=ALU.is_gt, op1=ALU.add, accum_out=a0[:])
                v.drain()
                v.tensor_tensor(cnt1[:, dp:dp + 1], cnt1[:, dp:dp + 1], a0[:], ALU.add)
                v.tensor_scalar(junk[dp % 2][:], ztile[:, colz:colz + TSL], g1nmat[:, dp:dp + 1], 0.0,
                                op0=ALU.is_lt, op1=ALU.add, accum_out=a1[:])
                v.drain()
                v.tensor_tensor(cnt1[:, 8 + dp:9 + dp], cnt1[:, 8 + dp:9 + dp], a1[:], ALU.add)
                v.drain()
            draininc(v, "ve")
        block.vector(z_fn)
    VE_Z = CNT["ve"]

    def quantile_final(cnt_tile, ncol, cc_in, cc_out, ccsem, thr_row, target, scale_slot, ready, qrow, irow):
        block.sync(lambda sy: (
            sy.wait_ge(SEM[ready[0]], ready[1]),
            sy.dma_start(out=cc_in[:, :], in_=cnt_tile).then_inc(inc("st_cc", 16), 16)))
        stc = CNT["st_cc"]
        block.gpsimd(lambda g: (
            g.wait_ge(SEM["st_cc"], stc),
            g.collective_compute("AllReduce", ALU.add, replica_groups=[list(range(NCORE))],
                                 ins=[cc_in.ap().opt()], outs=[cc_out.ap().opt()]).then_inc(inc(ccsem), 1)))
        ccv = CNT[ccsem]
        block.sync(lambda sy: (
            sy.wait_ge(SEM[ccsem], ccv),
            sy.dma_start(out=cnt_ar[:, 0:ncol], in_=cc_out[:, :]).then_inc(inc("ld_cnt", 16), 16)))
        ldv = CNT["ld_cnt"]
        block.tensor(lambda t: (
            t.wait_ge(SEM["ld_cnt"], ldv),
            t.wait_ge(SEM["pe"], CNT["pe"]),
            t.transpose(PB[5][0:ncol, 0:128], cnt_ar[:, 0:ncol], identf[:]).then_inc(inc("pe"), 1)))
        pet = CNT["pe"]
        block.vector(lambda v: (
            v.wait_ge(SEM["pe"], pet),
            v.tensor_copy(trsb[0:ncol, 0:128], PB[5][0:ncol, 0:128]),
            draininc(v, "ve")))
        vet = CNT["ve"]
        block.sync(lambda sy: (
            sy.wait_ge(SEM["ve"], vet),
            sy.dma_start(out=flat_dram.rearrange("o (k p) -> (o k) p", p=128)[0:ncol, :],
                         in_=trsb[0:ncol, 0:128]).then_inc(inc("st_cc", 16), 16)))
        stf = CNT["st_cc"]
        block.sync(lambda sy: (
            sy.wait_ge(SEM["st_cc"], stf),
            sy.dma_start(out=qrow[0:1, 0:ncol * 128], in_=flat_dram[0:1, 0:ncol * 128]).then_inc(inc("ld_cnt", 16), 16)))
        ldf = CNT["ld_cnt"]

        def interp(v, ncol=ncol, thr_row=thr_row, target=float(target), scale_slot=scale_slot):
            v.wait_ge(SEM["ld_cnt"], ldf)
            first = True
            for k in range(ncol):
                kk = k % NSH
                s0 = (128 - 16 * kk) % 128
                if first:
                    v.memzero(lvlrow[0:1, 0:128])
                    v.drain()
                    first = False
                if s0 == 0:
                    v.tensor_tensor(lvlrow[0:1, 0:128], lvlrow[0:1, 0:128],
                                    qrow[0:1, k * 128:(k + 1) * 128], ALU.add)
                    v.drain()
                else:
                    v.tensor_tensor(lvlrow[0:1, 0:128 - s0], lvlrow[0:1, 0:128 - s0],
                                    qrow[0:1, k * 128 + s0:(k + 1) * 128], ALU.add)
                    v.drain()
                    v.tensor_tensor(lvlrow[0:1, 128 - s0:128], lvlrow[0:1, 128 - s0:128],
                                    qrow[0:1, k * 128: k * 128 + s0], ALU.add)
                    v.drain()
            c0 = lvlrow[0:1, 0:127]
            c1_ = lvlrow[0:1, 1:128]
            v.drain()
            v.tensor_scalar(irow[0][0:1, 0:127], c0, target, None, op0=ALU.is_ge)
            v.tensor_scalar(irow[1][0:1, 0:127], c1_, target, None, op0=ALU.is_lt)
            v.drain()
            v.tensor_tensor(irow[2][0:1, 0:127], irow[0][0:1, 0:127], irow[1][0:1, 0:127], ALU.mult)
            v.tensor_tensor(irow[3][0:1, 0:127], c0, c1_, ALU.subtract)
            v.drain()
            v.tensor_scalar(irow[4][0:1, 0:127], irow[3][0:1, 0:127], 0.5, None, op0=ALU.max)
            v.drain()
            v.reciprocal(irow[5][0:1, 0:127], irow[4][0:1, 0:127])
            v.tensor_scalar(irow[6][0:1, 0:127], c0, target, None, op0=ALU.subtract)
            v.drain()
            v.tensor_tensor(irow[6][0:1, 0:127], irow[6][0:1, 0:127], irow[5][0:1, 0:127], ALU.mult)
            v.tensor_tensor(irow[7][0:1, 0:127], thr_row[0:1, 1:128], thr_row[0:1, 0:127], ALU.subtract)
            v.drain()
            v.tensor_tensor(irow[7][0:1, 0:127], irow[7][0:1, 0:127], irow[6][0:1, 0:127], ALU.mult)
            v.drain()
            v.tensor_tensor(irow[7][0:1, 0:127], irow[7][0:1, 0:127], thr_row[0:1, 0:127], ALU.add)
            v.drain()
            v.tensor_tensor(irow[8][0:1, 0:127], irow[7][0:1, 0:127], irow[2][0:1, 0:127], ALU.mult)
            v.drain()
            v.tensor_reduce(irow[9][0:1, 0:1], irow[8][0:1, 0:127], AXX, ALU.add)
            v.tensor_reduce(irow[9][0:1, 1:2], irow[2][0:1, 0:127], AXX, ALU.add)
            v.tensor_scalar(irow[0][0:1, 1:2], lvlrow[0:1, 0:1], target, None, op0=ALU.is_lt)
            v.tensor_scalar(irow[1][0:1, 1:2], lvlrow[0:1, 127:128], target, None, op0=ALU.is_ge)
            v.drain()
            v.tensor_tensor(irow[0][0:1, 2:3], irow[0][0:1, 1:2], thr_row[0:1, 0:1], ALU.mult)
            v.tensor_tensor(irow[1][0:1, 2:3], irow[1][0:1, 1:2], thr_row[0:1, 127:128], ALU.mult)
            v.drain()
            v.tensor_tensor(irow[9][0:1, 0:1], irow[9][0:1, 0:1], irow[0][0:1, 2:3], ALU.add)
            v.drain()
            v.tensor_tensor(irow[9][0:1, 0:1], irow[9][0:1, 0:1], irow[1][0:1, 2:3], ALU.add)
            v.tensor_tensor(irow[9][0:1, 2:3], irow[9][0:1, 1:2], irow[0][0:1, 1:2], ALU.add)
            v.drain()
            v.tensor_tensor(irow[9][0:1, 2:3], irow[9][0:1, 2:3], irow[1][0:1, 1:2], ALU.add)
            v.drain()
            v.tensor_scalar(irow[9][0:1, 3:4], irow[9][0:1, 2:3], 1.0, None, op0=ALU.max)
            v.drain()
            v.reciprocal(irow[9][0:1, 4:5], irow[9][0:1, 3:4])
            v.drain()
            v.tensor_tensor(irow[9][0:1, 5:6], irow[9][0:1, 0:1], irow[9][0:1, 4:5], ALU.mult)
            v.drain()
            v.tensor_scalar(srow[0:1, scale_slot:scale_slot + 1], irow[9][0:1, 5:6], 1e-5, None, op0=ALU.max)
            draininc(v, "ve")
        block.vector(interp)
        return CNT["ve"]

    VE_AMAX1 = quantile_final(cnt1[:], 2 * NSH, cc_q1_in, cc_q1_out, "cc_q1", thr1row, Q1_TARGET, 16, ("ve", VE_Z),
                              qrow5, irow5)

    def q1_scalars(v):
        v.reciprocal(srow[0:1, 17:18], srow[0:1, 16:17])
        v.drain()
        v.tensor_scalar(srow[0:1, 18:19], srow[0:1, 17:18], 127.0, None, op0=ALU.mult)
        v.tensor_scalar(srow[0:1, 19:20], srow[0:1, 16:17], 1.0 / 127.0, None, op0=ALU.mult)
        v.drain()
        v.tensor_tensor(srow[0:1, 20:21], srow[0:1, 19:20], srow[0:1, 4:5], ALU.mult)
        draininc(v, "ve")
    block.vector(q1_scalars)
    VE_Q1S = CNT["ve"]

    block.tensor(lambda t: (
        t.wait_ge(SEM["ve"], VE_Q1S),
        t.wait_ge(SEM["pe"], CNT["pe"]),
        t.matmul(PB[6][:, 0:3], onesr[:], srow[0:1, 18:21], start=True, stop=True).then_inc(inc("pe"), 1)))
    PE_Q1B = CNT["pe"]

    block.vector(lambda v: (
        v.wait_ge(SEM["pe"], PE_Q1B),
        v.tensor_copy(scol[:, 0:1], PB[6][:, 0:1]),
        v.tensor_copy(scol[:, 1:2], PB[6][:, 2:3]),
        draininc(v, "ve")))
    VE_Q1COL = CNT["ve"]

    q1_sc_marks = []
    for ts in range(NTS):
        def quant1_sc(s_, ts=ts):
            if ts == 0:
                s_.wait_ge(SEM["ve"], VE_Q1COL)
            for dp in range(NP):
                colz = dp * T + ts * TSL
                s_.activation(qsc5[:], ztile[:, colz:colz + TSL], ACT.Copy, scale=scol[:, 0:1], bias=MAGIC)
                s_.drain()
                s_.activation(ztile[:, colz:colz + TSL], qsc5[:], ACT.Copy, bias=-MAGIC)
                s_.drain()
            draininc(s_, "sc")
        block.scalar(quant1_sc)
        q1_sc_marks.append(CNT["sc"])

    qint13 = qint1[:].rearrange("p (dp q) -> p dp q", dp=NP)
    q1_ve_marks = []
    for ts in range(NTS):
        def quant1(v, ts=ts):
            v.wait_ge(SEM["sc"], q1_sc_marks[ts])
            v.tensor_scalar(qint13[:, :, ts * TSL:(ts + 1) * TSL], zt3[:, :, ts * TSL:(ts + 1) * TSL],
                            127.0, -128.0, op0=ALU.min, op1=ALU.max)
            draininc(v, "ve")
        block.vector(quant1)
        q1_ve_marks.append(CNT["ve"])
    VE_QINT1 = CNT["ve"]

    t_ = tap("zq", [D, T])
    if t_ is not None:
        tap_dma(t_.rearrange("(np p) t -> p np t", p=128), ztile[:].rearrange("p (np t) -> p np t", np=NP), [("ve", VE_QINT1)])
    t_ = tap("qint1", [D, T], BF16)
    if t_ is not None:
        tap_dma(t_.rearrange("(np p) t -> p np t", p=128), qint1[:].rearrange("p (np t) -> p np t", np=NP), [("ve", VE_QINT1)])
    if phases <= 5:
        t_ = tap("amax", [1, 64])
        if t_ is not None:
            tap_dma(t_, srow[0:1, 0:64], [("ve", VE_QINT1)])
        t_ = tap("lvl1", [1, 128])
        if t_ is not None:
            tap_dma(t_, lvlrow[0:1, 0:128], [("ve", VE_QINT1)])

    if phases <= 5:
        def out5(sy):
            sy.wait_ge(SEM["ve"], VE_QINT1)
            sy.wait_ge(SEM["st_ta"], ST_T12[0]); sy.wait_ge(SEM["st_tb"], ST_T12[1])
            for dp in range(NP):
                sy.dma_start(out=outT[dp * 128:(dp + 1) * 128, :], in_=ztile[:, dp * T:(dp + 1) * T]
                             ).then_inc(inc("st_out0", 16), 16)
        block.sync(out5)
        return finish([("st_out0", CNT["st_out0"])])

    close_stack(stack_z)
    sbuf_trace("close_stack")
    stack_p6 = new_stack()
    retain = stack_p6.enter_context(nc.sbuf_tensor("retain", [128, NRETAIN * TSL], F32))
    gel = [stack_p6.enter_context(nc.sbuf_tensor(f"gel{i}", [128, TSL], F32)) for i in range(2)]
    t1c = [stack_p6.enter_context(nc.sbuf_tensor(f"t1c{i}", [128, NP * 128], BF16)) for i in range(2)]
    sqacc = stack_p6.enter_context(nc.sbuf_tensor("sqacc", [128, NRETAIN], F32))
    sqout = stack_p6.enter_context(nc.sbuf_tensor("sqout", [128, TSL], F32))
    qrow6 = stack_p6.enter_context(nc.sbuf_tensor("qrow6", [1, 16 * 128], F32))
    irow6 = [stack_p6.enter_context(nc.sbuf_tensor(f"irow6_{i}", [1, 128], F32)) for i in range(10)]
    sbuf_trace("sqout")

    NT2 = NPH * NTS
    NB6 = 6
    pe_w1_base = CNT["pe"]
    sc_g_base = CNT["sc"]

    def mark_w1(tau):
        return pe_w1_base + 1 + tau if tau < NRETAIN else pe_w1_base + 3 + tau

    def mark_gelu(tau):
        return sc_g_base + 1 + tau if tau < NRETAIN else sc_g_base + 4 + tau

    SC_G2M_MARK = sc_g_base + NRETAIN + 3

    t1base = [CNT["ld_t1c0"], CNT["ld_t1c1"]]

    N_EARLY_T1 = min(NPH, (NRETAIN - 1) // NTS + 2)

    def t1_loads_a(g):
        for hs in range(N_EARLY_T1):
            slot = hs % 2
            if hs < 2:
                g.wait_ge(SEM["st_ta"], ST_T12[0]); g.wait_ge(SEM["st_tb"], ST_T12[1])
                g.wait_ge(SEM["ve"], VE_QINT1)
            else:
                g.wait_ge(SEM["pe"], mark_w1((hs - 1) * NTS - 1))
            src = t1_dram[:, hs * 128:(hs + 1) * 128].rearrange("(np p) h -> p np h", p=128)
            g.dma_start(out=t1c[slot][:].rearrange("p (np h) -> p np h", np=NP), in_=src).then_inc(inc("ld_t1c%d" % slot, 16), 16)
    block.gpsimd(t1_loads_a)

    def w1_pe_range(t, tau_lo, tau_hi):
        for tau in range(tau_lo, tau_hi):
            hs, ts = tau // NTS, tau % NTS
            if hs == 0:
                t.wait_ge(SEM["ve"], q1_ve_marks[ts])
            slot = hs % 2
            if ts == 0:
                t.wait_ge(SEM["ld_t1c%d" % slot], t1base[slot] + 16 * (hs // 2 + 1))
            bank = PB[tau % NB6]
            if tau >= NB6:
                t.wait_ge(SEM["sc"], mark_gelu(tau - NB6))
            t.wait_ge(SEM["pe"], CNT["pe"])
            for dp in range(NP):
                mm = t.matmul(bank[:, 0:TSL], t1c[slot][:, dp * 128:(dp + 1) * 128],
                              qint1[:, dp * T + ts * TSL: dp * T + (ts + 1) * TSL],
                              start=(dp == 0), stop=(dp == NP - 1))
            mm.then_inc(inc("pe"), 1)

    block.tensor(lambda t: w1_pe_range(t, 0, NRETAIN))
    assert CNT["pe"] == mark_w1(NRETAIN - 1)

    st_ge_cnt = {0: CNT["st_ge0"], 1: CNT["st_ge1"]}

    def gelu_retained(s_):
        for tau in range(NRETAIN):
            s_.wait_ge(SEM["pe"], mark_w1(tau))
            s_.activation(retain[:, tau * TSL:(tau + 1) * TSL], PB[tau % NB6][:, 0:TSL], ACT.Gelu,
                          scale=scol[:, 1:2])
            s_.activation(sqout[:], PB[tau % NB6][:, 0:TSL], ACT.Square, accum_out=sqacc[:, tau:tau + 1])
            draininc(s_, "sc")
    block.scalar(gelu_retained)
    assert CNT["sc"] == mark_gelu(NRETAIN - 1)

    block.vector(lambda v: (
        v.wait_ge(SEM["sc"], mark_gelu(NRETAIN - 1)),
        v.tensor_reduce(smallC[:, 2:3], sqacc[:, 0:NRETAIN], AXX, ALU.add),
        draininc(v, "ve")))
    VE_Q2A = CNT["ve"]

    block.tensor(lambda t: (
        t.wait_ge(SEM["ve"], VE_Q2A),
        t.wait_ge(SEM["pe"], CNT["pe"]),
        t.matmul(PB[6][0:1, 0:1], onesc[:], smallC[:, 2:3], start=True, stop=True).then_inc(inc("pe"), 1)))
    assert CNT["pe"] == pe_w1_base + NRETAIN + 1

    block.vector(lambda v: (
        v.wait_ge(SEM["pe"], pe_w1_base + NRETAIN + 1),
        v.tensor_copy(srow[0:1, 24:25], PB[6][0:1, 0:1]),
        draininc(v, "ve")))
    VE_Q2AC = CNT["ve"]

    block.sync(lambda sy: (
        sy.wait_ge(SEM["ve"], VE_Q2AC),
        sy.dma_start(out=cc_q2a_in[:, :], in_=srow[0:1, 24:25]).then_inc(inc("st_q2a", 16), 16)))
    ST_Q2A = CNT["st_q2a"]

    block.gpsimd(lambda g: (
        g.wait_ge(SEM["st_q2a"], ST_Q2A),
        g.collective_compute("AllReduce", ALU.add, replica_groups=[list(range(NCORE))],
                             ins=[cc_q2a_in.ap().opt()], outs=[cc_q2a_out.ap().opt()]).then_inc(inc("cc_q2a"), 1)))
    CC_Q2A = CNT["cc_q2a"]

    def t1_loads_b(g):
        for hs in range(N_EARLY_T1, NPH):
            slot = hs % 2
            g.wait_ge(SEM["pe"], mark_w1((hs - 1) * NTS - 1))
            src = t1_dram[:, hs * 128:(hs + 1) * 128].rearrange("(np p) h -> p np h", p=128)
            g.dma_start(out=t1c[slot][:].rearrange("p (np h) -> p np h", np=NP), in_=src).then_inc(inc("ld_t1c%d" % slot, 16), 16)
    block.gpsimd(t1_loads_b)

    block.sync(lambda sy: (
        sy.wait_ge(SEM["cc_q2a"], CC_Q2A),
        sy.dma_start(out=srow[0:1, 25:26], in_=cc_q2a_out[:, :]).then_inc(inc("ld_q2a", 16), 16)))
    LD_Q2A = CNT["ld_q2a"]

    NSQ = NCORE * NRETAIN * 128 * TSL

    block.scalar(lambda s_: (
        s_.wait_ge(SEM["ld_q2a"], LD_Q2A),
        s_.activation(srow[0:1, 26:27], srow[0:1, 25:26], ACT.Sqrt, scale=1.0 / NSQ),
        draininc(s_, "sc")))
    SC_C2 = CNT["sc"]
    assert SC_C2 == sc_g_base + NRETAIN + 1

    block.vector(lambda v: (
        v.wait_ge(SEM["sc"], SC_C2),
        v.tensor_scalar(srow[0:1, 27:28], srow[0:1, 26:27], KAPPA2, None, op0=ALU.mult),
        v.drain(),
        v.tensor_tensor(srow[0:1, 28:29], srow[0:1, 27:28], srow[0:1, 20:21], ALU.mult),
        v.drain(),
        v.tensor_scalar(srow[0:1, 29:30], srow[0:1, 28:29], 0.5, None, op0=ALU.max),
        v.drain(),
        v.tensor_scalar(lvlrow[0:1, 0:NLVL], gfrow_sb[0:1, 0:NLVL], srow[0:1, 29:30], None, op0=ALU.mult),
        draininc(v, "ve")))
    VE_C2U = CNT["ve"]

    block.tensor(lambda t: (
        t.wait_ge(SEM["ve"], VE_C2U),
        t.wait_ge(SEM["pe"], CNT["pe"]),
        t.matmul(PB[7][:, 0:1], onesr[:], srow[0:1, 29:30], start=True, stop=True).then_inc(inc("pe"), 1)))
    PE_C2B = CNT["pe"]
    assert PE_C2B == pe_w1_base + NRETAIN + 2

    block.scalar(lambda s_: (
        s_.wait_ge(SEM["ve"], VE_C2U),
        s_.activation(g2row[0:1, 0:NLVL], lvlrow[0:1, 0:NLVL], ACT.Gelu),
        draininc(s_, "sc")))
    assert CNT["sc"] == sc_g_base + NRETAIN + 2

    block.vector(lambda v: (
        v.wait_ge(SEM["pe"], PE_C2B),
        v.tensor_copy(scol[:, 5:6], PB[7][:, 0:1]),
        v.drain(),
        v.tensor_scalar(smallC[:, 8:16], gfine_sb[:], scol[:, 5:6], None, op0=ALU.mult),
        draininc(v, "ve")))
    VE_G2U = CNT["ve"]

    block.scalar(lambda s_: (
        s_.wait_ge(SEM["ve"], VE_G2U),
        s_.activation(g2mat[:], smallC[:, 8:16], ACT.Gelu),
        draininc(s_, "sc")))
    assert CNT["sc"] == SC_G2M_MARK

    block.tensor(lambda t: w1_pe_range(t, NRETAIN, NT2))
    assert CNT["pe"] == mark_w1(NT2 - 1)

    def gelu_live(s_):
        for tau in range(NRETAIN, NT2):
            s_.wait_ge(SEM["pe"], mark_w1(tau))
            if tau - 2 >= 0:
                s_.wait_ge(SEM["st_ge%d" % (tau % 2)], st_ge_cnt[tau % 2] + 16 * ((tau - 2) // 2 + 1))
            if (tau - 2) - NRETAIN + 1 >= 1:
                s_.wait_ge(SEM["vcnt"], (tau - 2) - NRETAIN + 1)
            s_.activation(gel[tau % 2][:], PB[tau % NB6][:, 0:TSL], ACT.Gelu, scale=scol[:, 1:2])
            draininc(s_, "sc")
    block.scalar(gelu_live)
    assert CNT["sc"] == mark_gelu(NT2 - 1)

    def spill_fn(sy):
        for tau in range(NT2):
            hs, ts = tau // NTS, tau % NTS
            sy.wait_ge(SEM["sc"], mark_gelu(tau))
            src = retain[:, tau * TSL:(tau + 1) * TSL] if tau < NRETAIN else gel[tau % 2][:]
            sy.dma_start(out=spill_ge[hs * 128:(hs + 1) * 128, ts * TSL:(ts + 1) * TSL],
                         in_=src).then_inc(inc("st_ge%d" % (tau % 2), 16), 16)
    block.sync(spill_fn)
    ST_GE = [CNT["st_ge0"], CNT["st_ge1"]]

    def count_live(v):
        for tau in range(NRETAIN, NT2):
            v.wait_ge(SEM["sc"], mark_gelu(tau))
            if tau == NRETAIN:
                v.wait_ge(SEM["sc"], SC_G2M_MARK)
            k = tau % NSH
            a0 = accs[tau % 2]
            v.tensor_scalar(junk[tau % 2][:], gel[tau % 2][:], g2mat[:, k:k + 1], 0.0,
                            op0=ALU.is_gt, op1=ALU.add, accum_out=a0[:])
            v.drain()
            v.tensor_tensor(cnt2[:, k:k + 1], cnt2[:, k:k + 1], a0[:], ALU.add)
            draininc(v, "vcnt")
    block.vector(count_live)
    assert CNT["vcnt"] == NT2 - NRETAIN

    def count_retained(v):
        v.wait_ge(SEM["sc"], SC_G2M_MARK)
        for tau in range(NRETAIN):
            k = tau % NSH
            a0 = accs[tau % 2]
            v.tensor_scalar(junk[tau % 2][:], retain[:, tau * TSL:(tau + 1) * TSL], g2mat[:, k:k + 1], 0.0,
                            op0=ALU.is_gt, op1=ALU.add, accum_out=a0[:])
            v.drain()
            v.tensor_tensor(cnt2[:, k:k + 1], cnt2[:, k:k + 1], a0[:], ALU.add)
            v.drain()
        draininc(v, "ve")
    block.vector(count_retained)

    VE_AMAX2 = quantile_final(cnt2[:], NSH, cc_q2_in, cc_q2_out, "cc_q2", g2row, Q2_TARGET, 32, ("ve", CNT["ve"]),
                              qrow6, irow6)

    def q2_scalars(v):
        v.reciprocal(srow[0:1, 33:34], srow[0:1, 32:33])
        v.drain()
        v.tensor_scalar(srow[0:1, 34:35], srow[0:1, 33:34], 127.0, None, op0=ALU.mult)
        v.tensor_scalar(srow[0:1, 35:36], srow[0:1, 32:33], 1.0 / 127.0, None, op0=ALU.mult)
        v.drain()
        v.tensor_tensor(srow[0:1, 36:37], srow[0:1, 35:36], srow[0:1, 5:6], ALU.mult)
        draininc(v, "ve")
    block.vector(q2_scalars)
    VE_Q2S = CNT["ve"]

    block.tensor(lambda t: (
        t.wait_ge(SEM["ve"], VE_Q2S),
        t.wait_ge(SEM["pe"], CNT["pe"]),
        t.matmul(PB[6][:, 0:3], onesr[:], srow[0:1, 34:37], start=True, stop=True).then_inc(inc("pe"), 1)))
    PE_Q2B = CNT["pe"]

    block.vector(lambda v: (
        v.wait_ge(SEM["pe"], PE_Q2B),
        v.tensor_copy(scol[:, 2:3], PB[6][:, 0:1]),
        v.tensor_copy(scol[:, 3:4], PB[6][:, 2:3]),
        v.drain(),
        v.tensor_scalar(derc[:, 40:48], derc[:, 8:16], scol[:, 3:4], None, op0=ALU.mult),
        draininc(v, "ve")))
    VE_ESC = CNT["ve"]

    t_ = tap("amax", [1, 64])
    if t_ is not None:
        tap_dma(t_, srow[0:1, 0:64], [("ve", VE_ESC)])
    t_ = tap("geluspill", [HD, T])
    if t_ is not None:
        tap_dma(t_, spill_ge.ap(), [("st_ge0", ST_GE[0]), ("st_ge1", ST_GE[1])])

    if phases <= 6:
        def out6(sy):
            sy.wait_ge(SEM["ve"], VE_ESC)
            sy.wait_ge(SEM["st_ge0"], ST_GE[0])
            sy.wait_ge(SEM["st_ge1"], ST_GE[1])
            for dp in range(NP):
                sy.dma_start(out=outT[dp * 128:(dp + 1) * 128, :], in_=xt[:, dp * T:(dp + 1) * T]
                             ).then_inc(inc("st_out0", 16), 16)
        block.sync(out6)
        return finish([("st_out0", CNT["st_out0"])])

    close_stack(stack_p6)
    close_stack(stack_ffn)
    sbuf_trace("close_stack")
    stack_p7 = new_stack()
    q2sb = [stack_p7.enter_context(nc.sbuf_tensor(f"q2sb{i}", [128, NPH * TSL], BF16)) for i in range(2)]
    grb = [stack_p7.enter_context(nc.sbuf_tensor(f"grb{i}", [128, 4 * TSL], F32)) for i in range(2)]
    t2c = [stack_p7.enter_context(nc.sbuf_tensor(f"t2c{i}", [128, NPH * 128], BF16)) for i in range(3)]
    ost = [stack_p7.enter_context(nc.sbuf_tensor(f"ost{i}", [128, TSL], F32)) for i in range(2)]
    qsc7 = stack_p7.enter_context(nc.sbuf_tensor("qsc7", [128, 4 * TSL], F32))
    sbuf_trace("jq")

    NGC = NPH // 4
    ve_q2_marks = {}
    pe_w2_marks = {}
    ve_out_marks = {}

    def emit_quant_stage(ts):
        for gc in range(NGC):
            slot = gc % 2
            ntot = ts * NGC + gc

            def ge_load(g, ts=ts, gc=gc, slot=slot, ntot=ntot):
                if ntot >= 2:
                    pts, pgc = divmod(ntot - 2, NGC)
                    g.wait_ge(SEM["ve"], ve_q2_marks[(pts, pgc)])
                else:
                    g.wait_ge(SEM["st_ge0"], ST_GE[0])
                    g.wait_ge(SEM["st_ge1"], ST_GE[1])
                    g.wait_ge(SEM["ve"], VE_ESC)
                src = spill_ge[gc * 512:(gc + 1) * 512, ts * TSL:(ts + 1) * TSL].rearrange(
                    "(np4 p) t -> p np4 t", p=128)
                g.dma_start(out=grb[slot][:].rearrange("p (np4 t) -> p np4 t", np4=4), in_=src).then_inc(inc("ld_ge%d" % slot, 16), 16)
            block.sync(ge_load)
            ldval = CNT["ld_ge%d" % slot]

            def q2sc(s_, ts=ts, gc=gc, slot=slot, ldval=ldval):
                if ts == 0 and gc == 0:
                    s_.wait_ge(SEM["ve"], VE_ESC)
                s_.wait_ge(SEM["ld_ge%d" % slot], ldval)
                s_.activation(qsc7[:], grb[slot][:], ACT.Copy, scale=scol[:, 2:3], bias=MAGIC)
                s_.drain()
                s_.activation(grb[slot][:], qsc7[:], ACT.Copy, bias=-MAGIC)
                draininc(s_, "sc")
            block.scalar(q2sc)
            sc_q2 = CNT["sc"]

            def q2c(v, ts=ts, gc=gc, slot=slot, sc_q2=sc_q2):
                if gc == 0 and ts >= 2:
                    v.wait_ge(SEM["pe"], pe_w2_marks[(ts - 2, NP - 1)])
                v.wait_ge(SEM["sc"], sc_q2)
                v.tensor_scalar(q2sb[ts % 2][:, gc * 4 * TSL:(gc + 1) * 4 * TSL], grb[slot][:],
                                127.0, -128.0, op0=ALU.min, op1=ALU.max)
                draininc(v, "ve")
            block.vector(q2c)
            ve_q2_marks[(ts, gc)] = CNT["ve"]

    def emit_w2_stage(ts):
        for ds in range(NP):
            slot = (ts * NP + ds) % 3
            nldt = ts * NP + ds

            def t2_load(g, ts=ts, ds=ds, slot=slot, nldt=nldt):
                if nldt >= 3:
                    pts, pds = divmod(nldt - 3, NP)
                    g.wait_ge(SEM["pe"], pe_w2_marks[(pts, pds)])
                else:
                    g.wait_ge(SEM["st_ta"], ST_T12[0]); g.wait_ge(SEM["st_tb"], ST_T12[1])
                    g.wait_ge(SEM["ve"], VE_ESC)
                src = t2_dram[:, ds * 128:(ds + 1) * 128].rearrange("(nph p) e -> p nph e", p=128)
                g.dma_start(out=t2c[slot][:].rearrange("p (nph e) -> p nph e", nph=NPH), in_=src).then_inc(inc("ld_t2c%d" % slot, 16), 16)
            block.gpsimd(t2_load)
            t2val = CNT["ld_t2c%d" % slot]

            def w2_pe(t, ts=ts, ds=ds, slot=slot, t2val=t2val):
                t.wait_ge(SEM["ld_t2c%d" % slot], t2val)
                t.wait_ge(SEM["ve"], ve_q2_marks[(ts, NGC - 1)])
                nprev = ts * NP + ds - 2
                if nprev >= 0:
                    pts, pds = divmod(nprev, NP)
                    t.wait_ge(SEM["ve"], ve_out_marks[(pts, pds)])
                bank = PB[ds % 2 + 2]
                t.wait_ge(SEM["pe"], CNT["pe"])
                for hp in range(NPH):
                    mm = t.matmul(bank[:, 0:TSL], t2c[slot][:, hp * 128:(hp + 1) * 128],
                                  q2sb[ts % 2][:, hp * TSL:(hp + 1) * TSL],
                                  start=(hp == 0), stop=(hp == NPH - 1))
                mm.then_inc(inc("pe"), 1)
            block.tensor(w2_pe)
            pe_w2_marks[(ts, ds)] = CNT["pe"]

            def w2_evac(v, ts=ts, ds=ds):
                n = ts * NP + ds
                oslot = n % 2
                v.wait_ge(SEM["pe"], pe_w2_marks[(ts, ds)])
                if n >= 2:
                    v.wait_ge(SEM["st_out%d" % oslot], 16 * (n // 2))
                colx = ds * T + ts * TSL
                v.scalar_tensor_tensor(ost[oslot][:], PB[ds % 2 + 2][:, 0:TSL], derc[:, 40 + ds:41 + ds],
                                       xt[:, colx:colx + TSL], op0=ALU.mult, op1=ALU.add)
                draininc(v, "ve")
            block.vector(w2_evac)
            ve_out_marks[(ts, ds)] = CNT["ve"]

            def out_dma(s_, ts=ts, ds=ds):
                n = ts * NP + ds
                oslot = n % 2
                s_.wait_ge(SEM["ve"], ve_out_marks[(ts, ds)])
                s_.dma_start(out=outT[ds * 128:(ds + 1) * 128, ts * TSL:(ts + 1) * TSL],
                             in_=ost[oslot][:]).then_inc(inc("st_out%d" % oslot, 16), 16)
            block.scalar(out_dma)

    for stage in range(NTS + 1):
        if stage < NTS:
            emit_quant_stage(stage)
        if stage >= 1:
            emit_w2_stage(stage - 1)

    return finish([("st_out0", CNT["st_out0"]), ("st_out1", CNT["st_out1"]),
                   ("st_ge0", ST_GE[0]), ("st_ge1", ST_GE[1]),
                   ("st_ta", ST_T12[0]), ("st_tb", ST_T12[1])])


def shard_inputs(ins):
    consts = make_constants()
    x = np.asarray(ins["L0_attn_x"])
    wmod = np.asarray(ins["L0_wmod"])
    cc = np.asarray(ins["L0_c"])
    shared = {
        "wqT": np.ascontiguousarray(np.asarray(ins["L0_attn_Wq"]).T),
        "wkT": np.ascontiguousarray(np.asarray(ins["L0_attn_Wk"]).T),
        "wvT": np.ascontiguousarray(np.asarray(ins["L0_attn_Wv"]).T),
        "woT": np.ascontiguousarray(np.asarray(ins["L0_attn_Wo"]).T),
        "w1T": np.ascontiguousarray(np.asarray(ins["L0_ffn_w1"]).T),
        "w2T": np.ascontiguousarray(np.asarray(ins["L0_ffn_w2"]).T),
        "bq": np.ascontiguousarray(np.asarray(ins["L0_attn_bq"]).reshape(D, 1)),
        "bk": np.ascontiguousarray(np.asarray(ins["L0_attn_bk"]).reshape(D, 1)),
        "bv": np.ascontiguousarray(np.asarray(ins["L0_attn_bv"]).reshape(D, 1)),
        "bo": np.ascontiguousarray(np.asarray(ins["L0_attn_bo"]).reshape(D, 1)),
        "lng": np.ascontiguousarray(np.asarray(ins["L0_ffn_ln_g"]).reshape(D, 1)),
        "lnb": np.ascontiguousarray(np.asarray(ins["L0_ffn_ln_b"]).reshape(D, 1)),
    }
    shared.update(consts)
    CW1, CW2 = HD // NCORE, D // NCORE
    maps = []
    for core in range(NCORE):
        b, g = core // GRP, core % GRP
        m = dict(shared)
        m["xT"] = np.ascontiguousarray(x[b, g * T:(g + 1) * T, :].T)
        m["wmodsl"] = np.ascontiguousarray(wmod[:, g * MSL:(g + 1) * MSL])
        m["cTcol"] = np.ascontiguousarray(cc[b].reshape(D, 1))
        m["w1sl"] = np.ascontiguousarray(shared["w1T"][:, core * CW1:(core + 1) * CW1])
        m["w2sl"] = np.ascontiguousarray(shared["w2T"][:, core * CW2:(core + 1) * CW2])
        maps.append({k: np.ascontiguousarray(v) for k, v in m.items()})
    return maps


def kernel(**inputs):
    maps = shard_inputs(inputs)
    nc, _ = build_program(debug=tuple(DEBUG), phases=PHASES)
    res = run_bass_kernel_spmd(nc, maps, core_ids=list(range(NCORE)))
    out = np.empty((B, S, D), np.float32)
    for core in range(NCORE):
        b, g = core // GRP, core % GRP
        out[b, g * T:(g + 1) * T, :] = res.results[core]["outT"].T
    return out


# version 11
# speedup vs baseline: 1.3817x; 1.0463x over previous
import numpy as np
import ml_dtypes
from contextlib import ExitStack

import concourse.bass as bass
import concourse.mybir as mybir
from concourse.bass_utils import run_bass_kernel_spmd

F32 = mybir.dt.float32
BF16 = mybir.dt.bfloat16
ALU = mybir.AluOpType
ACT = mybir.ActivationFunctionType
AXX = mybir.AxisListType.X

MAGIC = 12582912.0
B = 2
D = 1024
HD = 4096
NCORE = 8
GRP = 4
S = 8192
T = 2048
NP = D // 128
NPH = HD // 128
TSL = 512
NTS = T // TSL
QTS = 512
NQT = T // QTS
NKT = S // 128
KCH = 256
NKC = S // KCH
KTPC = KCH // 128
MSL = 6 * D // GRP
WCH = 256

NSH = 8
NLVL = 128
Q1_TARGET = (B * S * D - 1.0) * 0.005 / NLVL
Q2_TARGET = (B * S * HD - 1.0) * 0.005 / NLVL
KAPPA1 = 3.05
KAPPA2 = 2.576
NRETAIN = 16

DEBUG: list = []
PHASES = 99
SBUF_TRACE = False


def _ffac_level(l):
    if 16 <= l < 112:
        return 0.92 + (l - 16) * (1.08 - 0.92) / 95.0
    if l < 16:
        return 0.35 * (0.92 / 0.35) ** (l / 16.0)
    return 1.08 * (2.6 / 1.08) ** ((l - 111) / 16.0)


def make_constants():
    ffac = np.array([_ffac_level(l) for l in range(NLVL)], np.float32)
    gfine = np.zeros((128, NSH), np.float32)
    for p in range(128):
        for k in range(NSH):
            gfine[p, k] = ffac[(p + 16 * k) % 128]
    return dict(
        gfine=gfine,
        gfrow=np.ascontiguousarray(ffac.reshape(1, NLVL)),
        ones_bf=np.ones((128, 128), ml_dtypes.bfloat16),
        ident_f32=np.eye(128, dtype=np.float32),
        ones_f32row=np.ones((1, 128), np.float32),
        ones_f32col=np.ones((128, 1), np.float32),
        eps_col=np.full((128, 1), 1e-5, np.float32),
    )


def build_program(debug=(), phases=99):
    nc = bass.Bass("TRN2", target_bir_lowering=False, debug=False)
    es = ExitStack()
    CNT = {}
    SEM = {}

    def mksem(name):
        SEM[name] = es.enter_context(nc.semaphore(name))
        CNT[name] = 0

    def inc(name, by=1):
        CNT[name] += by
        return SEM[name]

    def nopinc(eng, name):
        eng.wait_ge(SEM[name], CNT[name])
        eng.nop().then_inc(inc(name), 1)

    def draininc(eng, name):
        eng.wait_ge(SEM[name], CNT[name])
        eng.drain().then_inc(inc(name), 1)

    di = lambda n, sh, dt=F32: nc.dram_tensor(n, sh, dt, kind="ExternalInput").ap()
    do = lambda n, sh, dt=F32: nc.dram_tensor(n, sh, dt, kind="ExternalOutput").ap()
    dint = lambda n, sh, dt=F32: nc.dram_tensor(n, sh, dt)

    xTI = di("xT", [D, T])
    wqT = di("wqT", [D, D]); wkT = di("wkT", [D, D])
    wvT = di("wvT", [D, D]); woT = di("woT", [D, D])
    w1T = di("w1T", [D, HD]); w2T = di("w2T", [HD, D])
    w1slI = di("w1sl", [D, HD // NCORE]); w2slI = di("w2sl", [HD, D // NCORE])
    wmodslI = di("wmodsl", [D, MSL])
    cTcol = di("cTcol", [D, 1])
    bqI = di("bq", [D, 1]); bkI = di("bk", [D, 1]); bvI = di("bv", [D, 1]); boI = di("bo", [D, 1])
    lngI = di("lng", [D, 1]); lnbI = di("lnb", [D, 1])
    gfineI = di("gfine", [128, NSH]); gfrowI = di("gfrow", [1, NLVL])
    onesbI = di("ones_bf", [128, 128], BF16)
    identI = di("ident_f32", [128, 128])
    onesrI = di("ones_f32row", [1, 128])
    onescI = di("ones_f32col", [128, 1])
    epsI = di("eps_col", [128, 1])

    outT = do("outT", [D, T])

    cc_mod_in = dint("cc_mod_in", [1, MSL]); cc_mod_out = dint("cc_mod_out", [GRP, MSL])
    cc_k_in = dint("cc_k_in", [D, T], BF16); cc_k_out = dint("cc_k_out", [GRP * D, T], BF16)
    ccv_in = dint("ccv_in", [T, D], BF16); ccv_out = dint("ccv_out", [S, D], BF16)
    cc_g_in = dint("cc_g_in", [1, 2]); cc_g_out = dint("cc_g_out", [1, 2])
    cc_q1_in = dint("cc_q1_in", [128, 2 * NSH]); cc_q1_out = dint("cc_q1_out", [128, 2 * NSH])
    cc_q2a_in = dint("cc_q2a_in", [1, 1]); cc_q2a_out = dint("cc_q2a_out", [1, 1])
    cc_q2_in = dint("cc_q2_in", [128, NSH]); cc_q2_out = dint("cc_q2_out", [128, NSH])
    q_dram = dint("q_dram", [D, T], BF16)
    spill_ge = dint("spill_ge", [HD, T], BF16)
    t1_dram = dint("t1_dram", [D, HD], BF16)
    t2_dram = dint("t2_dram", [HD, D], BF16)
    flat_dram = dint("flat_dram", [1, 16 * 128])

    dbg = {}
    def tap(name, sh, dt=F32):
        if name in debug and name not in dbg:
            dbg[name] = do("dbg_" + name, sh, dt)
        return dbg.get(name)

    for s in ["ld_const", "ld_x", "ld_wmod", "ld_wa", "ld_wb", "ld_gsl", "ld_gam",
              "st_cc", "ld_modc",
              "st_k0", "st_k1", "st_v0", "st_v1", "st_q0", "st_q1",
              "cc_mod", "cc_k", "cc_v", "cc_g", "cc_q1", "cc_q2a", "cc_q2",
              "ld_q",
              "ld_k0", "ld_k1", "ld_k2", "ld_k3", "ld_vb0", "ld_vb1", "ld_vb2",
              "ld_vc0", "ld_vc1", "ld_vc2", "ld_t2c2",
              "ld_wt0", "ld_wt1", "st_ta", "st_tb", "ld_t1c0", "ld_t1c1",
              "ld_t2c0", "ld_t2c1",
              "st_ge0", "st_ge1", "ld_ge0", "ld_ge1", "ld_ge2",
              "ld_cnt", "st_out0", "st_out1", "dbgs", "st_q2a", "ld_q2a",
              "pe", "ve", "sc", "gp", "vcnt"]:
        mksem(s)

    sbp = lambda n, sh, dt=F32: es.enter_context(nc.sbuf_tensor(n, sh, dt))
    xt = sbp("xt", [128, NP * T])
    modc = sbp("modc", [128, 48])
    derc = sbp("derc", [128, 48])
    c_sl = sbp("c_sl", [128, NP])
    bqc = sbp("bqc", [128, NP]); bkc = sbp("bkc", [128, NP]); bvc = sbp("bvc", [128, NP]); boc = sbp("boc", [128, NP])
    lngc = sbp("lngc", [128, NP]); lnbc = sbp("lnbc", [128, NP])
    bqp = sbp("bqp", [128, NP]); bkp = sbp("bkp", [128, NP]); bvp = sbp("bvp", [128, NP])
    onesb = sbp("onesb", [128, 128], BF16)
    identf = sbp("identf", [128, 128])
    onesr = sbp("onesr", [1, 128])
    onesc = sbp("onesc", [128, 1])
    epscol = sbp("epscol", [128, 1])
    gfine_sb = sbp("gfine_sb", [128, NSH]); gfrow_sb = sbp("gfrow_sb", [1, NLVL])
    g1mat = sbp("g1mat", [128, NSH]); g1nmat = sbp("g1nmat", [128, NSH])
    g2mat = sbp("g2mat", [128, NSH]); g2row = sbp("g2row", [1, NLVL])
    cnt1 = sbp("cnt1", [128, 2 * NSH])
    cnt2 = sbp("cnt2", [128, NSH])
    accs = [sbp(f"acc{i}", [128, 1]) for i in range(4)]
    srow = sbp("srow", [1, 64])
    lvlrow = sbp("lvlrow", [1, 128])
    thr1row = sbp("thr1row", [1, NLVL])
    scol = sbp("scol", [128, 16])
    gpart = sbp("gpart", [128, 2])
    cnt_ar = sbp("cnt_ar", [128, 16])
    trsb = sbp("trsb", [16, 128])
    smallA = sbp("smallA", [128, 16]); smallB = sbp("smallB", [128, 16]); smallC = sbp("smallC", [128, 16])
    junk = [sbp(f"junk{i}", [128, TSL]) for i in range(2)]
    c_sib = sbp("c_sib", [128, NP], F32)
    sa_b = sbp("sa_b", [128, NP], BF16)

    PB = [es.enter_context(nc.psum_tensor(f"pb{i}", [128, 512], F32)) for i in range(8)]

    open_stacks = []

    def new_stack():
        st = ExitStack()
        open_stacks.append(st)
        return st

    def close_stack(st):
        open_stacks.remove(st)
        st.close()

    def sbuf_trace(where):
        if SBUF_TRACE:
            print(f"[sbuf] {where}: remaining={nc.sbuf_bytes_remaining}")

    stack_w = new_stack()
    wbf_b = stack_w.enter_context(nc.sbuf_tensor("wbf_b", [128, NP * D], BF16))

    blockctx = nc.Block()
    block = blockctx.__enter__()

    def ph0_sy(sy):
        for dst, src in [(onesb, onesbI), (identf, identI), (gfine_sb, gfineI)]:
            sy.dma_start(out=dst[:], in_=src).then_inc(inc("ld_const", 16), 16)
        sy.dma_start(out=onesr[:], in_=onesrI).then_inc(inc("ld_const", 16), 16)
        sy.dma_start(out=onesc[:], in_=onescI).then_inc(inc("ld_const", 16), 16)
        sy.dma_start(out=epscol[:], in_=epsI).then_inc(inc("ld_const", 16), 16)
        sy.dma_start(out=gfrow_sb[0:1, 0:NLVL], in_=gfrowI).then_inc(inc("ld_const", 16), 16)
        sy.dma_start(out=c_sl[:].rearrange("p (np o) -> p np o", o=1), in_=cTcol.rearrange("(np p) o -> p np o", p=128), allow_slow_non_contiguous=True).then_inc(inc("ld_const", 16), 16)
        for dst, src in [(bqc, bqI), (bkc, bkI), (bvc, bvI), (boc, boI), (lngc, lngI), (lnbc, lnbI)]:
            sy.dma_start(out=dst[:].rearrange("p (np o) -> p np o", o=1), in_=src.rearrange("(np p) o -> p np o", p=128), allow_slow_non_contiguous=True).then_inc(inc("ld_const", 16), 16)
    block.sync(ph0_sy)
    LD_CONST = CNT["ld_const"]

    stack_proj = new_stack()
    wbf_a = stack_proj.enter_context(nc.sbuf_tensor("wbf_a", [128, NP * D], BF16))
    sbuf_trace("wbf_a")
    stack_gam = new_stack()
    gbuf = [stack_gam.enter_context(nc.sbuf_tensor(f"gbuf{i}", [128, 2048], F32)) for i in range(2)]
    sbuf_trace("gbuf")
    gtmp = stack_gam.enter_context(nc.sbuf_tensor("gtmp", [128, 4], F32))

    def ph0_g(g):
        g.dma_start(out=wbf_a[:].rearrange("p (np e) -> p np e", np=NP), in_=wqT.rearrange("(np p) e -> p np e", p=128)).then_inc(inc("ld_wa", 16), 16)
        g.dma_start(out=wbf_b[:].rearrange("p (np e) -> p np e", np=NP), in_=wkT.rearrange("(np p) e -> p np e", p=128)).then_inc(inc("ld_wb", 16), 16)
    block.gpsimd(ph0_g)
    LD_WQ = CNT["ld_wa"]; LD_WK = CNT["ld_wb"]

    gchunks = [
        lambda: w1slI[:, 0:256].rearrange("(np p) h -> p np h", p=128),
        lambda: w1slI[:, 256:512].rearrange("(np p) h -> p np h", p=128),
        lambda: w2slI[:, 0:64].rearrange("(nph p) e -> p nph e", p=128),
        lambda: w2slI[:, 64:128].rearrange("(nph p) e -> p nph e", p=128),
    ]
    block.vector(lambda v: (v.memzero(srow[:]), draininc(v, "ve")))
    gred_marks = []
    for gi in range(4):
        prev_mark = gred_marks[gi - 2] if gi >= 2 else None

        gsem = "ld_gsl" if gi % 2 == 0 else "ld_gam"

        def g_ld(sy, gi=gi, prev_mark=prev_mark, gsem=gsem):
            if prev_mark is not None:
                sy.wait_ge(SEM["ve"], prev_mark)
            sy.dma_start(out=gbuf[gi % 2][:].rearrange("p (a b) -> p a b", a=(NP if gi < 2 else NPH)),
                         in_=gchunks[gi]()).then_inc(inc(gsem, 16), 16)
        block.sync(g_ld)
        ldv = CNT[gsem]

        def g_red(v, gi=gi, ldv=ldv, gsem=gsem):
            v.wait_ge(SEM[gsem], ldv)
            v.tensor_reduce(gtmp[:, gi:gi + 1], gbuf[gi % 2][:], AXX, ALU.add, apply_absolute_value=True)
            draininc(v, "ve")
        block.vector(g_red)
        gred_marks.append(CNT["ve"])

    block.sync(lambda sy: sy.dma_start(out=xt[:].rearrange("p (np t) -> p np t", np=NP),
                                       in_=xTI.rearrange("(np p) t -> p np t", p=128)).then_inc(inc("ld_x", 16), 16))
    LD_X = CNT["ld_x"]

    block.vector(lambda v: (
        v.tensor_tensor(gpart[:, 0:1], gtmp[:, 0:1], gtmp[:, 1:2], ALU.add),
        v.tensor_tensor(gpart[:, 1:2], gtmp[:, 2:3], gtmp[:, 3:4], ALU.add),
        draininc(v, "ve")))
    VE_GP = CNT["ve"]

    block.tensor(lambda t: (
        t.wait_ge(SEM["ve"], VE_GP),
        t.wait_ge(SEM["ld_const"], LD_CONST),
        t.wait_ge(SEM["pe"], CNT["pe"]),
        t.matmul(PB[7][0:1, 0:2], onesc[:], gpart[:], start=True, stop=True).then_inc(inc("pe"), 1)))
    PE_GP = CNT["pe"]

    block.vector(lambda v: (
        v.wait_ge(SEM["pe"], PE_GP),
        v.tensor_copy(smallA[0:1, 0:2], PB[7][0:1, 0:2]),
        draininc(v, "ve")))
    VE_GPC = CNT["ve"]
    close_stack(stack_gam)
    sbuf_trace("close_stack")

    block.sync(lambda sy: (
        sy.wait_ge(SEM["ve"], VE_GPC),
        sy.dma_start(out=cc_g_in[:, :], in_=smallA[0:1, 0:2]).then_inc(inc("st_cc", 16), 16)))
    ST_G = CNT["st_cc"]

    block.gpsimd(lambda g: (
        g.wait_ge(SEM["st_cc"], ST_G),
        g.collective_compute("AllReduce", ALU.add, replica_groups=[list(range(NCORE))],
                             ins=[cc_g_in.ap().opt()], outs=[cc_g_out.ap().opt()]).then_inc(inc("cc_g"), 1)))
    CC_G = CNT["cc_g"]

    block.sync(lambda sy: (
        sy.wait_ge(SEM["cc_g"], CC_G),
        sy.dma_start(out=smallB[0:1, 0:2], in_=cc_g_out[:, :]).then_inc(inc("ld_gam", 16), 16)))
    LD_GAM = CNT["ld_gam"]

    def gam_scalars(v):
        v.wait_ge(SEM["ld_gam"], LD_GAM)
        v.tensor_scalar(srow[0:1, 4:5], smallB[0:1, 0:1], 1.0 / (HD * D), 1e-5, op0=ALU.mult, op1=ALU.add)
        v.tensor_scalar(srow[0:1, 5:6], smallB[0:1, 1:2], 1.0 / (HD * D), 1e-5, op0=ALU.mult, op1=ALU.add)
        v.drain()
        v.reciprocal(srow[0:1, 6:7], srow[0:1, 4:5])
        v.reciprocal(srow[0:1, 7:8], srow[0:1, 5:6])
        draininc(v, "ve")
    block.vector(gam_scalars)
    VE_GAMS = CNT["ve"]

    block.tensor(lambda t: (
        t.wait_ge(SEM["ve"], VE_GAMS),
        t.wait_ge(SEM["ve"], VE_GPC),
        t.wait_ge(SEM["pe"], CNT["pe"]),
        t.matmul(PB[7][:, 0:2], onesr[:], srow[0:1, 6:8], start=True, stop=True).then_inc(inc("pe"), 1)))
    PE_RG = CNT["pe"]

    block.vector(lambda v: (
        v.wait_ge(SEM["pe"], PE_RG),
        v.tensor_copy(scol[:, 6:8], PB[7][:, 0:2]),
        draininc(v, "ve")))
    VE_RGCOL = CNT["ve"]

    block.scalar(lambda s_: (
        s_.wait_ge(SEM["ld_const"], LD_CONST),
        s_.activation(c_sib[:], c_sl[:], ACT.Silu),
        draininc(s_, "sc")))
    SC_SILU = CNT["sc"]
    stack_wmod = new_stack()
    wmodc = stack_wmod.enter_context(nc.sbuf_tensor("wmodc", [128, NP * 512], F32))
    mod_row = stack_wmod.enter_context(nc.sbuf_tensor("mod_row", [1, MSL], F32))
    sbuf_trace("wmodc")

    NMOD = MSL // 512
    pe_mod_base = CNT["pe"]
    for m in range(NMOD):
        def wm_ld(g, m=m):
            if m > 0:
                g.wait_ge(SEM["pe"], pe_mod_base + m)
            src = wmodslI[:, m * 512:(m + 1) * 512].rearrange("(np p) m2 -> p np m2", p=128)
            g.dma_start(out=wmodc[:].rearrange("p (np m2) -> p np m2", np=NP), in_=src).then_inc(inc("ld_wmod", 16), 16)
        block.gpsimd(wm_ld)

    ve_modevac_base = CNT["ve"]

    def mod_mm(t):
        t.wait_ge(SEM["sc"], SC_SILU)
        t.wait_ge(SEM["ve"], VE_RGCOL)
        for m in range(NMOD):
            t.wait_ge(SEM["ld_wmod"], 16 * (m + 1))
            if m >= 2:
                t.wait_ge(SEM["ve"], ve_modevac_base + m - 1)
            bank = PB[6 + m % 2]
            t.wait_ge(SEM["pe"], CNT["pe"])
            for dp in range(NP):
                mm = t.matmul(bank[0:1, 0:512], c_sib[:, dp:dp + 1],
                              wmodc[:, dp * 512:(dp + 1) * 512],
                              start=(dp == 0), stop=(dp == NP - 1))
            mm.then_inc(inc("pe"), 1)
    block.tensor(mod_mm)
    PE_MODS = [pe_mod_base + 1 + i for i in range(NMOD)]

    def mod_evac(v):
        for m in range(NMOD):
            v.wait_ge(SEM["pe"], PE_MODS[m])
            v.tensor_copy(mod_row[0:1, m * 512:(m + 1) * 512], PB[6 + m % 2][0:1, 0:512])
            draininc(v, "ve")
    block.vector(mod_evac)
    VE_MODR = CNT["ve"]

    block.sync(lambda sy: (
        sy.wait_ge(SEM["ve"], VE_MODR),
        sy.dma_start(out=cc_mod_in[:, :], in_=mod_row[:]).then_inc(inc("st_cc", 16), 16)))
    ST_MOD = CNT["st_cc"]
    close_stack(stack_wmod)
    sbuf_trace("close_stack")

    block.gpsimd(lambda g: (
        g.wait_ge(SEM["st_cc"], ST_MOD),
        g.collective_compute("AllGather", ALU.bypass, replica_groups=[[0, 1, 2, 3], [4, 5, 6, 7]],
                             ins=[cc_mod_in.ap().opt()], outs=[cc_mod_out.ap().opt()]).then_inc(inc("cc_mod"), 1)))
    CC_MOD = CNT["cc_mod"]

    block.sync(lambda sy: (
        sy.wait_ge(SEM["cc_mod"], CC_MOD),
        sy.dma_start(out=modc[:, 0:48].rearrange("p (v np) -> p v np", np=NP),
                     in_=cc_mod_out.rearrange("g m -> (g m)").rearrange("(v np p) -> p v np", p=128, np=NP),
                     allow_slow_non_contiguous=True).then_inc(inc("ld_modc", 16), 16)))
    LD_MODC = CNT["ld_modc"]

    def derc_fn(v):
        v.wait_ge(SEM["ld_modc"], LD_MODC)
        v.memzero(derc[:])
        v.drain()
        for i, vsrc in [(0, 2), (1, 5)]:
            gsl = modc[:, vsrc * 8:(vsrc + 1) * 8]
            dcl = derc[:, i * 8:(i + 1) * 8]
            v.tensor_scalar(smallA[:, 0:8], gsl, 0.0, None, op0=ALU.is_ge)
            v.tensor_scalar(smallB[:, 0:8], gsl, -0.25, 0.25, op0=ALU.max, op1=ALU.min)
            v.drain()
            v.tensor_scalar(smallC[:, 0:8], smallA[:, 0:8], 2.0, -1.0, op0=ALU.mult, op1=ALU.add)
            v.drain()
            v.tensor_scalar(smallC[:, 8:16], smallC[:, 0:8], 0.25, None, op0=ALU.mult)
            v.drain()
            v.tensor_tensor(dcl, gsl, smallC[:, 8:16], ALU.add)
            v.drain()
            v.tensor_tensor(dcl, dcl, smallB[:, 0:8], ALU.subtract)
            v.drain()
        v.tensor_scalar(derc[:, 16:24], modc[:, 8:16], 1.0, None, op0=ALU.add)
        v.tensor_scalar(derc[:, 24:32], modc[:, 32:40], 1.0, None, op0=ALU.add)
        v.tensor_tensor(derc[:, 32:40], derc[:, 0:8], boc[:], ALU.mult)
        draininc(v, "ve")
    block.vector(derc_fn)
    VE_DERC = CNT["ve"]

    block.scalar(lambda s_: (
        s_.wait_ge(SEM["ld_const"], LD_CONST),
        s_.activation(smallA[:, 8:16], lngc[:], ACT.Square),
        s_.activation(smallB[:, 8:16], lnbc[:], ACT.Square),
        draininc(s_, "sc")))
    SC_GB2 = CNT["sc"]

    block.vector(lambda v: (
        v.wait_ge(SEM["sc"], SC_GB2),
        v.tensor_tensor(smallC[:, 8:16], smallA[:, 8:16], smallB[:, 8:16], ALU.add),
        v.drain(),
        v.tensor_reduce(smallC[:, 1:2], smallC[:, 8:16], AXX, ALU.add),
        draininc(v, "ve")))
    VE_C1A = CNT["ve"]

    block.tensor(lambda t: (
        t.wait_ge(SEM["ve"], VE_C1A),
        t.wait_ge(SEM["pe"], CNT["pe"]),
        t.matmul(PB[5][0:1, 0:1], onesc[:], smallC[:, 1:2], start=True, stop=True).then_inc(inc("pe"), 1)))
    PE_C1 = CNT["pe"]

    block.scalar(lambda s_: (
        s_.wait_ge(SEM["pe"], PE_C1),
        s_.activation(srow[0:1, 0:1], PB[5][0:1, 0:1], ACT.Sqrt, scale=1.0 / D),
        draininc(s_, "sc")))
    SC_C1 = CNT["sc"]

    block.vector(lambda v: (
        v.wait_ge(SEM["sc"], SC_C1),
        v.tensor_scalar(srow[0:1, 1:2], srow[0:1, 0:1], KAPPA1, None, op0=ALU.mult),
        v.drain(),
        v.tensor_scalar(thr1row[0:1, 0:NLVL], gfrow_sb[0:1, 0:NLVL], srow[0:1, 1:2], None, op0=ALU.mult),
        draininc(v, "ve")))
    VE_C1B = CNT["ve"]

    block.tensor(lambda t: (
        t.wait_ge(SEM["ve"], VE_C1B),
        t.wait_ge(SEM["pe"], CNT["pe"]),
        t.matmul(PB[5][:, 0:1], onesr[:], srow[0:1, 1:2], start=True, stop=True).then_inc(inc("pe"), 1)))
    PE_C1B = CNT["pe"]

    block.vector(lambda v: (
        v.wait_ge(SEM["pe"], PE_C1B),
        v.tensor_copy(scol[:, 4:5], PB[5][:, 0:1]),
        v.drain(),
        v.tensor_scalar(g1mat[:], gfine_sb[:], scol[:, 4:5], None, op0=ALU.mult),
        v.drain(),
        v.tensor_scalar(g1nmat[:], g1mat[:], -1.0, None, op0=ALU.mult),
        v.memzero(cnt1[:]),
        v.memzero(cnt2[:]),
        draininc(v, "ve")))
    VE_G1 = CNT["ve"]

    block.vector(lambda v: (
        v.wait_ge(SEM["ld_modc"], LD_MODC),
        v.tensor_copy(sa_b[:], modc[:, 0:8]),
        draininc(v, "ve")))
    VE_SAB = CNT["ve"]

    def bias_mod(wtile, wsem, wval, bc, bp):
        ve_now = CNT["ve"]

        def tm(t):
            t.wait_ge(SEM["ve"], VE_SAB)
            t.wait_ge(SEM["ve"], ve_now)
            t.wait_ge(SEM[wsem], wval)
            t.wait_ge(SEM["pe"], CNT["pe"])
            for es2 in range(NP):
                for dp in range(NP):
                    mm = t.matmul(PB[4][0:128, es2:es2 + 1],
                                  wtile[:, dp * D + es2 * 128: dp * D + (es2 + 1) * 128],
                                  sa_b[:, dp:dp + 1], start=(dp == 0), stop=(dp == NP - 1))
            mm.then_inc(inc("pe"), 1)
        block.tensor(tm)
        pe_done = CNT["pe"]

        block.vector(lambda v: (
            v.wait_ge(SEM["pe"], pe_done),
            v.tensor_tensor(bp[:], PB[4][:, 0:NP], bc[:], ALU.add),
            draininc(v, "ve")))
        return CNT["ve"]

    VE_BQP = bias_mod(wbf_a, "ld_wa", LD_WQ, bqc, bqp)
    VE_BKP = bias_mod(wbf_b, "ld_wb", LD_WK, bkc, bkp)

    ln1b = stack_proj.enter_context(nc.sbuf_tensor("ln1b", [128, NP * T], BF16))
    sbuf_trace("ln1b")
    stage = [stack_proj.enter_context(nc.sbuf_tensor(f"stage{i}", [128, NP * TSL], BF16)) for i in range(2)]
    castsq = stack_proj.enter_context(nc.sbuf_tensor("castsq", [128, NP * TSL], BF16))
    bcs = [stack_proj.enter_context(nc.sbuf_tensor(f"bc{i}", [128, TSL], F32)) for i in range(11)]
    wide1 = stack_proj.enter_context(nc.sbuf_tensor("wide1", [128, NP * TSL], F32))
    sbuf_trace("bcs")

    castsq3 = castsq[:].rearrange("p (dp t) -> p dp t", dp=NP)
    xt3 = xt[:].rearrange("p (dp q) -> p dp q", dp=NP)

    def bcast_t(bc):
        return bc[:].unsqueeze(1).broadcast_to((128, NP, TSL))

    def bcast_c(cols):
        return cols.unsqueeze(2).broadcast_to((128, NP, TSL))

    ln_state = {"pe_guard": None}

    def emit_ln_stats(src3, eps, ve_guard=None):
        def sc_fn(s_):
            if ve_guard is not None:
                s_.wait_ge(SEM["ve"], ve_guard)
            if ln_state["pe_guard"] is not None:
                s_.wait_ge(SEM["pe"], ln_state["pe_guard"])
            s_.activation(castsq3, src3, ACT.Copy)
            draininc(s_, "sc")
        block.scalar(sc_fn)
        sc_cast = CNT["sc"]

        def t_fn1(t):
            t.wait_ge(SEM["sc"], sc_cast)
            t.wait_ge(SEM["pe"], CNT["pe"])
            for dp in range(NP):
                mm = t.matmul(PB[0][:, 0:TSL], onesb[:], castsq[:, dp * TSL:(dp + 1) * TSL],
                              start=(dp == 0), stop=(dp == NP - 1))
            mm.then_inc(inc("pe"), 1)
        block.tensor(t_fn1)
        pe_a = CNT["pe"]

        def sc_fn2(s_):
            s_.wait_ge(SEM["pe"], pe_a)
            s_.activation(castsq3, src3, ACT.Square)
            draininc(s_, "sc")
        block.scalar(sc_fn2)
        sc_sq = CNT["sc"]

        def t_fn2(t):
            t.wait_ge(SEM["sc"], sc_sq)
            t.wait_ge(SEM["pe"], CNT["pe"])
            for dp in range(NP):
                mm = t.matmul(PB[1][:, 0:TSL], onesb[:], castsq[:, dp * TSL:(dp + 1) * TSL],
                              start=(dp == 0), stop=(dp == NP - 1))
            mm.then_inc(inc("pe"), 1)
        block.tensor(t_fn2)
        pe_st = CNT["pe"]
        ln_state["pe_guard"] = pe_st

        block.vector(lambda v: (
            v.wait_ge(SEM["pe"], pe_st),
            v.tensor_scalar(bcs[0][:], PB[0][:, 0:TSL], 1.0 / D, None, op0=ALU.mult),
            v.tensor_scalar(bcs[1][:], PB[1][:, 0:TSL], 1.0 / D, None, op0=ALU.mult),
            v.drain(),
            v.tensor_tensor(bcs[2][:], bcs[0][:], bcs[0][:], ALU.mult),
            v.drain(),
            v.tensor_tensor(bcs[3][:], bcs[1][:], bcs[2][:], ALU.subtract),
            draininc(v, "ve")))
        ve_var = CNT["ve"]

        block.scalar(lambda s_: (
            s_.wait_ge(SEM["ve"], ve_var),
            s_.activation(bcs[4][:], bcs[3][:], ACT.Sqrt, bias=epscol[:, 0:1]),
            draininc(s_, "sc")))
        sc_sd = CNT["sc"]

        block.vector(lambda v: (
            v.wait_ge(SEM["sc"], sc_sd),
            v.reciprocal(bcs[5][:], bcs[4][:]),
            v.drain(),
            v.tensor_tensor(bcs[6][:], bcs[0][:], bcs[5][:], ALU.mult),
            draininc(v, "ve")))
        return CNT["ve"]

    ln1b3 = ln1b[:].rearrange("p (dp q) -> p dp q", dp=NP)
    ln1_ve_marks = []
    for ts in range(NTS):
        if ts == 0:
            block.scalar(lambda s_: s_.wait_ge(SEM["ld_x"], LD_X))
        ve_stats = emit_ln_stats(xt3[:, :, ts * TSL:(ts + 1) * TSL], 1e-5)

        def norm_fn(v, ts=ts, ve_stats=ve_stats):
            v.wait_ge(SEM["ve"], ve_stats)
            if ts == 0:
                v.wait_ge(SEM["ve"], VE_DERC)
            x3s = xt3[:, :, ts * TSL:(ts + 1) * TSL]
            w3 = wide1[:].rearrange("p (dp t) -> p dp t", dp=NP)
            v.tensor_tensor(w3, x3s, bcast_t(bcs[5]), ALU.mult)
            v.drain()
            v.tensor_tensor(w3, w3, bcast_t(bcs[6]), ALU.subtract)
            v.drain()
            v.tensor_tensor(ln1b3[:, :, ts * TSL:(ts + 1) * TSL], w3, bcast_c(derc[:, 16:24]), ALU.mult)
            draininc(v, "ve")
        block.vector(norm_fn)
        ln1_ve_marks.append(CNT["ve"])
    VE_LN1 = CNT["ve"]

    def proj_T(wtile, bias_tile, out_dram, ssem, extra_waits, prev_store_waits=(), bias_add=True):
        pe_base = CNT["pe"]
        ve_base = CNT["ve"]
        st_base = [CNT[ssem[0]], CNT[ssem[1]]]

        def t_fn(t):
            for sname, sval in extra_waits:
                t.wait_ge(SEM[sname], sval)
            n = 0
            for ts in range(NTS):
                for es2 in range(NP):
                    if n >= 2:
                        t.wait_ge(SEM["ve"], ve_base + n - 1)
                    pbk = PB[2 + (n % 2)]
                    t.wait_ge(SEM["pe"], CNT["pe"])
                    for dp in range(NP):
                        mm = t.matmul(pbk[:, 0:TSL], wtile[:, dp * D + es2 * 128: dp * D + (es2 + 1) * 128],
                                      ln1b[:, dp * T + ts * TSL: dp * T + (ts + 1) * TSL],
                                      start=(dp == 0), stop=(dp == NP - 1))
                    mm.then_inc(inc("pe"), 1)
                    n += 1
        block.tensor(t_fn)

        def v_fn(v):
            n = 0
            for ts in range(NTS):
                st = stage[ts % 2]
                for es2 in range(NP):
                    v.wait_ge(SEM["pe"], pe_base + n + 1)
                    if n == 0:
                        for sname, sval in prev_store_waits:
                            v.wait_ge(SEM[sname], sval)
                    if es2 == 0 and ts >= 2:
                        v.wait_ge(SEM[ssem[ts % 2]], st_base[ts % 2] + 16 * (ts // 2))
                    if bias_add:
                        v.tensor_scalar(st[:, es2 * TSL:(es2 + 1) * TSL], PB[2 + (n % 2)][:, 0:TSL],
                                        bias_tile[:, es2:es2 + 1], None, op0=ALU.add)
                    else:
                        v.tensor_copy(st[:, es2 * TSL:(es2 + 1) * TSL], PB[2 + (n % 2)][:, 0:TSL])
                    draininc(v, "ve")
                    n += 1
        block.vector(v_fn)
        ve_all = CNT["ve"]

        def sy_fn(sy):
            for ts in range(NTS):
                sy.wait_ge(SEM["ve"], ve_base + (ts + 1) * NP)
                sy.dma_start(out=out_dram.rearrange("(np p) t -> p np t", p=128)[:, :, ts * TSL:(ts + 1) * TSL],
                             in_=stage[ts % 2][:].rearrange("p (np t) -> p np t", np=NP)
                             ).then_inc(inc(ssem[ts % 2], 16), 16)
        block.sync(sy_fn)
        return ve_all

    VE_K = proj_T(wbf_b, bkp, cc_k_in, ("st_k0", "st_k1"), [("ve", VE_LN1), ("ve", VE_BKP)])
    ST_K = [CNT["st_k0"], CNT["st_k1"]]

    block.gpsimd(lambda g: (
        g.wait_ge(SEM["st_k0"], ST_K[0]),
        g.wait_ge(SEM["st_k1"], ST_K[1]),
        g.collective_compute("AllGather", ALU.bypass, replica_groups=[[0, 1, 2, 3], [4, 5, 6, 7]],
                             ins=[cc_k_in.ap().opt()], outs=[cc_k_out.ap().opt()]).then_inc(inc("cc_k"), 1)))
    CC_K = CNT["cc_k"]

    VE_Q = proj_T(wbf_a, bqp, q_dram, ("st_q0", "st_q1"), [("ve", VE_K), ("ve", VE_BQP)],
                  prev_store_waits=[("st_k0", ST_K[0]), ("st_k1", ST_K[1])])
    ST_Q = [CNT["st_q0"], CNT["st_q1"]]

    block.gpsimd(lambda g: (
        g.wait_ge(SEM["ve"], VE_K),
        g.dma_start(out=wbf_b[:].rearrange("p (np e) -> p np e", np=NP), in_=woT.rearrange("(np p) e -> p np e", p=128)).then_inc(inc("ld_wb", 16), 16)))
    LD_WO = CNT["ld_wb"]

    block.gpsimd(lambda g: (
        g.wait_ge(SEM["ve"], VE_Q),
        g.dma_start(out=wbf_a[:].rearrange("p (np e) -> p np e", np=NP), in_=wvT.rearrange("(np p) e -> p np e", p=128)).then_inc(inc("ld_wa", 16), 16)))
    LD_WV = CNT["ld_wa"]

    VE_BVP = bias_mod(wbf_a, "ld_wa", LD_WV, bvc, bvp)

    pe_v_base = CNT["pe"]
    ve_v_base = CNT["ve"]

    def v_proj_t(t):
        t.wait_ge(SEM["ld_wa"], LD_WV)
        t.wait_ge(SEM["ve"], VE_Q)
        n = 0
        for ts in range(NTS):
            for tsub in range(TSL // 128):
                for eh in range(2):
                    if n >= 2:
                        t.wait_ge(SEM["ve"], ve_v_base + n - 1)
                    pbk = PB[2 + (n % 2)]
                    t.wait_ge(SEM["pe"], CNT["pe"])
                    for dp in range(NP):
                        mm = t.matmul(pbk[:, 0:TSL],
                                      ln1b[:, dp * T + ts * TSL + tsub * 128: dp * T + ts * TSL + (tsub + 1) * 128],
                                      wbf_a[:, dp * D + eh * 512: dp * D + (eh + 1) * 512],
                                      start=(dp == 0), stop=(dp == NP - 1))
                    mm.then_inc(inc("pe"), 1)
                    n += 1
    block.tensor(v_proj_t)

    def v_proj_v(v):
        n = 0
        for ts in range(NTS):
            st = stage[ts % 2]
            for tsub in range(TSL // 128):
                for eh in range(2):
                    v.wait_ge(SEM["pe"], pe_v_base + n + 1)
                    if n == 0:
                        v.wait_ge(SEM["st_q0"], ST_Q[0])
                        v.wait_ge(SEM["st_q1"], ST_Q[1])
                    if ts >= 2 and tsub == 0 and eh == 0:
                        v.wait_ge(SEM["st_v%d" % (ts % 2)], 16 * (ts // 2))
                    v.tensor_copy(st[:, tsub * 2 * TSL + eh * TSL: tsub * 2 * TSL + (eh + 1) * TSL],
                                  PB[2 + (n % 2)][:, 0:TSL])
                    draininc(v, "ve")
                    n += 1
    block.vector(v_proj_v)
    VE_V = CNT["ve"]

    def v_store(sy):
        for ts in range(NTS):
            sy.wait_ge(SEM["ve"], ve_v_base + (ts + 1) * (TSL // 128) * 2)
            sy.dma_start(out=ccv_in[ts * TSL:(ts + 1) * TSL, :].rearrange("(tsub p) (eh e) -> p tsub eh e", p=128, eh=2),
                         in_=stage[ts % 2][:].rearrange("p (tsub eh e) -> p tsub eh e", tsub=TSL // 128, eh=2)).then_inc(inc("st_v%d" % (ts % 2), 16), 16)
    block.sync(v_store)
    ST_V = [CNT["st_v0"], CNT["st_v1"]]

    block.gpsimd(lambda g: (
        g.wait_ge(SEM["st_v0"], ST_V[0]),
        g.wait_ge(SEM["st_v1"], ST_V[1]),
        g.collective_compute("AllGather", ALU.bypass, replica_groups=[[0, 1, 2, 3], [4, 5, 6, 7]],
                             ins=[ccv_in.ap().opt()], outs=[ccv_out.ap().opt()]).then_inc(inc("cc_v"), 1)))
    CC_V = CNT["cc_v"]

    def tap_dma(ap_out, src, waits):
        def f(sy):
            for sname, sval in waits:
                sy.wait_ge(SEM[sname], sval)
            sy.dma_start(out=ap_out, in_=src).then_inc(inc("dbgs", 16), 16)
        block.sync(f)

    t_ = tap("modcols", [128, 96])
    if t_ is not None:
        tap_dma(t_[:, 0:48], modc[:, 0:48], [("ve", VE_DERC)])
        tap_dma(t_[:, 48:96], derc[:, 0:48], [("ve", VE_DERC)])
    t_ = tap("ln1", [D, T], BF16)
    if t_ is not None:
        tap_dma(t_.rearrange("(np p) t -> p np t", p=128), ln1b[:].rearrange("p (np t) -> p np t", np=NP), [("ve", VE_LN1)])
    t_ = tap("qT", [D, T], BF16)
    if t_ is not None:
        tap_dma(t_, q_dram.ap(), [("st_q0", ST_Q[0]), ("st_q1", ST_Q[1])])
    t_ = tap("kTfull", [GRP * D, T], BF16)
    if t_ is not None:
        tap_dma(t_, cc_k_out.ap(), [("cc_k", CC_K)])
    t_ = tap("vfull", [S, D], BF16)
    if t_ is not None:
        tap_dma(t_, ccv_out.ap(), [("cc_v", CC_V)])
    t_ = tap("biasp", [128, 32])
    if t_ is not None:
        tap_dma(t_[:, 0:8], bqp[:], [("ve", VE_BVP)])
        tap_dma(t_[:, 8:16], bkp[:], [("ve", VE_BVP)])
        tap_dma(t_[:, 16:24], bvp[:], [("ve", VE_BVP)])
        tap_dma(t_[:, 24:32], scol[:, 0:8], [("ve", VE_RGCOL), ("ve", VE_G1)])

    def finish(extra_waits):
        def fin(sy):
            for sname, sval in extra_waits:
                sy.wait_ge(SEM[sname], sval)
            sy.wait_ge(SEM["dbgs"], CNT["dbgs"])
        block.sync(fin)
        blockctx.__exit__(None, None, None)
        for st in reversed(open_stacks):
            st.close()
        es.close()
        return nc, dbg

    if phases <= 3:
        block.vector(lambda v: (
            v.wait_ge(SEM["cc_v"], CC_V),
            v.memzero(bcs[0][:]),
            draininc(v, "ve")))
        VE_ZERO = CNT["ve"]

        def dummy_out(sy):
            sy.wait_ge(SEM["ve"], VE_ZERO)
            for dp in range(NP):
                for ts in range(NTS):
                    sy.dma_start(out=outT[dp * 128:(dp + 1) * 128, ts * TSL:(ts + 1) * TSL],
                                 in_=bcs[0][:]).then_inc(inc("st_out0", 16), 16)
        block.sync(dummy_out)
        return finish([("st_out0", CNT["st_out0"])])

    close_stack(stack_proj)
    sbuf_trace("close_stack")
    stack_att = new_stack()
    sba = lambda n, sh, dt=F32: stack_att.enter_context(nc.sbuf_tensor(n, sh, dt))
    Etile = sba("Etile", [128, NKT * QTS], BF16)
    qbuf = sba("qbuf", [128, NP * QTS], BF16)
    kbuf = [sba(f"kbuf{i}", [128, NP * KCH], BF16) for i in range(3)]
    vbufB = [sba(f"vbufB{i}", [128, KTPC * 512], BF16) for i in range(2)]
    vbufC = [sba(f"vbufC{i}", [128, KTPC * 512], BF16) for i in range(2)]
    ctxn = sba("ctxn", [128, NP * QTS], BF16)
    rsum = [sba(f"rsum{i}", [128, QTS], F32) for i in range(2)]
    t1w = sba("t1w", [128, NP * WCH], F32)
    t1q = sba("t1q", [128, NP * WCH], BF16)
    sbuf_trace("t1q")

    NWC1 = HD // WCH
    NWC2 = HD // WCH
    NTERN = NWC1 + NWC2
    tern_ve_marks = []

    tern_next = {"lq": 0, "st": 0}

    def emit_tern_lq(hi):
        for ci in range(tern_next["lq"], min(hi, NTERN)):
            wi, ch = (0, ci) if ci < NWC1 else (1, ci - NWC1)
            prev_q = tern_ve_marks[ci - 1] if ci >= 1 else None

            def g_load(sy, wi=wi, ch=ch, prev_q=prev_q, ci=ci):
                if ci == 0:
                    sy.wait_ge(SEM["st_v0"], ST_V[0])
                    sy.wait_ge(SEM["st_v1"], ST_V[1])
                if prev_q is not None:
                    sy.wait_ge(SEM["ve"], prev_q)
                if wi == 0:
                    src = w1T[:, ch * WCH:(ch + 1) * WCH].rearrange("(np p) h -> p np h", p=128)
                else:
                    src = w2T[ch * WCH:(ch + 1) * WCH, :].rearrange("(np p) e -> p np e", p=128)
                sy.dma_start(out=t1w[:].rearrange("p (np w) -> p np w", w=(WCH if wi == 0 else D)), in_=src).then_inc(inc("ld_wt%d" % (ci % 2), 16), 16)
            block.sync(g_load)
            ldsem = "ld_wt%d" % (ci % 2)
            ldval = CNT[ldsem]

            def v_q(v, ldsem=ldsem, ldval=ldval, wi=wi, ci=ci):
                v.wait_ge(SEM[ldsem], ldval)
                if ci == 0:
                    v.wait_ge(SEM["ve"], VE_RGCOL)
                if ci >= 1:
                    v.wait_ge(SEM["st_ta"], 16 * ci)
                v.tensor_scalar(t1w[:], t1w[:], scol[:, 6 + wi:7 + wi], MAGIC, op0=ALU.mult, op1=ALU.add)
                v.drain()
                v.tensor_scalar(t1w[:], t1w[:], MAGIC, 1.0, op0=ALU.subtract, op1=ALU.min)
                v.drain()
                v.tensor_scalar(t1q[:], t1w[:], -1.0, None, op0=ALU.max)
                draininc(v, "ve")
            block.vector(v_q)
            tern_ve_marks.append(CNT["ve"])
        tern_next["lq"] = min(hi, NTERN)

    def emit_tern_st(hi):
        for ci in range(tern_next["st"], min(hi, tern_next["lq"])):
            wi, ch = (0, ci) if ci < NWC1 else (1, ci - NWC1)

            def g_store(g, wi=wi, ch=ch, vq=tern_ve_marks[ci]):
                g.wait_ge(SEM["ve"], vq)
                if wi == 0:
                    dst = t1_dram[:, ch * WCH:(ch + 1) * WCH].rearrange("(np p) h -> p np h", p=128)
                else:
                    dst = t2_dram[ch * WCH:(ch + 1) * WCH, :].rearrange("(np p) e -> p np e", p=128)
                g.dma_start(out=dst, in_=t1q[:].rearrange("p (np w) -> p np w", w=(WCH if wi == 0 else D))).then_inc(inc("st_ta", 16), 16)
            block.gpsimd(g_store)
        tern_next["st"] = min(hi, tern_next["lq"])

    CPQ = -(-NTERN // NQT)
    CP3 = -(-CPQ // 3)

    SBANKS = [PB[0], PB[1], PB[5], PB[6]]
    prev_C_marks = None
    prev_exp_last = None
    prev_score_last = 0
    prev_B_last = 0
    prev_wo_ve = 0
    prev_rsum_ve = {0: 0, 1: 0}
    x2_ve_marks = []

    for qt in range(NQT):
        def q_load(sy, qt=qt, prev_score_last=prev_score_last):
            if qt == 0:
                sy.wait_ge(SEM["st_q0"], ST_Q[0])
                sy.wait_ge(SEM["st_q1"], ST_Q[1])
                sy.wait_ge(SEM["st_v0"], ST_V[0])
                sy.wait_ge(SEM["st_v1"], ST_V[1])
                sy.wait_ge(SEM["ve"], VE_V)
            else:
                sy.wait_ge(SEM["pe"], prev_score_last)
            sy.dma_start(out=qbuf[:].rearrange("p (np t) -> p np t", np=NP), in_=q_dram[:, qt * QTS:(qt + 1) * QTS].rearrange("(np p) t -> p np t", p=128)
                         ).then_inc(inc("ld_q", 16), 16)
        block.sync(q_load)
        LD_QT = CNT["ld_q"]

        exp_base = CNT["sc"]
        exp_marks = [exp_base + 1 + kt for kt in range(NKT)]
        kbase = [CNT["ld_k%d" % i] for i in range(3)]
        pe_scores = []
        pe_sums_last = None

        def passA_pe(t, qt=qt, kbase=kbase, exp_marks=exp_marks,
                     prev_exp_last=prev_exp_last, prev_rsum=prev_rsum_ve[(qt - 1) % 2] if qt > 0 else 0):
            t.wait_ge(SEM["ld_q"], LD_QT)
            if qt == 0:
                t.wait_ge(SEM["cc_k"], CC_K)
                t.wait_ge(SEM["ve"], VE_BVP)
            for kt in range(NKT):
                chunk = kt // KTPC
                slot = chunk % 3
                if kt % KTPC == 0:
                    t.wait_ge(SEM["ld_k%d" % slot], kbase[slot] + 16 * (chunk // 3 + 1))
                bank = SBANKS[kt % 4]
                if kt >= 4:
                    t.wait_ge(SEM["sc"], exp_marks[kt - 4])
                elif qt > 0:
                    t.wait_ge(SEM["sc"], prev_exp_last[NKT - 4 + kt])
                t.wait_ge(SEM["pe"], CNT["pe"])
                for dp in range(NP):
                    mm = t.matmul(bank[:, 0:QTS],
                                  kbuf[slot][:, dp * KCH + (kt % KTPC) * 128: dp * KCH + (kt % KTPC + 1) * 128],
                                  qbuf[:, dp * QTS:(dp + 1) * QTS],
                                  start=(dp == 0), stop=(dp == NP - 1))
                mm.then_inc(inc("pe"), 1)
                if kt >= 1:
                    t.wait_ge(SEM["sc"], exp_marks[kt - 1])
                    if kt == 1 and qt > 0:
                        t.wait_ge(SEM["ve"], prev_rsum)
                    t.wait_ge(SEM["pe"], CNT["pe"])
                    t.matmul(PB[2][:, 0:QTS], onesb[:], Etile[:, (kt - 1) * QTS: kt * QTS],
                             start=(kt - 1 == 0), stop=False).then_inc(inc("pe"), 1)
            t.wait_ge(SEM["sc"], exp_marks[NKT - 1])
            t.wait_ge(SEM["pe"], CNT["pe"])
            t.matmul(PB[2][:, 0:QTS], onesb[:], Etile[:, (NKT - 1) * QTS: NKT * QTS],
                     start=False, stop=True).then_inc(inc("pe"), 1)

        pe0 = CNT["pe"]
        block.tensor(passA_pe)
        n_inc = 0
        for kt in range(NKT):
            n_inc += 1
            pe_scores.append(pe0 + n_inc)
            if kt >= 1:
                n_inc += 1
        n_inc += 1
        assert CNT["pe"] == pe0 + n_inc, (CNT["pe"], pe0, n_inc)
        pe_sums_last = CNT["pe"]
        score_last = pe_scores[-1]

        def k_loads(sy, qt=qt, pe_scores=list(pe_scores), prev_score_last=prev_score_last):
            for kc in range(NKC):
                slot = kc % 3
                if kc >= 3:
                    sy.wait_ge(SEM["pe"], pe_scores[(kc - 2) * KTPC - 1])
                elif qt > 0:
                    sy.wait_ge(SEM["pe"], prev_score_last)
                else:
                    sy.wait_ge(SEM["cc_k"], CC_K)
                    sy.wait_ge(SEM["st_v0"], ST_V[0])
                    sy.wait_ge(SEM["st_v1"], ST_V[1])
                    sy.wait_ge(SEM["ve"], VE_V)
                r = (kc * KCH) // T
                c0 = (kc * KCH) % T
                src = cc_k_out[r * D:(r + 1) * D, c0:c0 + KCH].rearrange("(np p) t -> p np t", p=128)
                sy.dma_start(out=kbuf[slot][:].rearrange("p (np t) -> p np t", np=NP), in_=src).then_inc(inc("ld_k%d" % slot, 16), 16)
        block.sync(k_loads)

        def exp_fn(s_, qt=qt, pe_scores=list(pe_scores), prev_C_marks=prev_C_marks):
            for kt in range(NKT):
                s_.wait_ge(SEM["pe"], pe_scores[kt])
                if prev_C_marks is not None:
                    s_.wait_ge(SEM["pe"], prev_C_marks[kt])
                s_.activation(Etile[:, kt * QTS:(kt + 1) * QTS], SBANKS[kt % 4][:, 0:QTS], ACT.Exp, scale=1.0 / 32.0)
                draininc(s_, "sc")
        block.scalar(exp_fn)
        assert CNT["sc"] == exp_marks[-1], (CNT["sc"], exp_marks[-1])
        prev_exp_last = exp_marks

        def rsum_fn(v, qt=qt, pe_sums_last=pe_sums_last):
            v.wait_ge(SEM["pe"], pe_sums_last)
            v.reciprocal(rsum[qt % 2][:, 0:QTS], PB[2][:, 0:QTS])
            draininc(v, "ve")
        block.vector(rsum_fn)
        VE_RSUM = CNT["ve"]
        prev_rsum_ve[qt % 2] = VE_RSUM
        emit_tern_lq(qt * CPQ + CP3)

        pe_b0 = CNT["pe"]
        pe_B = [pe_b0 + 1 + kt for kt in range(NKT)]

        def passB_pe(t, qt=qt, vbase=[CNT["ld_vb%d" % i] for i in range(2)], prev_wo_ve=prev_wo_ve):
            if qt > 0:
                t.wait_ge(SEM["ve"], prev_wo_ve)
            for kt in range(NKT):
                chunk = kt // KTPC
                slot = chunk % 2
                if kt % KTPC == 0:
                    t.wait_ge(SEM["ld_vb%d" % slot], vbase[slot] + 16 * (chunk // 2 + 1))
                t.wait_ge(SEM["pe"], CNT["pe"])
                for ds in range(4):
                    mm = t.matmul(PB[3 + ds][:, 0:QTS],
                                  vbufB[slot][:, (kt % KTPC) * 512 + ds * 128: (kt % KTPC) * 512 + (ds + 1) * 128],
                                  Etile[:, kt * QTS:(kt + 1) * QTS],
                                  start=(kt == 0), stop=(kt == NKT - 1))
                mm.then_inc(inc("pe"), 1)
        block.tensor(passB_pe)
        assert CNT["pe"] == pe_B[-1]

        def b_loads(g, qt=qt, pe_B=list(pe_B), prev_B_last=prev_B_last):
            for kc in range(NKC):
                slot = kc % 2
                if kc >= 2:
                    g.wait_ge(SEM["pe"], pe_B[(kc - 1) * KTPC - 1])
                elif qt > 0:
                    g.wait_ge(SEM["pe"], prev_B_last)
                else:
                    g.wait_ge(SEM["cc_v"], CC_V)
                src = ccv_out[kc * KCH:(kc + 1) * KCH, 0:512].rearrange("(ts2 p) e -> p ts2 e", p=128)
                g.dma_start(out=vbufB[slot][:].rearrange("p (ts2 e) -> p ts2 e", ts2=KTPC), in_=src).then_inc(inc("ld_vb%d" % slot, 16), 16)
        block.gpsimd(b_loads)
        prev_B_last = pe_B[-1]

        def b_evac(v, qt=qt, pe_last=pe_B[-1]):
            v.wait_ge(SEM["pe"], pe_last)
            for ds in range(4):
                v.tensor_tensor(junk[ds % 2][:, 0:QTS], PB[3 + ds][:, 0:QTS], rsum[qt % 2][:, 0:QTS], ALU.mult)
                v.drain()
                v.tensor_scalar(ctxn[:, ds * QTS:(ds + 1) * QTS], junk[ds % 2][:, 0:QTS], bvp[:, ds:ds + 1], None, op0=ALU.add)
                v.drain()
            draininc(v, "ve")
        block.vector(b_evac)
        VE_BEV = CNT["ve"]
        emit_tern_lq(qt * CPQ + 2 * CP3)
        emit_tern_st(qt * CPQ + CP3)

        Cbanks = [PB[7], PB[0], PB[1], PB[2]]
        pe_c0 = CNT["pe"]
        pe_C = [pe_c0 + 1 + kt for kt in range(NKT)]

        def passC_pe(t, qt=qt, vbase=[CNT["ld_vc%d" % i] for i in range(2)], VE_RSUM=VE_RSUM):
            t.wait_ge(SEM["ve"], VE_RSUM)
            for kt in range(NKT):
                chunk = kt // KTPC
                slot = chunk % 2
                if kt % KTPC == 0:
                    t.wait_ge(SEM["ld_vc%d" % slot], vbase[slot] + 16 * (chunk // 2 + 1))
                t.wait_ge(SEM["pe"], CNT["pe"])
                for ds in range(4):
                    mm = t.matmul(Cbanks[ds][:, 0:QTS],
                                  vbufC[slot][:, (kt % KTPC) * 512 + ds * 128: (kt % KTPC) * 512 + (ds + 1) * 128],
                                  Etile[:, kt * QTS:(kt + 1) * QTS],
                                  start=(kt == 0), stop=(kt == NKT - 1))
                mm.then_inc(inc("pe"), 1)
        block.tensor(passC_pe)
        assert CNT["pe"] == pe_C[-1]
        prev_C_marks = list(pe_C)

        def c_loads(g, qt=qt, pe_C=list(pe_C), prev_C_first=pe_c0):
            for kc in range(NKC):
                slot = kc % 2
                if kc >= 2:
                    g.wait_ge(SEM["pe"], pe_C[(kc - 1) * KTPC - 1])
                elif qt == 0:
                    g.wait_ge(SEM["cc_v"], CC_V)
                src = ccv_out[kc * KCH:(kc + 1) * KCH, 512:1024].rearrange("(ts2 p) e -> p ts2 e", p=128)
                g.dma_start(out=vbufC[slot][:].rearrange("p (ts2 e) -> p ts2 e", ts2=KTPC), in_=src).then_inc(inc("ld_vc%d" % slot, 16), 16)
        block.gpsimd(c_loads)

        def c_evac(v, qt=qt, pe_last=pe_C[-1]):
            v.wait_ge(SEM["pe"], pe_last)
            for ds in range(4):
                v.tensor_tensor(junk[ds % 2][:, 0:QTS], Cbanks[ds][:, 0:QTS], rsum[qt % 2][:, 0:QTS], ALU.mult)
                v.drain()
                v.tensor_scalar(ctxn[:, (4 + ds) * QTS:(5 + ds) * QTS], junk[ds % 2][:, 0:QTS], bvp[:, 4 + ds:5 + ds],
                                None, op0=ALU.add)
                v.drain()
            draininc(v, "ve")
        block.vector(c_evac)
        VE_CEV = CNT["ve"]
        emit_tern_lq((qt + 1) * CPQ)
        emit_tern_st(qt * CPQ + 2 * CP3)

        ve_wo_base = CNT["ve"]
        pe_wo0 = CNT["pe"]
        pe_wo = [pe_wo0 + 1 + es2 for es2 in range(NP)]

        def wo_pe(t, qt=qt, VE_CEV=VE_CEV, ve_wo_base=ve_wo_base):
            t.wait_ge(SEM["ve"], VE_CEV)
            if qt == 0:
                t.wait_ge(SEM["ld_wb"], LD_WO)
            for es2 in range(NP):
                if es2 >= 2:
                    t.wait_ge(SEM["ve"], ve_wo_base + es2 - 1)
                bank = PB[3 + es2 % 2]
                t.wait_ge(SEM["pe"], CNT["pe"])
                for dp in range(NP):
                    mm = t.matmul(bank[:, 0:QTS],
                                  wbf_b[:, dp * D + es2 * 128: dp * D + (es2 + 1) * 128],
                                  ctxn[:, dp * QTS:(dp + 1) * QTS],
                                  start=(dp == 0), stop=(dp == NP - 1))
                mm.then_inc(inc("pe"), 1)
        block.tensor(wo_pe)
        assert CNT["pe"] == pe_wo[-1]

        def wo_evac(v, qt=qt, pe_wo=list(pe_wo)):
            for es2 in range(NP):
                v.wait_ge(SEM["pe"], pe_wo[es2])
                colx = es2 * T + qt * QTS
                v.tensor_scalar(junk[es2 % 2][:, 0:QTS], PB[3 + es2 % 2][:, 0:QTS], derc[:, es2:es2 + 1],
                                derc[:, 32 + es2:33 + es2], op0=ALU.mult, op1=ALU.add)
                v.drain()
                v.tensor_tensor(xt[:, colx:colx + QTS], junk[es2 % 2][:, 0:QTS], xt[:, colx:colx + QTS], ALU.add)
                draininc(v, "ve")
        block.vector(wo_evac)
        prev_wo_ve = CNT["ve"]
        x2_ve_marks.append(prev_wo_ve)
        emit_tern_st((qt + 1) * CPQ)
        prev_score_last = score_last
    emit_tern_lq(NTERN)
    emit_tern_st(NTERN)
    ST_T12 = (CNT["st_ta"], CNT["st_tb"])
    VE_ATT = CNT["ve"]
    PE_ATT = CNT["pe"]

    t_ = tap("x2", [D, T])
    if t_ is not None:
        tap_dma(t_.rearrange("(np p) t -> p np t", p=128), xt[:].rearrange("p (np t) -> p np t", np=NP), [("ve", VE_ATT)])

    if phases <= 4:
        def out4(sy):
            sy.wait_ge(SEM["ve"], VE_ATT)
            sy.wait_ge(SEM["st_ta"], ST_T12[0]); sy.wait_ge(SEM["st_tb"], ST_T12[1])
            for dp in range(NP):
                sy.dma_start(out=outT[dp * 128:(dp + 1) * 128, :], in_=xt[:, dp * T:(dp + 1) * T]
                             ).then_inc(inc("st_out0", 16), 16)
        block.sync(out4)
        return finish([("st_out0", CNT["st_out0"])])

    close_stack(stack_att)
    close_stack(stack_w)
    sbuf_trace("close_stack")
    stack_ffn = new_stack()
    qint1 = stack_ffn.enter_context(nc.sbuf_tensor("qint1", [128, NP * T], BF16))
    sbuf_trace("qint1")
    stack_z = new_stack()
    ztile = stack_z.enter_context(nc.sbuf_tensor("ztile", [128, NP * T], F32))
    castsq2 = qint1[:, 0:NP * TSL]
    qsc5 = stack_z.enter_context(nc.sbuf_tensor("qsc5", [128, TSL], F32))
    qrow5 = stack_z.enter_context(nc.sbuf_tensor("qrow5", [1, 16 * 128], F32))
    irow5 = [stack_z.enter_context(nc.sbuf_tensor(f"irow5_{i}", [1, 128], F32)) for i in range(10)]
    bc2 = [stack_z.enter_context(nc.sbuf_tensor(f"bc2_{i}", [128, TSL], F32)) for i in range(11)]
    sbuf_trace("bc2")

    ln2_state = {"pe_guard": PE_ATT, "first": True, "first_sc": True}
    castsq23 = castsq2.rearrange("p (dp t) -> p dp t", dp=NP)
    zt3 = ztile[:].rearrange("p (dp q) -> p dp q", dp=NP)

    def emit_ln_stats2(src3, eps, ve_guard):
        def sc_fn(s_):
            if ln2_state["first_sc"]:
                s_.wait_ge(SEM["st_ta"], ST_T12[0])
                s_.wait_ge(SEM["st_tb"], ST_T12[1])
                ln2_state["first_sc"] = False
            if ve_guard is not None:
                s_.wait_ge(SEM["ve"], ve_guard)
            if ln2_state["pe_guard"] is not None:
                s_.wait_ge(SEM["pe"], ln2_state["pe_guard"])
            s_.activation(castsq23, src3, ACT.Copy)
            draininc(s_, "sc")
        block.scalar(sc_fn)
        sc_cast = CNT["sc"]

        def t_fn1(t):
            t.wait_ge(SEM["sc"], sc_cast)
            if ln2_state["first"]:
                t.wait_ge(SEM["ve"], VE_ATT)
                ln2_state["first"] = False
            t.wait_ge(SEM["pe"], CNT["pe"])
            for dp in range(NP):
                mm = t.matmul(PB[0][:, 0:TSL], onesb[:], castsq2[:, dp * TSL:(dp + 1) * TSL],
                              start=(dp == 0), stop=(dp == NP - 1))
            mm.then_inc(inc("pe"), 1)
        block.tensor(t_fn1)
        pe_a = CNT["pe"]

        def sc_fn2(s_):
            s_.wait_ge(SEM["pe"], pe_a)
            s_.activation(castsq23, src3, ACT.Square)
            draininc(s_, "sc")
        block.scalar(sc_fn2)
        sc_sq = CNT["sc"]

        def t_fn2(t):
            t.wait_ge(SEM["sc"], sc_sq)
            t.wait_ge(SEM["pe"], CNT["pe"])
            for dp in range(NP):
                mm = t.matmul(PB[1][:, 0:TSL], onesb[:], castsq2[:, dp * TSL:(dp + 1) * TSL],
                              start=(dp == 0), stop=(dp == NP - 1))
            mm.then_inc(inc("pe"), 1)
        block.tensor(t_fn2)
        pe_st = CNT["pe"]
        ln2_state["pe_guard"] = pe_st

        block.vector(lambda v: (
            v.wait_ge(SEM["pe"], pe_st),
            v.tensor_scalar(bc2[0][:], PB[0][:, 0:TSL], 1.0 / D, None, op0=ALU.mult),
            v.tensor_scalar(bc2[1][:], PB[1][:, 0:TSL], 1.0 / D, None, op0=ALU.mult),
            v.drain(),
            v.tensor_tensor(bc2[2][:], bc2[0][:], bc2[0][:], ALU.mult),
            v.drain(),
            v.tensor_tensor(bc2[3][:], bc2[1][:], bc2[2][:], ALU.subtract),
            draininc(v, "ve")))
        ve_var = CNT["ve"]

        block.scalar(lambda s_: (
            s_.wait_ge(SEM["ve"], ve_var),
            s_.activation(bc2[4][:], bc2[3][:], ACT.Sqrt, bias=epscol[:, 0:1]),
            draininc(s_, "sc")))
        sc_sd = CNT["sc"]

        block.vector(lambda v: (
            v.wait_ge(SEM["sc"], sc_sd),
            v.reciprocal(bc2[5][:], bc2[4][:]),
            v.drain(),
            v.tensor_tensor(bc2[6][:], bc2[0][:], bc2[5][:], ALU.mult),
            draininc(v, "ve")))
        return CNT["ve"]

    for ts in range(NTS):
        ve_ln2 = emit_ln_stats2(xt3[:, :, ts * TSL:(ts + 1) * TSL], 1e-5, VE_ATT if ts == 0 else None)

        def y_fn(v, ts=ts, ve_ln2=ve_ln2):
            v.wait_ge(SEM["ve"], ve_ln2)
            z3s = zt3[:, :, ts * TSL:(ts + 1) * TSL]
            v.tensor_tensor(z3s, xt3[:, :, ts * TSL:(ts + 1) * TSL], bcast_t(bc2[5]), ALU.mult)
            v.drain()
            v.tensor_tensor(z3s, z3s, bcast_t(bc2[6]), ALU.subtract)
            v.drain()
            v.tensor_tensor(z3s, z3s, bcast_c(derc[:, 24:32]), ALU.mult)
            v.drain()
            v.tensor_tensor(z3s, z3s, bcast_c(modc[:, 24:32]), ALU.add)
            draininc(v, "ve")
        block.vector(y_fn)
        VE_Y = CNT["ve"]

        ve_ln3 = emit_ln_stats2(zt3[:, :, ts * TSL:(ts + 1) * TSL], 1e-5, VE_Y)

        def z_fn(v, ts=ts, ve_ln3=ve_ln3):
            v.wait_ge(SEM["ve"], ve_ln3)
            z3s = zt3[:, :, ts * TSL:(ts + 1) * TSL]
            v.tensor_tensor(z3s, z3s, bcast_t(bc2[5]), ALU.mult)
            v.drain()
            v.tensor_tensor(z3s, z3s, bcast_t(bc2[6]), ALU.subtract)
            v.drain()
            v.tensor_tensor(z3s, z3s, bcast_c(lngc[:, 0:NP]), ALU.mult)
            v.drain()
            v.tensor_tensor(z3s, z3s, bcast_c(lnbc[:, 0:NP]), ALU.add)
            v.drain()
            for dp in range(NP):
                colz = dp * T + ts * TSL
                a0, a1 = accs[dp % 2], accs[2 + dp % 2]
                v.tensor_scalar(junk[dp % 2][:], ztile[:, colz:colz + TSL], g1mat[:, dp:dp + 1], 0.0,
                                op0=ALU.is_gt, op1=ALU.add, accum_out=a0[:])
                v.drain()
                v.tensor_tensor(cnt1[:, dp:dp + 1], cnt1[:, dp:dp + 1], a0[:], ALU.add)
                v.tensor_scalar(junk[dp % 2][:], ztile[:, colz:colz + TSL], g1nmat[:, dp:dp + 1], 0.0,
                                op0=ALU.is_lt, op1=ALU.add, accum_out=a1[:])
                v.drain()
                v.tensor_tensor(cnt1[:, 8 + dp:9 + dp], cnt1[:, 8 + dp:9 + dp], a1[:], ALU.add)
                v.drain()
            draininc(v, "ve")
        block.vector(z_fn)
    VE_Z = CNT["ve"]

    def quantile_final(cnt_tile, ncol, cc_in, cc_out, ccsem, thr_row, target, scale_slot, ready, qrow, irow):
        block.sync(lambda sy: (
            sy.wait_ge(SEM[ready[0]], ready[1]),
            sy.dma_start(out=cc_in[:, :], in_=cnt_tile).then_inc(inc("st_cc", 16), 16)))
        stc = CNT["st_cc"]
        block.gpsimd(lambda g: (
            g.wait_ge(SEM["st_cc"], stc),
            g.collective_compute("AllReduce", ALU.add, replica_groups=[list(range(NCORE))],
                                 ins=[cc_in.ap().opt()], outs=[cc_out.ap().opt()]).then_inc(inc(ccsem), 1)))
        ccv = CNT[ccsem]
        block.sync(lambda sy: (
            sy.wait_ge(SEM[ccsem], ccv),
            sy.dma_start(out=cnt_ar[:, 0:ncol], in_=cc_out[:, :]).then_inc(inc("ld_cnt", 16), 16)))
        ldv = CNT["ld_cnt"]
        block.tensor(lambda t: (
            t.wait_ge(SEM["ld_cnt"], ldv),
            t.wait_ge(SEM["pe"], CNT["pe"]),
            t.transpose(PB[5][0:ncol, 0:128], cnt_ar[:, 0:ncol], identf[:]).then_inc(inc("pe"), 1)))
        pet = CNT["pe"]
        block.vector(lambda v: (
            v.wait_ge(SEM["pe"], pet),
            v.tensor_copy(trsb[0:ncol, 0:128], PB[5][0:ncol, 0:128]),
            draininc(v, "ve")))
        vet = CNT["ve"]
        block.sync(lambda sy: (
            sy.wait_ge(SEM["ve"], vet),
            sy.dma_start(out=flat_dram.rearrange("o (k p) -> (o k) p", p=128)[0:ncol, :],
                         in_=trsb[0:ncol, 0:128]).then_inc(inc("st_cc", 16), 16)))
        stf = CNT["st_cc"]
        block.sync(lambda sy: (
            sy.wait_ge(SEM["st_cc"], stf),
            sy.dma_start(out=qrow[0:1, 0:ncol * 128], in_=flat_dram[0:1, 0:ncol * 128]).then_inc(inc("ld_cnt", 16), 16)))
        ldf = CNT["ld_cnt"]

        def interp(v, ncol=ncol, thr_row=thr_row, target=float(target), scale_slot=scale_slot):
            v.wait_ge(SEM["ld_cnt"], ldf)
            first = True
            for k in range(ncol):
                kk = k % NSH
                s0 = (128 - 16 * kk) % 128
                if first:
                    v.memzero(lvlrow[0:1, 0:128])
                    v.drain()
                    first = False
                if s0 == 0:
                    v.tensor_tensor(lvlrow[0:1, 0:128], lvlrow[0:1, 0:128],
                                    qrow[0:1, k * 128:(k + 1) * 128], ALU.add)
                    v.drain()
                else:
                    v.tensor_tensor(lvlrow[0:1, 0:128 - s0], lvlrow[0:1, 0:128 - s0],
                                    qrow[0:1, k * 128 + s0:(k + 1) * 128], ALU.add)
                    v.drain()
                    v.tensor_tensor(lvlrow[0:1, 128 - s0:128], lvlrow[0:1, 128 - s0:128],
                                    qrow[0:1, k * 128: k * 128 + s0], ALU.add)
                    v.drain()
            c0 = lvlrow[0:1, 0:127]
            c1_ = lvlrow[0:1, 1:128]
            v.drain()
            v.tensor_scalar(irow[0][0:1, 0:127], c0, target, None, op0=ALU.is_ge)
            v.tensor_scalar(irow[1][0:1, 0:127], c1_, target, None, op0=ALU.is_lt)
            v.drain()
            v.tensor_tensor(irow[2][0:1, 0:127], irow[0][0:1, 0:127], irow[1][0:1, 0:127], ALU.mult)
            v.tensor_tensor(irow[3][0:1, 0:127], c0, c1_, ALU.subtract)
            v.drain()
            v.tensor_scalar(irow[4][0:1, 0:127], irow[3][0:1, 0:127], 0.5, None, op0=ALU.max)
            v.drain()
            v.reciprocal(irow[5][0:1, 0:127], irow[4][0:1, 0:127])
            v.tensor_scalar(irow[6][0:1, 0:127], c0, target, None, op0=ALU.subtract)
            v.drain()
            v.tensor_tensor(irow[6][0:1, 0:127], irow[6][0:1, 0:127], irow[5][0:1, 0:127], ALU.mult)
            v.tensor_tensor(irow[7][0:1, 0:127], thr_row[0:1, 1:128], thr_row[0:1, 0:127], ALU.subtract)
            v.drain()
            v.tensor_tensor(irow[7][0:1, 0:127], irow[7][0:1, 0:127], irow[6][0:1, 0:127], ALU.mult)
            v.drain()
            v.tensor_tensor(irow[7][0:1, 0:127], irow[7][0:1, 0:127], thr_row[0:1, 0:127], ALU.add)
            v.drain()
            v.tensor_tensor(irow[8][0:1, 0:127], irow[7][0:1, 0:127], irow[2][0:1, 0:127], ALU.mult)
            v.drain()
            v.tensor_reduce(irow[9][0:1, 0:1], irow[8][0:1, 0:127], AXX, ALU.add)
            v.tensor_reduce(irow[9][0:1, 1:2], irow[2][0:1, 0:127], AXX, ALU.add)
            v.tensor_scalar(irow[0][0:1, 1:2], lvlrow[0:1, 0:1], target, None, op0=ALU.is_lt)
            v.tensor_scalar(irow[1][0:1, 1:2], lvlrow[0:1, 127:128], target, None, op0=ALU.is_ge)
            v.drain()
            v.tensor_tensor(irow[0][0:1, 2:3], irow[0][0:1, 1:2], thr_row[0:1, 0:1], ALU.mult)
            v.tensor_tensor(irow[1][0:1, 2:3], irow[1][0:1, 1:2], thr_row[0:1, 127:128], ALU.mult)
            v.drain()
            v.tensor_tensor(irow[9][0:1, 0:1], irow[9][0:1, 0:1], irow[0][0:1, 2:3], ALU.add)
            v.drain()
            v.tensor_tensor(irow[9][0:1, 0:1], irow[9][0:1, 0:1], irow[1][0:1, 2:3], ALU.add)
            v.tensor_tensor(irow[9][0:1, 2:3], irow[9][0:1, 1:2], irow[0][0:1, 1:2], ALU.add)
            v.drain()
            v.tensor_tensor(irow[9][0:1, 2:3], irow[9][0:1, 2:3], irow[1][0:1, 1:2], ALU.add)
            v.drain()
            v.tensor_scalar(irow[9][0:1, 3:4], irow[9][0:1, 2:3], 1.0, None, op0=ALU.max)
            v.drain()
            v.reciprocal(irow[9][0:1, 4:5], irow[9][0:1, 3:4])
            v.drain()
            v.tensor_tensor(irow[9][0:1, 5:6], irow[9][0:1, 0:1], irow[9][0:1, 4:5], ALU.mult)
            v.drain()
            v.tensor_scalar(srow[0:1, scale_slot:scale_slot + 1], irow[9][0:1, 5:6], 1e-5, None, op0=ALU.max)
            draininc(v, "ve")
        block.vector(interp)
        return CNT["ve"]

    VE_AMAX1 = quantile_final(cnt1[:], 2 * NSH, cc_q1_in, cc_q1_out, "cc_q1", thr1row, Q1_TARGET, 16, ("ve", VE_Z),
                              qrow5, irow5)

    def q1_scalars(v):
        v.reciprocal(srow[0:1, 17:18], srow[0:1, 16:17])
        v.drain()
        v.tensor_scalar(srow[0:1, 18:19], srow[0:1, 17:18], 127.0, None, op0=ALU.mult)
        v.tensor_scalar(srow[0:1, 19:20], srow[0:1, 16:17], 1.0 / 127.0, None, op0=ALU.mult)
        v.drain()
        v.tensor_tensor(srow[0:1, 20:21], srow[0:1, 19:20], srow[0:1, 4:5], ALU.mult)
        draininc(v, "ve")
    block.vector(q1_scalars)
    VE_Q1S = CNT["ve"]

    block.tensor(lambda t: (
        t.wait_ge(SEM["ve"], VE_Q1S),
        t.wait_ge(SEM["pe"], CNT["pe"]),
        t.matmul(PB[6][:, 0:3], onesr[:], srow[0:1, 18:21], start=True, stop=True).then_inc(inc("pe"), 1)))
    PE_Q1B = CNT["pe"]

    block.vector(lambda v: (
        v.wait_ge(SEM["pe"], PE_Q1B),
        v.tensor_copy(scol[:, 0:1], PB[6][:, 0:1]),
        v.tensor_copy(scol[:, 1:2], PB[6][:, 2:3]),
        draininc(v, "ve")))
    VE_Q1COL = CNT["ve"]

    q1_sc_marks = []
    for ts in range(NTS):
        def quant1_sc(s_, ts=ts):
            if ts == 0:
                s_.wait_ge(SEM["ve"], VE_Q1COL)
            for dp in range(NP):
                colz = dp * T + ts * TSL
                s_.activation(qsc5[:], ztile[:, colz:colz + TSL], ACT.Copy, scale=scol[:, 0:1], bias=MAGIC)
                s_.drain()
                s_.activation(ztile[:, colz:colz + TSL], qsc5[:], ACT.Copy, bias=-MAGIC)
                s_.drain()
            draininc(s_, "sc")
        block.scalar(quant1_sc)
        q1_sc_marks.append(CNT["sc"])

    qint13 = qint1[:].rearrange("p (dp q) -> p dp q", dp=NP)
    q1_ve_marks = []
    for ts in range(NTS):
        def quant1(v, ts=ts):
            v.wait_ge(SEM["sc"], q1_sc_marks[ts])
            v.tensor_scalar(qint13[:, :, ts * TSL:(ts + 1) * TSL], zt3[:, :, ts * TSL:(ts + 1) * TSL],
                            127.0, -128.0, op0=ALU.min, op1=ALU.max)
            draininc(v, "ve")
        block.vector(quant1)
        q1_ve_marks.append(CNT["ve"])
    VE_QINT1 = CNT["ve"]

    t_ = tap("zq", [D, T])
    if t_ is not None:
        tap_dma(t_.rearrange("(np p) t -> p np t", p=128), ztile[:].rearrange("p (np t) -> p np t", np=NP), [("ve", VE_QINT1)])
    t_ = tap("qint1", [D, T], BF16)
    if t_ is not None:
        tap_dma(t_.rearrange("(np p) t -> p np t", p=128), qint1[:].rearrange("p (np t) -> p np t", np=NP), [("ve", VE_QINT1)])
    if phases <= 5:
        t_ = tap("amax", [1, 64])
        if t_ is not None:
            tap_dma(t_, srow[0:1, 0:64], [("ve", VE_QINT1)])
        t_ = tap("lvl1", [1, 128])
        if t_ is not None:
            tap_dma(t_, lvlrow[0:1, 0:128], [("ve", VE_QINT1)])

    if phases <= 5:
        def out5(sy):
            sy.wait_ge(SEM["ve"], VE_QINT1)
            sy.wait_ge(SEM["st_ta"], ST_T12[0]); sy.wait_ge(SEM["st_tb"], ST_T12[1])
            for dp in range(NP):
                sy.dma_start(out=outT[dp * 128:(dp + 1) * 128, :], in_=ztile[:, dp * T:(dp + 1) * T]
                             ).then_inc(inc("st_out0", 16), 16)
        block.sync(out5)
        return finish([("st_out0", CNT["st_out0"])])

    close_stack(stack_z)
    sbuf_trace("close_stack")
    stack_p6 = new_stack()
    retain = stack_p6.enter_context(nc.sbuf_tensor("retain", [128, NRETAIN * TSL], BF16))
    gel = [stack_p6.enter_context(nc.sbuf_tensor(f"gel{i}", [128, TSL], BF16)) for i in range(2)]
    t1c = [stack_p6.enter_context(nc.sbuf_tensor(f"t1c{i}", [128, NP * 128], BF16)) for i in range(2)]
    sqacc = stack_p6.enter_context(nc.sbuf_tensor("sqacc", [128, NRETAIN], F32))
    sqout = stack_p6.enter_context(nc.sbuf_tensor("sqout", [128, TSL], F32))
    qrow6 = stack_p6.enter_context(nc.sbuf_tensor("qrow6", [1, 16 * 128], F32))
    irow6 = [stack_p6.enter_context(nc.sbuf_tensor(f"irow6_{i}", [1, 128], F32)) for i in range(10)]
    sbuf_trace("sqout")

    NT2 = NPH * NTS
    NB6 = 6
    pe_w1_base = CNT["pe"]
    sc_g_base = CNT["sc"]

    def mark_w1(tau):
        return pe_w1_base + 1 + tau if tau < NRETAIN else pe_w1_base + 3 + tau

    def mark_gelu(tau):
        return sc_g_base + 1 + tau if tau < NRETAIN else sc_g_base + 4 + tau

    SC_G2M_MARK = sc_g_base + NRETAIN + 3

    t1base = [CNT["ld_t1c0"], CNT["ld_t1c1"]]

    N_EARLY_T1 = min(NPH, (NRETAIN - 1) // NTS + 2)

    def t1_loads_a(g):
        for hs in range(N_EARLY_T1):
            slot = hs % 2
            if hs < 2:
                g.wait_ge(SEM["st_ta"], ST_T12[0]); g.wait_ge(SEM["st_tb"], ST_T12[1])
                g.wait_ge(SEM["ve"], VE_QINT1)
            else:
                g.wait_ge(SEM["pe"], mark_w1((hs - 1) * NTS - 1))
            src = t1_dram[:, hs * 128:(hs + 1) * 128].rearrange("(np p) h -> p np h", p=128)
            g.dma_start(out=t1c[slot][:].rearrange("p (np h) -> p np h", np=NP), in_=src).then_inc(inc("ld_t1c%d" % slot, 16), 16)
    block.gpsimd(t1_loads_a)

    def w1_pe_range(t, tau_lo, tau_hi):
        for tau in range(tau_lo, tau_hi):
            hs, ts = tau // NTS, tau % NTS
            if hs == 0:
                t.wait_ge(SEM["ve"], q1_ve_marks[ts])
            slot = hs % 2
            if ts == 0:
                t.wait_ge(SEM["ld_t1c%d" % slot], t1base[slot] + 16 * (hs // 2 + 1))
            bank = PB[tau % NB6]
            if tau >= NB6:
                t.wait_ge(SEM["sc"], mark_gelu(tau - NB6))
            t.wait_ge(SEM["pe"], CNT["pe"])
            for dp in range(NP):
                mm = t.matmul(bank[:, 0:TSL], t1c[slot][:, dp * 128:(dp + 1) * 128],
                              qint1[:, dp * T + ts * TSL: dp * T + (ts + 1) * TSL],
                              start=(dp == 0), stop=(dp == NP - 1))
            mm.then_inc(inc("pe"), 1)

    block.tensor(lambda t: w1_pe_range(t, 0, NRETAIN))
    assert CNT["pe"] == mark_w1(NRETAIN - 1)

    st_ge_cnt = {0: CNT["st_ge0"], 1: CNT["st_ge1"]}

    def gelu_retained(s_):
        for tau in range(NRETAIN):
            s_.wait_ge(SEM["pe"], mark_w1(tau))
            s_.activation(retain[:, tau * TSL:(tau + 1) * TSL], PB[tau % NB6][:, 0:TSL], ACT.Gelu,
                          scale=scol[:, 1:2])
            s_.activation(sqout[:], PB[tau % NB6][:, 0:TSL], ACT.Square, accum_out=sqacc[:, tau:tau + 1])
            draininc(s_, "sc")
    block.scalar(gelu_retained)
    assert CNT["sc"] == mark_gelu(NRETAIN - 1)

    block.vector(lambda v: (
        v.wait_ge(SEM["sc"], mark_gelu(NRETAIN - 1)),
        v.tensor_reduce(smallC[:, 2:3], sqacc[:, 0:NRETAIN], AXX, ALU.add),
        draininc(v, "ve")))
    VE_Q2A = CNT["ve"]

    block.tensor(lambda t: (
        t.wait_ge(SEM["ve"], VE_Q2A),
        t.wait_ge(SEM["pe"], CNT["pe"]),
        t.matmul(PB[6][0:1, 0:1], onesc[:], smallC[:, 2:3], start=True, stop=True).then_inc(inc("pe"), 1)))
    assert CNT["pe"] == pe_w1_base + NRETAIN + 1

    block.vector(lambda v: (
        v.wait_ge(SEM["pe"], pe_w1_base + NRETAIN + 1),
        v.tensor_copy(srow[0:1, 24:25], PB[6][0:1, 0:1]),
        draininc(v, "ve")))
    VE_Q2AC = CNT["ve"]

    block.sync(lambda sy: (
        sy.wait_ge(SEM["ve"], VE_Q2AC),
        sy.dma_start(out=cc_q2a_in[:, :], in_=srow[0:1, 24:25]).then_inc(inc("st_q2a", 16), 16)))
    ST_Q2A = CNT["st_q2a"]

    block.gpsimd(lambda g: (
        g.wait_ge(SEM["st_q2a"], ST_Q2A),
        g.collective_compute("AllReduce", ALU.add, replica_groups=[list(range(NCORE))],
                             ins=[cc_q2a_in.ap().opt()], outs=[cc_q2a_out.ap().opt()]).then_inc(inc("cc_q2a"), 1)))
    CC_Q2A = CNT["cc_q2a"]

    def t1_loads_b(g):
        for hs in range(N_EARLY_T1, NPH):
            slot = hs % 2
            g.wait_ge(SEM["pe"], mark_w1((hs - 1) * NTS - 1))
            src = t1_dram[:, hs * 128:(hs + 1) * 128].rearrange("(np p) h -> p np h", p=128)
            g.dma_start(out=t1c[slot][:].rearrange("p (np h) -> p np h", np=NP), in_=src).then_inc(inc("ld_t1c%d" % slot, 16), 16)
    block.gpsimd(t1_loads_b)

    block.sync(lambda sy: (
        sy.wait_ge(SEM["cc_q2a"], CC_Q2A),
        sy.dma_start(out=srow[0:1, 25:26], in_=cc_q2a_out[:, :]).then_inc(inc("ld_q2a", 16), 16)))
    LD_Q2A = CNT["ld_q2a"]

    NSQ = NCORE * NRETAIN * 128 * TSL

    block.scalar(lambda s_: (
        s_.wait_ge(SEM["ld_q2a"], LD_Q2A),
        s_.activation(srow[0:1, 26:27], srow[0:1, 25:26], ACT.Sqrt, scale=1.0 / NSQ),
        draininc(s_, "sc")))
    SC_C2 = CNT["sc"]
    assert SC_C2 == sc_g_base + NRETAIN + 1

    block.vector(lambda v: (
        v.wait_ge(SEM["sc"], SC_C2),
        v.tensor_scalar(srow[0:1, 27:28], srow[0:1, 26:27], KAPPA2, None, op0=ALU.mult),
        v.drain(),
        v.tensor_tensor(srow[0:1, 28:29], srow[0:1, 27:28], srow[0:1, 20:21], ALU.mult),
        v.drain(),
        v.tensor_scalar(srow[0:1, 29:30], srow[0:1, 28:29], 0.5, None, op0=ALU.max),
        v.drain(),
        v.tensor_scalar(lvlrow[0:1, 0:NLVL], gfrow_sb[0:1, 0:NLVL], srow[0:1, 29:30], None, op0=ALU.mult),
        draininc(v, "ve")))
    VE_C2U = CNT["ve"]

    block.tensor(lambda t: (
        t.wait_ge(SEM["ve"], VE_C2U),
        t.wait_ge(SEM["pe"], CNT["pe"]),
        t.matmul(PB[7][:, 0:1], onesr[:], srow[0:1, 29:30], start=True, stop=True).then_inc(inc("pe"), 1)))
    PE_C2B = CNT["pe"]
    assert PE_C2B == pe_w1_base + NRETAIN + 2

    block.scalar(lambda s_: (
        s_.wait_ge(SEM["ve"], VE_C2U),
        s_.activation(g2row[0:1, 0:NLVL], lvlrow[0:1, 0:NLVL], ACT.Gelu),
        draininc(s_, "sc")))
    assert CNT["sc"] == sc_g_base + NRETAIN + 2

    block.vector(lambda v: (
        v.wait_ge(SEM["pe"], PE_C2B),
        v.tensor_copy(scol[:, 5:6], PB[7][:, 0:1]),
        v.drain(),
        v.tensor_scalar(smallC[:, 8:16], gfine_sb[:], scol[:, 5:6], None, op0=ALU.mult),
        draininc(v, "ve")))
    VE_G2U = CNT["ve"]

    block.scalar(lambda s_: (
        s_.wait_ge(SEM["ve"], VE_G2U),
        s_.activation(g2mat[:], smallC[:, 8:16], ACT.Gelu),
        draininc(s_, "sc")))
    assert CNT["sc"] == SC_G2M_MARK

    block.tensor(lambda t: w1_pe_range(t, NRETAIN, NT2))
    assert CNT["pe"] == mark_w1(NT2 - 1)

    def gelu_live(s_):
        for tau in range(NRETAIN, NT2):
            s_.wait_ge(SEM["pe"], mark_w1(tau))
            if tau - 2 >= 0:
                s_.wait_ge(SEM["st_ge%d" % (tau % 2)], st_ge_cnt[tau % 2] + 16 * ((tau - 2) // 2 + 1))
            if (tau - 2) - NRETAIN + 1 >= 1:
                s_.wait_ge(SEM["vcnt"], (tau - 2) - NRETAIN + 1)
            s_.activation(gel[tau % 2][:], PB[tau % NB6][:, 0:TSL], ACT.Gelu, scale=scol[:, 1:2])
            draininc(s_, "sc")
    block.scalar(gelu_live)
    assert CNT["sc"] == mark_gelu(NT2 - 1)

    def spill_fn(sy):
        for tau in range(NT2):
            hs, ts = tau // NTS, tau % NTS
            sy.wait_ge(SEM["sc"], mark_gelu(tau))
            src = retain[:, tau * TSL:(tau + 1) * TSL] if tau < NRETAIN else gel[tau % 2][:]
            sy.dma_start(out=spill_ge[hs * 128:(hs + 1) * 128, ts * TSL:(ts + 1) * TSL],
                         in_=src).then_inc(inc("st_ge%d" % (tau % 2), 16), 16)
    block.sync(spill_fn)
    ST_GE = [CNT["st_ge0"], CNT["st_ge1"]]

    def count_live(v):
        for tau in range(NRETAIN, NT2):
            v.wait_ge(SEM["sc"], mark_gelu(tau))
            if tau == NRETAIN:
                v.wait_ge(SEM["sc"], SC_G2M_MARK)
            k = tau % NSH
            a0 = accs[tau % 2]
            v.tensor_scalar(junk[tau % 2][:], gel[tau % 2][:], g2mat[:, k:k + 1], 0.0,
                            op0=ALU.is_gt, op1=ALU.add, accum_out=a0[:])
            v.drain()
            v.tensor_tensor(cnt2[:, k:k + 1], cnt2[:, k:k + 1], a0[:], ALU.add)
            draininc(v, "vcnt")
    block.vector(count_live)
    assert CNT["vcnt"] == NT2 - NRETAIN

    def count_retained(v):
        v.wait_ge(SEM["sc"], SC_G2M_MARK)
        for tau in range(NRETAIN):
            k = tau % NSH
            a0 = accs[tau % 2]
            v.tensor_scalar(junk[tau % 2][:], retain[:, tau * TSL:(tau + 1) * TSL], g2mat[:, k:k + 1], 0.0,
                            op0=ALU.is_gt, op1=ALU.add, accum_out=a0[:])
            v.drain()
            v.tensor_tensor(cnt2[:, k:k + 1], cnt2[:, k:k + 1], a0[:], ALU.add)
            v.drain()
        draininc(v, "ve")
    block.vector(count_retained)

    VE_AMAX2 = quantile_final(cnt2[:], NSH, cc_q2_in, cc_q2_out, "cc_q2", g2row, Q2_TARGET, 32, ("ve", CNT["ve"]),
                              qrow6, irow6)

    def q2_scalars(v):
        v.reciprocal(srow[0:1, 33:34], srow[0:1, 32:33])
        v.drain()
        v.tensor_scalar(srow[0:1, 34:35], srow[0:1, 33:34], 127.0, None, op0=ALU.mult)
        v.tensor_scalar(srow[0:1, 35:36], srow[0:1, 32:33], 1.0 / 127.0, None, op0=ALU.mult)
        v.drain()
        v.tensor_tensor(srow[0:1, 36:37], srow[0:1, 35:36], srow[0:1, 5:6], ALU.mult)
        draininc(v, "ve")
    block.vector(q2_scalars)
    VE_Q2S = CNT["ve"]

    block.tensor(lambda t: (
        t.wait_ge(SEM["ve"], VE_Q2S),
        t.wait_ge(SEM["pe"], CNT["pe"]),
        t.matmul(PB[6][:, 0:3], onesr[:], srow[0:1, 34:37], start=True, stop=True).then_inc(inc("pe"), 1)))
    PE_Q2B = CNT["pe"]

    block.vector(lambda v: (
        v.wait_ge(SEM["pe"], PE_Q2B),
        v.tensor_copy(scol[:, 2:3], PB[6][:, 0:1]),
        v.tensor_copy(scol[:, 3:4], PB[6][:, 2:3]),
        v.drain(),
        v.tensor_scalar(derc[:, 40:48], derc[:, 8:16], scol[:, 3:4], None, op0=ALU.mult),
        draininc(v, "ve")))
    VE_ESC = CNT["ve"]

    t_ = tap("amax", [1, 64])
    if t_ is not None:
        tap_dma(t_, srow[0:1, 0:64], [("ve", VE_ESC)])
    t_ = tap("geluspill", [HD, T])
    if t_ is not None:
        tap_dma(t_, spill_ge.ap(), [("st_ge0", ST_GE[0]), ("st_ge1", ST_GE[1])])

    if phases <= 6:
        def out6(sy):
            sy.wait_ge(SEM["ve"], VE_ESC)
            sy.wait_ge(SEM["st_ge0"], ST_GE[0])
            sy.wait_ge(SEM["st_ge1"], ST_GE[1])
            for dp in range(NP):
                sy.dma_start(out=outT[dp * 128:(dp + 1) * 128, :], in_=xt[:, dp * T:(dp + 1) * T]
                             ).then_inc(inc("st_out0", 16), 16)
        block.sync(out6)
        return finish([("st_out0", CNT["st_out0"])])

    close_stack(stack_p6)
    close_stack(stack_ffn)
    sbuf_trace("close_stack")
    stack_p7 = new_stack()
    q2sb = [stack_p7.enter_context(nc.sbuf_tensor(f"q2sb{i}", [128, NPH * TSL], BF16)) for i in range(2)]
    grb = [stack_p7.enter_context(nc.sbuf_tensor(f"grb{i}", [128, 4 * TSL], BF16)) for i in range(3)]
    t2c = [stack_p7.enter_context(nc.sbuf_tensor(f"t2c{i}", [128, NPH * 128], BF16)) for i in range(3)]
    ost = [stack_p7.enter_context(nc.sbuf_tensor(f"ost{i}", [128, TSL], F32)) for i in range(2)]
    qsc7 = stack_p7.enter_context(nc.sbuf_tensor("qsc7", [128, 4 * TSL], F32))
    qsc7b = [stack_p7.enter_context(nc.sbuf_tensor(f"qsc7b{i}", [128, 4 * TSL], F32)) for i in range(2)]
    sbuf_trace("jq")

    NGC = NPH // 4
    ve_q2_marks = {}
    pe_w2_marks = {}
    ve_out_marks = {}

    def emit_quant_stage(ts):
        for gc in range(NGC):
            ntot = ts * NGC + gc
            slot = ntot % 3

            def ge_load(g, ts=ts, gc=gc, slot=slot, ntot=ntot):
                if ntot >= 3:
                    pts, pgc = divmod(ntot - 3, NGC)
                    g.wait_ge(SEM["ve"], ve_q2_marks[(pts, pgc)])
                else:
                    g.wait_ge(SEM["st_ge0"], ST_GE[0])
                    g.wait_ge(SEM["st_ge1"], ST_GE[1])
                    g.wait_ge(SEM["ve"], VE_ESC)
                src = spill_ge[gc * 512:(gc + 1) * 512, ts * TSL:(ts + 1) * TSL].rearrange(
                    "(np4 p) t -> p np4 t", p=128)
                g.dma_start(out=grb[slot][:].rearrange("p (np4 t) -> p np4 t", np4=4), in_=src).then_inc(inc("ld_ge%d" % slot, 16), 16)
            block.sync(ge_load)
            ldval = CNT["ld_ge%d" % slot]

            def q2sc(s_, ts=ts, gc=gc, slot=slot, ldval=ldval, ntot=ntot):
                if ts == 0 and gc == 0:
                    s_.wait_ge(SEM["ve"], VE_ESC)
                s_.wait_ge(SEM["ld_ge%d" % slot], ldval)
                s_.activation(qsc7[:], grb[slot][:], ACT.Copy, scale=scol[:, 2:3], bias=MAGIC)
                s_.drain()
                if ntot >= 2:
                    pts, pgc = divmod(ntot - 2, NGC)
                    s_.wait_ge(SEM["ve"], ve_q2_marks[(pts, pgc)])
                s_.activation(qsc7b[ntot % 2][:], qsc7[:], ACT.Copy, bias=-MAGIC)
                draininc(s_, "sc")
            block.scalar(q2sc)
            sc_q2 = CNT["sc"]

            def q2c(v, ts=ts, gc=gc, slot=slot, sc_q2=sc_q2, ntot=ntot):
                if gc == 0 and ts >= 2:
                    v.wait_ge(SEM["pe"], pe_w2_marks[(ts - 2, NP - 1)])
                v.wait_ge(SEM["sc"], sc_q2)
                v.tensor_scalar(q2sb[ts % 2][:, gc * 4 * TSL:(gc + 1) * 4 * TSL], qsc7b[ntot % 2][:],
                                127.0, -128.0, op0=ALU.min, op1=ALU.max)
                draininc(v, "ve")
            block.vector(q2c)
            ve_q2_marks[(ts, gc)] = CNT["ve"]

    def emit_w2_stage(ts):
        for ds in range(NP):
            slot = (ts * NP + ds) % 3
            nldt = ts * NP + ds

            def t2_load(g, ts=ts, ds=ds, slot=slot, nldt=nldt):
                if nldt >= 3:
                    pts, pds = divmod(nldt - 3, NP)
                    g.wait_ge(SEM["pe"], pe_w2_marks[(pts, pds)])
                else:
                    g.wait_ge(SEM["st_ta"], ST_T12[0]); g.wait_ge(SEM["st_tb"], ST_T12[1])
                    g.wait_ge(SEM["ve"], VE_ESC)
                src = t2_dram[:, ds * 128:(ds + 1) * 128].rearrange("(nph p) e -> p nph e", p=128)
                g.dma_start(out=t2c[slot][:].rearrange("p (nph e) -> p nph e", nph=NPH), in_=src).then_inc(inc("ld_t2c%d" % slot, 16), 16)
            block.gpsimd(t2_load)
            t2val = CNT["ld_t2c%d" % slot]

            def w2_pe(t, ts=ts, ds=ds, slot=slot, t2val=t2val):
                t.wait_ge(SEM["ld_t2c%d" % slot], t2val)
                t.wait_ge(SEM["ve"], ve_q2_marks[(ts, NGC - 1)])
                nprev = ts * NP + ds - 2
                if nprev >= 0:
                    pts, pds = divmod(nprev, NP)
                    t.wait_ge(SEM["ve"], ve_out_marks[(pts, pds)])
                bank = PB[ds % 2 + 2]
                t.wait_ge(SEM["pe"], CNT["pe"])
                for hp in range(NPH):
                    mm = t.matmul(bank[:, 0:TSL], t2c[slot][:, hp * 128:(hp + 1) * 128],
                                  q2sb[ts % 2][:, hp * TSL:(hp + 1) * TSL],
                                  start=(hp == 0), stop=(hp == NPH - 1))
                mm.then_inc(inc("pe"), 1)
            block.tensor(w2_pe)
            pe_w2_marks[(ts, ds)] = CNT["pe"]

            def w2_evac(v, ts=ts, ds=ds):
                n = ts * NP + ds
                oslot = n % 2
                v.wait_ge(SEM["pe"], pe_w2_marks[(ts, ds)])
                if n >= 2:
                    v.wait_ge(SEM["st_out%d" % oslot], 16 * (n // 2))
                colx = ds * T + ts * TSL
                v.scalar_tensor_tensor(ost[oslot][:], PB[ds % 2 + 2][:, 0:TSL], derc[:, 40 + ds:41 + ds],
                                       xt[:, colx:colx + TSL], op0=ALU.mult, op1=ALU.add)
                draininc(v, "ve")
            block.vector(w2_evac)
            ve_out_marks[(ts, ds)] = CNT["ve"]

            def out_dma(s_, ts=ts, ds=ds):
                n = ts * NP + ds
                oslot = n % 2
                s_.wait_ge(SEM["ve"], ve_out_marks[(ts, ds)])
                s_.dma_start(out=outT[ds * 128:(ds + 1) * 128, ts * TSL:(ts + 1) * TSL],
                             in_=ost[oslot][:]).then_inc(inc("st_out%d" % oslot, 16), 16)
            block.scalar(out_dma)

    for stage in range(NTS + 1):
        if stage < NTS:
            emit_quant_stage(stage)
        if stage >= 1:
            emit_w2_stage(stage - 1)

    return finish([("st_out0", CNT["st_out0"]), ("st_out1", CNT["st_out1"]),
                   ("st_ge0", ST_GE[0]), ("st_ge1", ST_GE[1]),
                   ("st_ta", ST_T12[0]), ("st_tb", ST_T12[1])])


def shard_inputs(ins):
    consts = make_constants()
    x = np.asarray(ins["L0_attn_x"])
    wmod = np.asarray(ins["L0_wmod"])
    cc = np.asarray(ins["L0_c"])
    shared = {
        "wqT": np.ascontiguousarray(np.asarray(ins["L0_attn_Wq"]).T),
        "wkT": np.ascontiguousarray(np.asarray(ins["L0_attn_Wk"]).T),
        "wvT": np.ascontiguousarray(np.asarray(ins["L0_attn_Wv"]).T),
        "woT": np.ascontiguousarray(np.asarray(ins["L0_attn_Wo"]).T),
        "w1T": np.ascontiguousarray(np.asarray(ins["L0_ffn_w1"]).T),
        "w2T": np.ascontiguousarray(np.asarray(ins["L0_ffn_w2"]).T),
        "bq": np.ascontiguousarray(np.asarray(ins["L0_attn_bq"]).reshape(D, 1)),
        "bk": np.ascontiguousarray(np.asarray(ins["L0_attn_bk"]).reshape(D, 1)),
        "bv": np.ascontiguousarray(np.asarray(ins["L0_attn_bv"]).reshape(D, 1)),
        "bo": np.ascontiguousarray(np.asarray(ins["L0_attn_bo"]).reshape(D, 1)),
        "lng": np.ascontiguousarray(np.asarray(ins["L0_ffn_ln_g"]).reshape(D, 1)),
        "lnb": np.ascontiguousarray(np.asarray(ins["L0_ffn_ln_b"]).reshape(D, 1)),
    }
    shared.update(consts)
    CW1, CW2 = HD // NCORE, D // NCORE
    maps = []
    for core in range(NCORE):
        b, g = core // GRP, core % GRP
        m = dict(shared)
        m["xT"] = np.ascontiguousarray(x[b, g * T:(g + 1) * T, :].T)
        m["wmodsl"] = np.ascontiguousarray(wmod[:, g * MSL:(g + 1) * MSL])
        m["cTcol"] = np.ascontiguousarray(cc[b].reshape(D, 1))
        m["w1sl"] = np.ascontiguousarray(shared["w1T"][:, core * CW1:(core + 1) * CW1])
        m["w2sl"] = np.ascontiguousarray(shared["w2T"][:, core * CW2:(core + 1) * CW2])
        maps.append({k: np.ascontiguousarray(v) for k, v in m.items()})
    return maps


def kernel(**inputs):
    maps = shard_inputs(inputs)
    nc, _ = build_program(debug=tuple(DEBUG), phases=PHASES)
    res = run_bass_kernel_spmd(nc, maps, core_ids=list(range(NCORE)))
    out = np.empty((B, S, D), np.float32)
    for core in range(NCORE):
        b, g = core // GRP, core % GRP
        out[b, g * T:(g + 1) * T, :] = res.results[core]["outT"].T
    return out
